# version 18
# baseline (speedup 1.0000x reference)
import math
import numpy as np
from contextlib import ExitStack

import concourse.bass as bass
import concourse.mybir as mybir
from concourse.bass_utils import run_bass_kernel_spmd

F32 = mybir.dt.float32
BF16 = mybir.dt.bfloat16
AF = mybir.ActivationFunctionType
ALU = mybir.AluOpType
AX = mybir.AxisListType

L = 4096
D = 1024
NCORE = 8
H1 = 16
HD = 64
CH = 512
H2 = 8
WIN = 512
HID = 2730
HIDP = 2816
NHC = HIDP // 128
W1C = 2 * HIDP
ENGS = ("sync", "tensor", "vector", "scalar", "gpsimd")


class Builder:

    def __init__(self, nc):
        self.nc = nc
        self.plan = {e: [] for e in ENGS}
        self.semh = {}
        self.semc = {}

    def sem(self, name):
        if name not in self.semh:
            self.semh[name] = self.es.enter_context(self.nc.semaphore(name))
            self.semc[name] = 0
        return name

    def wait(self, eng, sem, val):
        if val <= 0:
            return
        self.plan[eng].append(("w", sem, int(val)))

    def op(self, eng, fn, inc=None, amt=1):
        t = None
        if inc is not None:
            self.sem(inc)
            self.semc[inc] += amt
            t = self.semc[inc]
        self.plan[eng].append(("o", fn, inc, amt))
        return t

    def dma(self, eng, out, in_, sem):
        self.sem(sem)
        self.semc[sem] += 16
        t = self.semc[sem]
        self.plan[eng].append(("o", lambda e, o=out, i=in_: e.dma_start(out=o, in_=i), sem, 16))
        return t

    def cur(self, sem):
        self.sem(sem)
        return self.semc[sem]

    def check_deadlock(self):
        idx = {e: 0 for e in ENGS}
        vals = dict.fromkeys(self.semh, 0)
        total = sum(len(p) for p in self.plan.values())
        done = 0
        progress = True
        while progress:
            progress = False
            for e in ENGS:
                p = self.plan[e]
                while idx[e] < len(p):
                    it = p[idx[e]]
                    if it[0] == "w":
                        if vals[it[1]] >= it[2]:
                            idx[e] += 1
                            done += 1
                            progress = True
                        else:
                            break
                    else:
                        if it[2] is not None:
                            vals[it[2]] += it[3]
                        idx[e] += 1
                        done += 1
                        progress = True
        if done != total:
            heads = {e: (self.plan[e][idx[e]] if idx[e] < len(self.plan[e]) else None) for e in ENGS}
            raise RuntimeError(f"deadlock: {done}/{total}; heads={ {e: (h[1], h[2]) if h and h[0]=='w' else h and 'op' for e, h in heads.items()} }")

    def emit(self):
        self.check_deadlock()
        nc = self.nc
        block = self.es.enter_context(nc.Block())

        def runner(eng_name):
            def run(engobj):
                for it in self.plan[eng_name]:
                    if it[0] == "w":
                        engobj.wait_ge(self.semh[it[1]], it[2])
                    else:
                        ins = it[1](engobj)
                        if it[2] is not None:
                            ins.then_inc(self.semh[it[2]], it[3])
            return run

        block.sync(runner("sync"))
        block.tensor(runner("tensor"))
        block.vector(runner("vector"))
        block.scalar(runner("scalar"))
        block.gpsimd(runner("gpsimd"))


def _rotary_consts():
    t = np.arange(L, dtype=np.float64)
    half = np.arange(0, HD, 2, dtype=np.float64)
    inv_freq = 1.0 / (10000.0 ** (half / HD))
    freqs = t[:, None] * inv_freq[None, :]
    freqs = np.concatenate([freqs, freqs], axis=1)
    base = (half + 0.4 * HD) / (1.4 * HD)
    power = (t - L // 2) / (WIN // 2)
    scale = base[None, :] ** power[:, None]
    scale = np.concatenate([scale, scale], axis=1)
    cos = np.cos(freqs)
    sin = np.sin(freqs)
    sgn = np.where(np.arange(HD) < 32, -1.0, 1.0)[None, :]
    csq = (cos * scale * 8.0).T
    ssq = (sin * scale * 8.0 * sgn).T
    csk = (cos / scale).T
    ssk = (sin / scale * sgn).T

    def stack2(m):
        return np.concatenate([m, m], axis=0).astype(np.float32)
    return stack2(csq), stack2(ssq), stack2(csk), stack2(ssk)


def _host_prep(inp):
    import ml_dtypes
    d = {}
    d["x"] = None
    d["wqT"] = np.ascontiguousarray(inp["L0_attn_wq"].T, np.float32)
    d["wkT"] = np.ascontiguousarray(inp["L0_attn_wk"].T, np.float32)
    d["wvT"] = np.ascontiguousarray(inp["L0_attn_wv"].T, np.float32)
    d["woT"] = np.ascontiguousarray(inp["L0_attn_wo"].T, np.float32)
    d["gam128"] = np.tile(np.asarray(inp["L0_attn_gammas"], np.float32)[None, :], (128, 1))
    d["wqkv"] = np.ascontiguousarray(inp["L0_ffn_wqkv"], np.float32)
    d["wo2"] = np.ascontiguousarray(inp["L0_ffn_wo"], np.float32)
    w1 = np.asarray(inp["L0_ffn_w1"], np.float32)
    z = np.zeros((2, D, HIDP - HID), np.float32)
    d["w1p"] = np.ascontiguousarray(
        np.concatenate([w1[:, :, :HID], z, w1[:, :, HID:], z], axis=2))
    w2 = np.asarray(inp["L0_ffn_w2"], np.float32)
    w2p = np.concatenate([w2, np.zeros((2, HIDP - HID, 1024), np.float32)], axis=1)
    aug = np.zeros((2, 128, 1024), np.float32)
    aug[:, 0, :] = np.asarray(inp["L0_ffn_b2"], np.float32)
    d["w2aug"] = np.ascontiguousarray(np.concatenate([w2p, aug], axis=1))
    b1 = np.asarray(inp["L0_ffn_b1"], np.float32)
    zb = np.zeros((2, HIDP - HID), np.float32)
    b1p = np.concatenate([b1[:, :HID], zb, b1[:, HID:], zb], axis=1)
    d["b1c"] = np.ascontiguousarray(b1p.reshape(2, 2 * NHC, 128).transpose(0, 2, 1))
    d["lbc"] = np.ascontiguousarray(
        np.asarray(inp["L0_ffn_ln_b"], np.float32).reshape(2, 8, 128).transpose(0, 2, 1)
        .astype(ml_dtypes.bfloat16))
    d["lgc"] = np.ascontiguousarray(
        np.asarray(inp["L0_ffn_ln_g"], np.float32).reshape(2, 8, 128).transpose(0, 2, 1))
    qs = np.asarray(inp["L0_ffn_q_scale"], np.float32)
    ks = np.asarray(inp["L0_ffn_k_scale"], np.float32)
    def til(v):
        return np.tile(v, 2)[:, None]
    qsks = np.stack([
        np.concatenate([til(qs[l]), til(np.roll(qs[l], -32)),
                        til(ks[l]), til(np.roll(ks[l], -32))], axis=1)
        for l in range(2)])
    d["qsks"] = np.ascontiguousarray(qsks)
    import ml_dtypes
    csq, ssq, csk, ssk = _rotary_consts()
    d["rotc"] = np.ascontiguousarray(
        np.stack([csq, ssq, csk, ssk]).astype(ml_dtypes.bfloat16))
    E8 = np.zeros((8, 128, 16), np.float32)
    for g in range(4):
        E8[g, 0:64, 2 * g] = 1.0
        E8[g, 64:128, 2 * g + 1] = 1.0
        E8[4 + g, 0:64, 8 + 2 * g] = 1.0
        E8[4 + g, 64:128, 9 + 2 * g] = 1.0
    d["E8"] = np.ascontiguousarray(E8.transpose(1, 0, 2).reshape(128, 128).astype(ml_dtypes.bfloat16))
    SelP = np.zeros((16, 1024), np.float32)
    for g in range(4):
        SelP[2 * g, 128 * g:128 * g + 64] = 1.0
        SelP[2 * g + 1, 128 * g + 64:128 * g + 128] = 1.0
        SelP[8 + 2 * g, 512 + 128 * g:512 + 128 * g + 64] = 1.0
        SelP[9 + 2 * g, 512 + 128 * g + 64:512 + 128 * g + 128] = 1.0
    d["SelH"] = SelP
    P = np.zeros((128, 128), np.float32)
    for m in range(128):
        blk = (m // 64) * 64
        src = blk + ((m % 64) + 32) % 64
        P[src, m] = 1.0
    d["PROT"] = np.ascontiguousarray(P.astype(ml_dtypes.bfloat16))
    d["zrow"] = np.zeros((1, 512), np.float32)
    return d



def build_program(dbg=(), phases=("r0", "r1", "r2", "a0", "b0", "c0", "a1", "b1", "c1", "f")):
    nc = bass.Bass("TRN2", target_bir_lowering=False)
    es = ExitStack()
    b = Builder(nc)
    b.es = es

    def dram(name, shape, dtype, kind):
        if kind == "Internal" and name in dbg:
            kind = "ExternalOutput"
        return nc.dram_tensor(name, list(shape), dtype, kind=kind).ap()

    x_h = dram("x", (L, D), F32, "ExternalInput")
    out_h = dram("out", (L, D), F32, "ExternalOutput")
    wqT_h = dram("wqT", (D, D), F32, "ExternalInput")
    wkT_h = dram("wkT", (D, D), F32, "ExternalInput")
    wvT_h = dram("wvT", (D, D), F32, "ExternalInput")
    woT_h = dram("woT", (D, D), F32, "ExternalInput")
    gam_h = dram("gam128", (128, 16), F32, "ExternalInput")
    wqkv_h = dram("wqkv", (2, D, 1536), F32, "ExternalInput")
    wo2_h = dram("wo2", (2, 512, D), F32, "ExternalInput")
    w1p_h = dram("w1p", (2, D, W1C), F32, "ExternalInput")
    w2aug_h = dram("w2aug", (2, 2944, D), F32, "ExternalInput")
    b1c_h = dram("b1c", (2, 128, 2 * NHC), F32, "ExternalInput")
    lbc_h = dram("lbc", (2, 128, 8), BF16, "ExternalInput")
    lgc_h = dram("lgc", (2, 128, 8), F32, "ExternalInput")
    qsks_h = dram("qsks", (2, 128, 4), F32, "ExternalInput")
    rotc_h = dram("rotc", (4, 128, L), BF16, "ExternalInput")
    E8_h = dram("E8", (128, 128), BF16, "ExternalInput")
    SelH_h = dram("SelH", (16, 1024), F32, "ExternalInput")
    PROT_h = dram("PROT", (128, 128), BF16, "ExternalInput")
    zrow_h = dram("zrow", (1, 512), F32, "ExternalInput")

    sc_qT = dram("sc_qT", (D, L), BF16, "Internal")
    sc_kT = dram("sc_kT", (D, L), BF16, "Internal")
    sc_ktok = dram("sc_ktok", (L, D), BF16, "Internal")
    sc_v = dram("sc_v", (L, D), BF16, "Internal")
    sc_retT = dram("sc_retT", (D, L), BF16, "Internal")
    sc_x1 = dram("sc_x1", (L, D), F32, "Internal")
    sc_yA = dram("sc_yA", (L, D), F32, "Internal")
    sc_yB = dram("sc_yB", (L, D), F32, "Internal")
    sc_fq = dram("sc_fq", (512, L), BF16, "Internal")
    sc_fk = dram("sc_fk", (512, L), BF16, "Internal")
    sc_fv = dram("sc_fv", (L, 512), BF16, "Internal")
    sc_w1 = dram("sc_w1", (2, D, W1C), BF16, "Internal")
    sc_w2 = dram("sc_w2", (2, 2944, D), BF16, "Internal")

    def sb(name, shape, dtype):
        return es.enter_context(nc.sbuf_tensor(name, list(shape), dtype))

    ABF = sb("abf", [128, 80000], BF16)
    AF32 = sb("af32", [128, 8192], F32)

    class Arena:
        def __init__(self, t, cols):
            self.t = t
            self.cols = cols
            self.pos = 0
        def alloc(self, n):
            assert self.pos + n <= self.cols, f"arena overflow {self.pos}+{n}>{self.cols}"
            a = self.t[:, self.pos:self.pos + n]
            self.pos += n
            return a
        def mark(self):
            return self.pos
        def reset(self, m):
            self.pos = m

    arb = Arena(ABF, 80000)
    arf = Arena(AF32, 8192)

    IOT = arf.alloc(512)
    Abf = arb.alloc(4096)
    KSbf = arb.alloc(4096)
    Bpbf = arb.alloc(4096)

    gam_sb = sb("gam_sb", [128, 16], F32)
    lngB = sb("lngB", [128, 16], F32)
    G512 = sb("G512", [128, 16], F32)
    E8_sb = sb("E8_sb", [128, 16 * 8], BF16)
    SelH_sb = sb("SelH_sb", [16, 1024], F32)
    PROT_sb = sb("PROT_sb", [128, 128], BF16)
    IDB = sb("IDB", [128, 128], BF16)
    IDF = sb("IDF", [128, 128], F32)
    TRI = sb("TRI", [128, 128], BF16)
    IOC = sb("IOC", [128, 4 * 64], F32)
    MB = sb("MB", [128, 640], BF16)
    MW0 = sb("MW0", [128, 1280], BF16)
    junkA = sb("junkA", [128, 1024], BF16)
    junkB = sb("junkB", [128, 1024], BF16)
    smf = sb("smf", [128, 160], F32)
    Ssb = sb("Ssb", [128, 512], F32)
    Sbf = sb("Sbf", [128, 512], BF16)
    b1e = sb("b1e", [128, 2 * NHC], F32)
    b1c_sb = sb("b1c_sb", [128, 2 * NHC], F32)
    lbc_sb = sb("lbc_sb", [128, 8], BF16)
    lgc_sb = sb("lgc_sb", [128, 8], F32)
    qsks_sb = sb("qsks_sb", [128, 4], F32)
    onesr = sb("onesr", [128, 512], BF16)
    epsc = sb("epsc", [128, 1], F32)

    P0 = es.enter_context(nc.psum_tensor("P0", [128, 1024], F32))
    P1 = es.enter_context(nc.psum_tensor("P1", [128, 1024], F32))
    P2 = es.enter_context(nc.psum_tensor("P2", [128, 1024], F32))
    PB0 = es.enter_context(nc.psum_tensor("PB0", [128, 1024], BF16))
    PB1 = es.enter_context(nc.psum_tensor("PB1", [128, 1024], BF16))

    PK = [P0[:, 0:512], P0[:, 512:1024], P1[:, 0:512], P1[:, 512:1024],
          P2[:, 0:512], P2[:, 512:1024]]
    PSLOT = list(PK)

    SL = "sL"
    SS = "sS"
    ST = "sT"
    SV = "sV"
    SC = "sC"
    SG = "sG"
    SO = "sO"
    for s in (SL, SS, ST, SV, SC, SG, SO):
        b.sem(s)

    W = b.wait
    OP = b.op
    DMA = b.dma

    def ph_init():
        t_g = DMA("sync", gam_sb[:], gam_h, "dgam")
        DMA("sync", E8_sb[:], E8_h, "dE8")
        t_sel = DMA("sync", SelH_sb[:], SelH_h, "dSel")
        t_prot = DMA("sync", PROT_sb[:], PROT_h, "dPROT")
        def gchain(fns):
            t = None
            for f in fns:
                if t is not None:
                    W("gpsimd", SG, t)
                t = OP("gpsimd", f, inc=SG)
            return t
        gchain([lambda e: e.memset(IDB[:], 0.0),
                lambda e: e.affine_select(out=IDB[:], in_=IDB[:], compare_op=ALU.not_equal,
                                          fill=1.0, base=0, pattern=[[-1, 128]], channel_multiplier=1)])
        gchain([lambda e: e.memset(IDF[:], 0.0),
                lambda e: e.affine_select(out=IDF[:], in_=IDF[:], compare_op=ALU.not_equal,
                                          fill=1.0, base=0, pattern=[[-1, 128]], channel_multiplier=1)])
        gchain([lambda e: e.memset(TRI[:], 1.0),
                lambda e: e.affine_select(out=TRI[:], in_=TRI[:], compare_op=ALU.is_ge,
                                          fill=0.0, base=0, pattern=[[1, 128]], channel_multiplier=-1)])
        gchain([lambda e: e.memset(MB[:], 1.0),
                lambda e: e.affine_select(out=MB[:], in_=MB[:], compare_op=ALU.is_ge,
                                          fill=0.0, base=0, pattern=[[1, 640]], channel_multiplier=-1),
                lambda e: e.affine_select(out=MB[:], in_=MB[:], compare_op=ALU.is_ge,
                                          fill=0.0, base=512, pattern=[[-1, 640]], channel_multiplier=1)])
        MW0OFF = [0, 128, 384, 768]
        mwfns = [lambda e: e.memset(MW0[:], 1.0)]
        tmw = gchain(mwfns)
        for r0 in range(4):
            w = 128 * r0 + 128
            o = MW0OFF[r0]
            W("gpsimd", SG, tmw)
            OP("gpsimd", lambda e, r0=r0, w=w, o=o: e.affine_select(
                out=MW0[:, o:o + w], in_=MW0[:, o:o + w],
                compare_op=ALU.is_ge, fill=0.0, base=128 * r0,
                pattern=[[-1, w]], channel_multiplier=1), inc=SG)
        OP("gpsimd", lambda e: e.iota(IOT[:], pattern=[[1, 512]], base=0, channel_multiplier=0,
                                      allow_small_or_imprecise_dtypes=True), inc=SG)
        for j in range(4):
            OP("gpsimd", lambda e, j=j: e.iota(IOC[:, 64 * j:64 * j + 64], pattern=[[0, 64]],
                                               base=512 - 128 * j, channel_multiplier=-1,
                                               allow_small_or_imprecise_dtypes=True), inc=SG)
        t_gps = OP("gpsimd", lambda e: e.memset(onesr[:], 0.0), inc=SG)
        W("gpsimd", SG, t_gps)
        OP("gpsimd", lambda e: e.memset(onesr[0:1, :], 1.0), inc=SG)
        OP("gpsimd", lambda e: e.memset(epsc[:], 1e-5), inc=SG)
        t_gps = OP("gpsimd", lambda e: e.memset(Ssb[:], 0.0), inc=SG)
        t_gps = OP("gpsimd", lambda e: e.memset(Sbf[:], 0.0), inc=SG)

        W("scalar", "dgam", t_g)
        t_ln = OP("scalar", lambda e: e.activation(lngB[:], gam_sb[:], AF.Ln), inc=SC)
        W("scalar", SC, t_ln)
        OP("scalar", lambda e: e.activation(G512[:], lngB[:], AF.Exp, scale=512.0), inc=SC)
        W("scalar", SG, b.cur(SG))
        for h in range(H1):
            hp, hh = h % 2, h // 2
            OP("scalar", lambda e, hp=hp, hh=hh, h=h: e.activation(
                Abf[64 * hp:64 * hp + 64, 512 * hh:512 * hh + 512],
                IOT[64 * hp:64 * hp + 64, :], AF.Exp,
                scale=lngB[64 * hp:64 * hp + 64, h:h + 1]), inc=SC)
        for j in range(4):
            for h in range(H1):
                OP("scalar", lambda e, j=j, h=h: e.activation(
                    KSbf[:, 1024 * j + 64 * h:1024 * j + 64 * h + 64],
                    IOC[:, 64 * j:64 * j + 64], AF.Exp, scale=lngB[:, h:h + 1]), inc=SC)
        for j in range(4):
            OP("gpsimd", lambda e, j=j: e.iota(smf[:, 0:64], pattern=[[0, 64]],
                                               base=-128 * j, channel_multiplier=-1,
                                               allow_small_or_imprecise_dtypes=True), inc=SG)
            tg = b.cur(SG)
            W("scalar", SG, tg)
            for h in range(H1):
                t_bp = OP("scalar", lambda e, j=j, h=h: e.activation(
                    Bpbf[:, 1024 * j + 64 * h:1024 * j + 64 * h + 64],
                    smf[:, 0:64], AF.Exp, scale=lngB[:, h:h + 1]), inc=SC)
            W("gpsimd", SC, t_bp)
        return b.cur(SC), b.cur(SG)

    def emit_ln(xf, ln_out, tx, sx, c0=0, extra_wait=None):
        s1 = smf[:, c0:c0 + 1]
        s2 = smf[:, c0 + 1:c0 + 2]
        mean = smf[:, c0 + 2:c0 + 3]
        q2 = smf[:, c0 + 3:c0 + 4]
        msq = smf[:, c0 + 4:c0 + 5]
        varr = smf[:, c0 + 5:c0 + 6]
        lnv = smf[:, c0 + 6:c0 + 7]
        rstd = smf[:, c0 + 7:c0 + 8]
        negmr = smf[:, c0 + 8:c0 + 9]
        W("scalar", sx, tx)
        if extra_wait is not None:
            W("scalar", extra_wait[0], extra_wait[1])
        tsq = OP("scalar", lambda e: e.activation(junkA[:], xf, AF.Square, accum_out=s2), inc=SC)
        W("vector", sx, tx)
        tsum = OP("vector", lambda e: e.tensor_scalar(junkB[:], xf, 1.0, None, op0=ALU.mult,
                                                      op1=ALU.add, accum_out=s1), inc=SV)
        W("vector", SC, tsq)
        tq2 = OP("vector", lambda e: e.tensor_scalar(q2, s2, 1.0 / D, None, op0=ALU.mult), inc=SV)
        W("vector", SV, tsum)
        tm = OP("vector", lambda e: e.tensor_scalar(mean, s1, 1.0 / D, None, op0=ALU.mult), inc=SV)
        W("vector", SV, tm)
        tms = OP("vector", lambda e: e.tensor_tensor(msq, mean, mean, op=ALU.mult), inc=SV)
        W("vector", SV, tms)
        tv = OP("vector", lambda e: e.scalar_tensor_tensor(varr, msq, -1.0, q2,
                                                           op0=ALU.mult, op1=ALU.add), inc=SV)
        W("scalar", SV, tv)
        tl = OP("scalar", lambda e: e.activation(lnv, varr, AF.Ln, bias=epsc[:, 0:1]), inc=SC)
        W("scalar", SC, tl)
        tr = OP("scalar", lambda e: e.activation(rstd, lnv, AF.Exp, scale=-0.5), inc=SC)
        W("vector", SC, tr)
        tn = OP("vector", lambda e: e.scalar_tensor_tensor(negmr, mean, -1.0, rstd,
                                                           op0=ALU.mult, op1=ALU.mult), inc=SV)
        W("vector", SV, tn)
        tap = OP("vector", lambda e: e.tensor_scalar(ln_out, xf, rstd, negmr,
                                                     op0=ALU.mult, op1=ALU.add), inc=SV)
        return tap

    wstage = [arf.alloc(1024), arf.alloc(1024)]
    wst_tick = [(None, 0), (None, 0)]
    wst_i = [0]

    def load_conv(src_ap, dst_bf, cols, conv_eng="scalar"):
        tc = None
        for c0 in range(0, cols, 1024):
            cw = min(1024, cols - c0)
            sl = wst_i[0] % 2
            wst_i[0] += 1
            q = "gpsimd"
            stg = wstage[sl][:, 0:cw]
            ps, pt = wst_tick[sl]
            if ps is not None:
                W(q, ps, pt)
            td = DMA(q, stg, src_ap[:, c0:c0 + cw], f"dW{sl}")
            W(conv_eng, f"dW{sl}", td)
            sem = {"scalar": SC, "vector": SV, "gpsimd": SG}[conv_eng]
            dpart = dst_bf[:, c0:c0 + cw]
            if conv_eng == "scalar":
                tc = OP("scalar", lambda e, dpart=dpart, stg=stg: e.copy(dpart, stg), inc=sem)
            else:
                tc = OP("vector", lambda e, dpart=dpart, stg=stg: e.tensor_copy(dpart, stg), inc=sem)
            wst_tick[sl] = (sem, tc)
        return tc

    def ph_r0(t_init_sc, t_init_sg):
        m_b = arb.mark()
        m_f = arf.mark()
        W("vector", SC, t_init_sc)
        W("tensor", SG, t_init_sg)
        W("tensor", SC, t_init_sc)
        wq_bf = [arb.alloc(1024) for _ in range(8)]
        wk_bf = [arb.alloc(1024) for _ in range(8)]
        wv_bf = [arb.alloc(1024) for _ in range(8)]
        wt = {}
        for d in range(8):
            wt[("q", d)] = load_conv(wqT_h[128 * d:128 * d + 128, :], wq_bf[d], 1024,
                                     "scalar" if d % 2 == 0 else "vector")
            wt[("k", d)] = load_conv(wkT_h[128 * d:128 * d + 128, :], wk_bf[d], 1024,
                                     "scalar" if d % 2 == 0 else "vector")
            wt[("v", d)] = load_conv(wvT_h[128 * d:128 * d + 128, :], wv_bf[d], 1024,
                                     "scalar" if d % 2 == 0 else "vector")
        xf = [arf.alloc(1024) for _ in range(2)]
        lnbf = [arb.alloc(1024) for _ in range(2)]
        lnT = [arb.alloc(4096) for _ in range(2)]
        ebf = [arb.alloc(512) for _ in range(6)]
        x_t = [0, 0]
        lnbf_t = [0, 0]
        lnbf_cons = [0, 0]
        lnT_t = [0, 0]
        lnT_cons = [0, 0]
        eb_t = [(None, 0)] * 6
        eb_i = [0]

        def store_evac(psrc, dst_hbm, scale=None, n=512, eng="scalar"):
            sl = eb_i[0] % 6
            eb_i[0] += 1
            buf = ebf[sl][:, 0:n]
            ps, pt = eb_t[sl]
            sem = SC if eng == "scalar" else SV
            if ps is not None:
                W(eng, ps, pt)
            if eng == "scalar":
                if scale is None:
                    tc = OP("scalar", lambda e: e.copy(buf, psrc), inc=SC)
                else:
                    tc = OP("scalar", lambda e: e.activation(buf, psrc, AF.Copy, scale=scale), inc=SC)
            else:
                tc = OP("vector", lambda e: e.tensor_copy(buf, psrc), inc=SV)
            W("gpsimd", sem, tc)
            ts = DMA("gpsimd", dst_hbm, buf, f"seb{sl}")
            eb_t[sl] = (f"seb{sl}", ts)
            return tc, ts

        ps_i = [0]
        ps_t = [(None, 0)] * 6

        def next_ps():
            sl = ps_i[0] % 6
            ps_i[0] += 1
            return sl

        pb_i = [0]
        pb_t = [(None, 0)] * 2

        for tt in range(8):
            for s in range(4):
                i = 4 * tt + s
                sl = i % 2
                if lnbf_t[sl]:
                    W("sync", SV, lnbf_t[sl])
                x_t[sl] = DMA("sync", xf[sl][:], x_h[128 * i:128 * i + 128, :], f"dx{sl}")
                if lnbf_cons[sl]:
                    W("vector", ST, lnbf_cons[sl])
                tap = emit_ln(xf[sl][:], lnbf[sl][:], x_t[sl], f"dx{sl}", c0=64 + 16 * sl)
                lnbf_t[sl] = tap
                pbsl = pb_i[0] % 2
                pb_i[0] += 1
                PB = PB0 if pbsl == 0 else PB1
                ps, pt = pb_t[pbsl]
                W("tensor", SV, tap)
                if ps is not None:
                    W("tensor", ps, pt)
                for k in range(8):
                    tmm = OP("tensor", lambda e, k=k, PB=PB, sl=sl: e.transpose(
                        PB[:, 128 * k:128 * k + 128], lnbf[sl][:, 128 * k:128 * k + 128], IDB[:]),
                        inc=ST)
                lnbf_cons[sl] = tmm
                tsl = tt % 2
                dst = lnT[tsl].rearrange("p (k c) -> p k c", k=8)
                W("scalar", ST, tmm)
                if s == 0 and lnT_cons[tsl]:
                    W("scalar", ST, lnT_cons[tsl])
                te1 = OP("scalar", lambda e, PB=PB, dst=dst, s=s: e.copy(
                    dst[:, 0:4, 128 * s:128 * s + 128], PB[:, 0:512].rearrange("p (k c) -> p k c", k=4)), inc=SC)
                te2 = OP("scalar", lambda e, PB=PB, dst=dst, s=s: e.copy(
                    dst[:, 4:8, 128 * s:128 * s + 128], PB[:, 512:1024].rearrange("p (k c) -> p k c", k=4)), inc=SC)
                pb_t[pbsl] = (SC, te2)
                lnT_t[tsl] = te2
            tsl = tt % 2
            tLN = lnT_t[tsl]
            for which, wbf, dsth, scale in (("q", wq_bf, sc_qT, 0.125), ("k", wk_bf, sc_kT, None)):
                for g in range(8):
                    sl6 = next_ps()
                    pp = PSLOT[sl6]
                    pse, pst = ps_t[sl6]
                    W("tensor", SC, tLN)
                    if pse is not None:
                        W("tensor", pse, pst)
                    for d in range(8):
                        W("tensor", SC if (d % 2 == 0) else SV, wt[(which, d)])
                        tmm = OP("tensor", lambda e, d=d, g=g, wbf=wbf, pp=pp, tsl=tsl: e.matmul(
                            pp, wbf[d][:, 128 * g:128 * g + 128],
                            lnT[tsl][:, 512 * d:512 * d + 512],
                            start=(d == 0), stop=(d == 7)), inc=ST)
                    W("scalar", ST, tmm)
                    tc, _ = store_evac(pp, dsth[128 * g:128 * g + 128, 512 * tt:512 * tt + 512],
                                       scale=scale)
                    ps_t[sl6] = (SC, tc)
            for which, wbf, dsth in (("k", wk_bf, sc_ktok), ("v", wv_bf, sc_v)):
                for s in range(4):
                    for half in range(2):
                        sl6 = next_ps()
                        pp = PSLOT[sl6]
                        pse, pst = ps_t[sl6]
                        W("tensor", SC, tLN)
                        if pse is not None:
                            W("tensor", pse, pst)
                        for d in range(8):
                            W("tensor", SC if (d % 2 == 0) else SV, wt[(which, d)])
                            tmm = OP("tensor", lambda e, d=d, s=s, half=half, wbf=wbf, pp=pp, tsl=tsl: e.matmul(
                                pp, lnT[tsl][:, 512 * d + 128 * s:512 * d + 128 * s + 128],
                                wbf[d][:, 512 * half:512 * half + 512],
                                start=(d == 0), stop=(d == 7)), inc=ST)
                        W("scalar", ST, tmm)
                        i = 4 * tt + s
                        tc, _ = store_evac(pp, dsth[128 * i:128 * i + 128, 512 * half:512 * half + 512])
                        ps_t[sl6] = (SC, tc)
            lnT_cons[tsl] = b.cur(ST)
        arb.reset(m_b)
        arf.reset(m_f)

    def phase_barrier():
        curs = [(s, b.semc[s]) for s in list(b.semh) if b.semc[s] > 0]
        for e in ENGS:
            for s, v in curs:
                W(e, s, v)

    def ph_r1(nchunks=8):
        import os
        NOSTATE = os.environ.get("R1_NOSTATE") == "1"
        NODIAG = os.environ.get("R1_NODIAG") == "1"
        NORET = os.environ.get("R1_NORET") == "1"
        NOTRI = os.environ.get("R1_NOTRI") == "1"
        m_b = arb.mark()
        m_f = arf.mark()
        qTb = [[arb.alloc(512) for _ in range(8)] for _ in range(2)]
        kTb = [[arb.alloc(512) for _ in range(8)] for _ in range(2)]
        ktb = [[arb.alloc(1024) for _ in range(4)] for _ in range(2)]
        vb = [[arb.alloc(1024) for _ in range(4)] for _ in range(2)]
        qA = [[arb.alloc(512) for _ in range(8)] for _ in range(2)]
        kpp = [[arb.alloc(1024) for _ in range(4)] for _ in range(2)]
        vpp = [[arb.alloc(1024) for _ in range(4)] for _ in range(2)]
        wT = [arb.alloc(512) for _ in range(3)]
        retb = [arb.alloc(512) for _ in range(3)]
        ld_t = [dict(), dict()]
        cons_t = [dict(), dict()]
        wT_t = [(None, 0)] * 3
        retb_t = [(None, 0)] * 3
        simslots = [PK[0][:, :], PK[1][:, :], PK[2][:, :]]
        sim_t = [[], [], []]
        pretslots = [PK[4][:, :], PK[5][:, :]]
        pret_t = [(None, 0)] * 2
        pP = PK[3][:, 0:64]
        pP_t = [None]
        si = [0]
        pi = [0]
        wti = [0]
        rbi = [0]
        s_up_t = {}
        chunk_st = {}

        for c in range(nchunks):
            sl = c % 2
            if c >= 2:
                W("vector", ST, chunk_st[c - 2])
            for g in range(8):
                if c >= 2:
                    for sem, t in cons_t[sl].get(("qT", g), []):
                        W("sync", sem, t)
                ld_t[sl][("qT", g)] = DMA("sync", qTb[sl][g][:],
                                          sc_qT[128 * g:128 * g + 128, 512 * c:512 * c + 512], f"dq{sl}")
                if c >= 2:
                    for sem, t in cons_t[sl].get(("kT", g), []):
                        W("sync", sem, t)
                ld_t[sl][("kT", g)] = DMA("sync", kTb[sl][g][:],
                                          sc_kT[128 * g:128 * g + 128, 512 * c:512 * c + 512], f"dk{sl}")
            for j in range(4):
                if c >= 2:
                    for sem, t in cons_t[sl].get(("kt", j), []):
                        W("sync", sem, t)
                ld_t[sl][("kt", j)] = DMA("sync", ktb[sl][j][:],
                                          sc_ktok[512 * c + 128 * j:512 * c + 128 * j + 128, :], f"dt{sl}")
                if c >= 2:
                    for sem, t in cons_t[sl].get(("v", j), []):
                        W("sync", sem, t)
                ld_t[sl][("v", j)] = DMA("sync", vb[sl][j][:],
                                         sc_v[512 * c + 128 * j:512 * c + 128 * j + 128, :], f"dv{sl}")
            grp = {"qT": ld_t[sl][("qT", 7)], "kT": ld_t[sl][("kT", 7)],
                   "kt": ld_t[sl][("kt", 3)], "v": ld_t[sl][("v", 3)]}
            cons_t[sl] = dict()
            prep_t = {}
            for g in range(8):
                W("vector", f"dq{sl}", grp["qT"])
                if c >= 2:
                    pass
                tq = OP("vector", lambda e, sl=sl, g=g: e.tensor_tensor(
                    qA[sl][g][:], qTb[sl][g][:], Abf[:, 512 * g:512 * g + 512], op=ALU.mult), inc=SV)
                prep_t[("qA", g)] = tq
                cons_t[sl].setdefault(("qT", g), []).append((SV, tq))
            for j in range(4):
                W("vector", f"dt{sl}", grp["kt"])
                tk = OP("vector", lambda e, sl=sl, j=j: e.tensor_tensor(
                    kpp[sl][j][:], ktb[sl][j][:], KSbf[:, 1024 * j:1024 * j + 1024], op=ALU.mult), inc=SV)
                prep_t[("kpp", j)] = tk
                cons_t[sl].setdefault(("kt", j), []).append((SV, tk))
                W("vector", f"dv{sl}", grp["v"])
                tv = OP("vector", lambda e, sl=sl, j=j: e.tensor_tensor(
                    vpp[sl][j][:], vb[sl][j][:], Bpbf[:, 1024 * j:1024 * j + 1024], op=ALU.mult), inc=SV)
                prep_t[("vpp", j)] = tv
                cons_t[sl].setdefault(("v", j), []).append((SV, tv))
            for g in range(8):
                prsl = pi[0] % 2
                pi[0] += 1
                pret = pretslots[prsl]
                pse, pst = pret_t[prsl]
                if pse is not None:
                    W("tensor", pse, pst)
                for hp in range(2):
                    h = 2 * g + hp
                    W("tensor", SV, prep_t[("qA", g)])
                    if c > 0 and not NOSTATE:
                        W("tensor", SV, s_up_t[h])
                        tsq_mm = OP("tensor", lambda e, hp=hp, g=g, sl=sl, pret=pret: e.matmul(
                            pret[64 * hp:64 * hp + 64, :],
                            Sbf[64 * hp:64 * hp + 64, 64 * g:64 * g + 64],
                            qA[sl][g][64 * hp:64 * hp + 64, :],
                            start=True, stop=False, skip_group_check=True), inc=ST)
                    rpend = None
                    for j in ([] if NODIAG else range(4)):
                        nq = 512 - 128 * j
                        ssl = si[0] % 3
                        si[0] += 1
                        psim = simslots[ssl][:, 0:nq]
                        for sse, sst in sim_t[ssl]:
                            W("tensor", sse, sst)
                        W("tensor", f"dk{sl}", grp["kT"])
                        tsim = OP("tensor", lambda e, sl=sl, g=g, hp=hp, j=j, nq=nq, psim=psim: e.matmul(
                            psim, kTb[sl][g][64 * hp:64 * hp + 64, 128 * j:128 * j + 128],
                            qA[sl][g][64 * hp:64 * hp + 64, 128 * j:512],
                            start=True, stop=True), inc=ST)
                        wsl = wti[0] % 3
                        wti[0] += 1
                        wse, wst = wT_t[wsl]
                        if wse is not None:
                            W("vector", wse, wst)
                        W("vector", ST, tsim)
                        t_tri = OP("vector", lambda e, wsl=wsl, psim=psim: e.tensor_tensor(
                            wT[wsl][:, 0:128], psim[:, 0:128], TRI[:], op=ALU.mult), inc=SV)
                        t_full = None
                        if nq > 128:
                            if wse is not None:
                                W("scalar", wse, wst)
                            W("scalar", ST, tsim)
                            W("scalar", SV, t_tri)
                            t_full = OP("scalar", lambda e, wsl=wsl, psim=psim, nq=nq: e.copy(
                                wT[wsl][:, 128:nq], psim[:, 128:nq]), inc=SC)
                        sim_t[ssl] = [(SV, t_tri)] + ([(SC, t_full)] if t_full is not None else [])

                        def emit_ret(rp):
                            W("tensor", SV, rp["t_tri"])
                            if rp["t_full"] is not None:
                                W("tensor", SC, rp["t_full"])
                            W("tensor", SV, prep_t[("vpp", rp["j"])])
                            tret = OP("tensor", lambda e, sl=sl, j=rp["j"], h=h, hp=hp,
                                      nq=rp["nq"], wsl=rp["wsl"], pret=pret, start=rp["start"]: e.matmul(
                                pret[64 * hp:64 * hp + 64, 128 * j:512],
                                vpp[sl][j][:, 64 * h:64 * h + 64], wT[wsl][:, 0:nq],
                                start=start, stop=(j == 3), skip_group_check=True), inc=ST)
                            wT_t[rp["wsl"]] = (ST, tret)
                            return tret
                        rp = dict(t_tri=t_tri, t_full=t_full, j=j, nq=nq, wsl=wsl,
                                  start=(c == 0 and j == 0))
                        if rpend is not None:
                            emit_ret(rpend)
                        rpend = rp
                    if rpend is not None:
                        tret = emit_ret(rpend)
                    if c < 7 and not NOSTATE:
                        if pP_t[0] is not None:
                            W("tensor", SV, pP_t[0])
                        for j in range(4):
                            W("tensor", SV, prep_t[("kpp", j)])
                            tp = OP("tensor", lambda e, sl=sl, j=j, h=h: e.matmul(
                                pP[0:64, :], kpp[sl][j][:, 64 * h:64 * h + 64],
                                vb[sl][j][:, 64 * h:64 * h + 64],
                                start=(j == 0), stop=(j == 3)), inc=ST)
                        W("vector", ST, tp)
                        if c > 0:
                            W("vector", ST, tsq_mm)
                        tsu = OP("vector", lambda e, hp=hp, g=g, h=h: e.scalar_tensor_tensor(
                            Ssb[64 * hp:64 * hp + 64, 64 * g:64 * g + 64],
                            Ssb[64 * hp:64 * hp + 64, 64 * g:64 * g + 64],
                            G512[64 * hp:64 * hp + 64, h:h + 1],
                            pP[0:64, :], op0=ALU.mult, op1=ALU.add), inc=SV)
                        W("vector", SV, tsu)
                        tsb = OP("vector", lambda e, hp=hp, g=g: e.tensor_copy(
                            Sbf[64 * hp:64 * hp + 64, 64 * g:64 * g + 64],
                            Ssb[64 * hp:64 * hp + 64, 64 * g:64 * g + 64]), inc=SV)
                        s_up_t[h] = tsb
                        pP_t[0] = tsu
                rsl = rbi[0] % 3
                rbi[0] += 1
                rse, rst = retb_t[rsl]
                if rse is not None:
                    W("scalar", rse, rst)
                W("scalar", ST, b.cur(ST))
                tev = OP("scalar", lambda e, rsl=rsl, pret=pret: e.copy(retb[rsl][:], pret[:]), inc=SC)
                pret_t[prsl] = (SC, tev)
                W("gpsimd", SC, tev)
                tst = DMA("gpsimd", sc_retT[128 * g:128 * g + 128, 512 * c:512 * c + 512],
                          retb[rsl][:], f"srb{rsl}")
                retb_t[rsl] = (f"srb{rsl}", tst)
            chunk_st[c] = b.cur(ST)
            for g in range(8):
                cons_t[sl][("kT", g)] = [(ST, chunk_st[c])]
            for j in range(4):
                cons_t[sl].setdefault(("v", j), []).append((ST, chunk_st[c]))
        arb.reset(m_b)
        arf.reset(m_f)

    def ph_r2():
        m_b = arb.mark()
        m_f = arf.mark()
        wo_bf = [arb.alloc(1024) for _ in range(8)]
        wt = {}
        for d in range(8):
            wt[d] = load_conv(woT_h[128 * d:128 * d + 128, :], wo_bf[d], 1024,
                              "scalar" if d % 2 == 0 else "vector")
        rT = [[arb.alloc(512) for _ in range(8)] for _ in range(2)]
        rT_t = [dict(), dict()]
        rT_cons = [0, 0]
        xf = [arf.alloc(1024) for _ in range(2)]
        x1f = [arf.alloc(1024) for _ in range(2)]
        y0f = [arf.alloc(1024) for _ in range(1)]
        x_cons = [0, 0]
        x1_cons = [(0, 0), (0, 0)]
        y0_cons = [0]
        psl_t = [(None, 0)] * 2
        for tt in range(8):
            sl = tt % 2
            if tt >= 2 and rT_cons[sl]:
                W("sync", ST, rT_cons[sl])
            for g in range(8):
                rT_t[sl][g] = DMA("sync", rT[sl][g][:],
                                  sc_retT[128 * g:128 * g + 128, 512 * tt:512 * tt + 512], f"drt{sl}")
            for s in range(4):
                i = 4 * tt + s
                isl = i % 2
                if x_cons[isl]:
                    W("sync", SV, x_cons[isl])
                tx = DMA("sync", xf[isl][:], x_h[128 * i:128 * i + 128, :], f"dx{isl}")
                t_half = []
                for half in range(2):
                    psl = (2 * i + half) % 2
                    pp = PK[psl][:, :]
                    pse, pst = psl_t[psl]
                    if pse is not None:
                        W("tensor", pse, pst)
                    for g in range(8):
                        W("tensor", f"drt{sl}", rT_t[sl][7])
                        W("tensor", SC if (g % 2 == 0) else SV, wt[g])
                        tmm = OP("tensor", lambda e, g=g, s=s, half=half, sl=sl, pp=pp: e.matmul(
                            pp, rT[sl][g][:, 128 * s:128 * s + 128],
                            wo_bf[g][:, 512 * half:512 * half + 512],
                            start=(g == 0), stop=(g == 7)), inc=ST)
                    W("vector", f"dx{isl}", tx)
                    W("vector", ST, tmm)
                    if half == 0 and x1_cons[isl][0]:
                        W("vector", f"sx1{isl}", x1_cons[isl][0])
                        W("vector", SV, x1_cons[isl][1])
                    th = OP("vector", lambda e, isl=isl, half=half, pp=pp: e.tensor_tensor(
                        x1f[isl][:, 512 * half:512 * half + 512], pp,
                        xf[isl][:, 512 * half:512 * half + 512], op=ALU.add), inc=SV)
                    t_half.append(th)
                    psl_t[psl] = (SV, th)
                rT_cons[sl] = b.cur(ST)
                x_cons[isl] = t_half[-1]
                W("gpsimd", SV, t_half[-1])
                tsx1 = DMA("gpsimd", sc_x1[128 * i:128 * i + 128, :], x1f[isl][:], f"sx1{isl}")
                if y0_cons[0]:
                    W("vector", "sy0", y0_cons[0])
                tap = emit_ln(x1f[isl][:], y0f[0][:], t_half[-1], SV, c0=64 + 16 * isl)
                x1_cons[isl] = (tsx1, tap)
                W("gpsimd", SV, tap)
                tsy = DMA("gpsimd", sc_yA[128 * i:128 * i + 128, :], y0f[0][:], f"sy0")
                y0_cons[0] = tsy
        arb.reset(m_b)
        arf.reset(m_f)

    def ph_a(l, yin_h):
        m_b = arb.mark()
        m_f = arf.mark()
        rot = [arb.alloc(4096) for _ in range(4)]
        t_qs = DMA("sync", qsks_sb[:], qsks_h[l], "dqsks")
        rot_t = []
        for r in range(4):
            tl_ = DMA("sync", rot[r], rotc_h[r], f"drot{r}")
            W("vector", f"drot{r}", tl_)
            W("vector", "dqsks", t_qs)
            rot_t.append(OP("vector", lambda e, r=r, rr=rot[r]: e.tensor_scalar(
                rr, rr, qsks_sb[:, r:r + 1], None, op0=ALU.mult), inc=SV))
        wbf = [arb.alloc(1536) for _ in range(8)]
        wt = {}
        for d in range(8):
            eng = "scalar" if d % 2 == 0 else "vector"
            load_conv(wqkv_h[l, 128 * d:128 * d + 128, 0:1024], wbf[d][:, 0:1024], 1024, eng)
            t2 = load_conv(wqkv_h[l, 128 * d:128 * d + 128, 1024:1536], wbf[d][:, 1024:1536], 512, eng)
            wt[d] = ({"scalar": SC, "vector": SV}[eng], t2)
        yf = [arf.alloc(512) for _ in range(2)]
        ysh = [arf.alloc(512) for _ in range(2)]
        hbf = [arb.alloc(1024) for _ in range(2)]
        hT = [arb.alloc(4096) for _ in range(2)]
        sqb = [arb.alloc(512) for _ in range(3)]
        raw = [arb.alloc(512) for _ in range(8)]
        krnb = [arb.alloc(512) for _ in range(4)]
        qrnb = [arb.alloc(512) for _ in range(4)]
        m1b = [arb.alloc(512) for _ in range(2)]
        m2b = [arb.alloc(512) for _ in range(2)]
        fout = [arb.alloc(512) for _ in range(3)]
        vout = [arb.alloc(512) for _ in range(2)]
        ssf = arf.alloc(512)
        rln = arf.alloc(512)
        r_all = arf.alloc(512)
        yf_t = [0, 0]
        yf_cons = [[], []]
        hbf_cons = [0, 0]
        hT_cons = [0, 0]
        hT_t = [0, 0]
        sq_t = [(None, 0)] * 3
        raw_cons = [[] for _ in range(8)]
        krn_t = [[], [], [], []]
        qrn_t = [[], [], [], []]
        m1_t = [(None, 0)] * 2
        m2_t = [(None, 0)] * 2
        fout_t = [(None, 0)] * 3
        vout_t = [(None, 0)] * 2
        pool = [PK[0][:, :], PK[1][:, :], PK[2][:, :], PK[5][:, :]]
        pool_t = [[] for _ in range(4)]
        ssps = PK[3][0:16, :]
        rtps = PK[4][:, :]
        pb_t = [(None, 0)] * 2
        ssps_cons = [0]
        pli = [0]
        sqi = [0]
        fi = [0]
        kri = [0]
        m12i = [0]
        voi = [0]
        pbi = [0]

        def pool_next():
            s = pli[0] % 4
            pli[0] += 1
            return s

        for tt in range(8):
            tsl = tt % 2
            for s in range(4):
                i = 4 * tt + s
                sl = i % 2
                psl = (i + 1) % 2
                for sem, t in yf_cons[sl]:
                    W("sync", sem, t)
                yf_cons[sl] = []
                yf_t[sl] = DMA("sync", yf[sl][:], yin_h[128 * i:128 * i + 128, 0:512], f"dyf{sl}")
                if i == 0:
                    DMA("sync", ysh[sl][1:128, :], yin_h[0:127, 512:1024], f"dys{sl}")
                    tsh = DMA("sync", ysh[sl][0:1, :], zrow_h[0:1, 0:512], f"dys{sl}")
                else:
                    tsh = DMA("sync", ysh[sl][:], yin_h[128 * i - 1:128 * i + 127, 512:1024], f"dys{sl}")
                if hbf_cons[sl]:
                    W("vector", ST, hbf_cons[sl])
                W("vector", f"dyf{sl}", yf_t[sl])
                tc1 = OP("vector", lambda e, sl=sl: e.tensor_copy(
                    hbf[sl][:, 0:512], yf[sl][:]), inc=SV)
                W("vector", f"dys{sl}", tsh)
                tc3 = OP("vector", lambda e, sl=sl: e.tensor_copy(
                    hbf[sl][:, 512:1024], ysh[sl][:]), inc=SV)
                tc2 = tc3
                yf_cons[sl].append((SV, tc3))
                pbsl = pbi[0] % 2
                pbi[0] += 1
                PB = PB0 if pbsl == 0 else PB1
                ps, pt = pb_t[pbsl]
                if ps is not None:
                    W("tensor", ps, pt)
                W("tensor", SV, tc3)
                W("tensor", SV, max(tc1, tc2))
                for k in range(8):
                    tmm = OP("tensor", lambda e, k=k, PB=PB, sl=sl: e.transpose(
                        PB[:, 128 * k:128 * k + 128], hbf[sl][:, 128 * k:128 * k + 128], IDB[:]),
                        inc=ST)
                hbf_cons[sl] = tmm
                dst = hT[tsl].rearrange("p (k c) -> p k c", k=8)
                W("scalar", ST, tmm)
                if s == 0 and hT_cons[tsl]:
                    W("scalar", ST, hT_cons[tsl])
                te1 = OP("scalar", lambda e, PB=PB, dst=dst, s=s: e.copy(
                    dst[:, 0:4, 128 * s:128 * s + 128], PB[:, 0:512].rearrange("p (k c) -> p k c", k=4)), inc=SC)
                te2 = OP("scalar", lambda e, PB=PB, dst=dst, s=s: e.copy(
                    dst[:, 4:8, 128 * s:128 * s + 128], PB[:, 512:1024].rearrange("p (k c) -> p k c", k=4)), inc=SC)
                pb_t[pbsl] = (SC, te2)
                hT_t[tsl] = te2
            raw_t = {}
            for gg in range(8):
                psl4 = pool_next()
                pp = pool[psl4]
                for sem, t in pool_t[psl4]:
                    W("tensor", sem, t)
                W("tensor", SC, hT_t[tsl])
                for d in range(8):
                    W("tensor", wt[d][0], wt[d][1])
                    tmm = OP("tensor", lambda e, d=d, gg=gg, pp=pp, tsl=tsl: e.matmul(
                        pp, wbf[d][:, 128 * gg:128 * gg + 128],
                        hT[tsl][:, 512 * d:512 * d + 512],
                        start=(d == 0), stop=(d == 7)), inc=ST)
                sqsl = sqi[0] % 3
                sqi[0] += 1
                sse, sst = sq_t[sqsl]
                W("scalar", ST, tmm)
                for sem, t in raw_cons[gg]:
                    W("scalar", sem, t)
                raw_cons[gg] = []
                trw = OP("scalar", lambda e, gg=gg, pp=pp: e.copy(raw[gg][:], pp), inc=SC)
                raw_t[gg] = trw
                if sse is not None:
                    W("vector", sse, sst)
                W("vector", SC, trw)
                tsq = OP("vector", lambda e, sqsl=sqsl, gg=gg: e.tensor_tensor(
                    sqb[sqsl][:], raw[gg][:], raw[gg][:], op=ALU.mult), inc=SV)
                raw_cons[gg].append((SV, tsq))
                W("tensor", SV, tsq)
                if gg == 0:
                    W("tensor", "dE8", b.cur("dE8"))
                    if ssps_cons[0]:
                        W("tensor", SV, ssps_cons[0])
                tss = OP("tensor", lambda e, gg=gg, sqsl=sqsl: e.matmul(
                    ssps, E8_sb[:, 16 * gg:16 * gg + 16], sqb[sqsl][:],
                    start=(gg == 0), stop=(gg == 7)), inc=ST)
                sq_t[sqsl] = (ST, tss)
                pool_t[psl4] = [(SC, trw)]
            for s in range(4):
                i = 4 * tt + s
                psl4 = pool_next()
                pp = pool[psl4]
                for sem, t in pool_t[psl4]:
                    W("tensor", sem, t)
                for d in range(8):
                    tmm = OP("tensor", lambda e, d=d, s=s, pp=pp, tsl=tsl: e.matmul(
                        pp, hT[tsl][:, 512 * d + 128 * s:512 * d + 128 * s + 128],
                        wbf[d][:, 1024:1536], start=(d == 0), stop=(d == 7)), inc=ST)
                vsl = voi[0] % 2
                voi[0] += 1
                vse, vst = vout_t[vsl]
                if vse is not None:
                    W("scalar", vse, vst)
                W("scalar", ST, tmm)
                tve = OP("scalar", lambda e, vsl=vsl, pp=pp: e.copy(vout[vsl][:], pp), inc=SC)
                pool_t[psl4] = [(SC, tve)]
                W("gpsimd", SC, tve)
                tst = DMA("gpsimd", sc_fv[128 * i:128 * i + 128, :], vout[vsl][:], f"svo{vsl}")
                vout_t[vsl] = (f"svo{vsl}", tst)
            W("vector", ST, tss)
            t1 = OP("vector", lambda e: e.tensor_scalar(ssf[0:16, :], ssps, 6e-20, None,
                                                        op0=ALU.max), inc=SV)
            ssps_cons[0] = t1
            W("scalar", SV, t1)
            t2_ = OP("scalar", lambda e: e.activation(rln[0:16, :], ssf[0:16, :], AF.Ln), inc=SC)
            W("scalar", SC, t2_)
            t3_ = OP("scalar", lambda e: e.activation(ssf[0:16, :], rln[0:16, :], AF.Exp, scale=-0.5), inc=SC)
            W("vector", SC, t3_)
            t4_ = OP("vector", lambda e: e.tensor_scalar(r_all[0:16, :], ssf[0:16, :], 1e12, None,
                                                         op0=ALU.min), inc=SV)
            rotq = {}
            for gg in range(4):
                psl4 = pool_next()
                Rps = pool[psl4]
                for sem, t in pool_t[psl4]:
                    W("tensor", sem, t)
                W("tensor", SV, t4_)
                if tt == 0 and gg == 0:
                    W("tensor", "dSel", b.cur("dSel"))
                tR = OP("tensor", lambda e, gg=gg, Rps=Rps: e.matmul(
                    Rps, SelH_sb[0:16, 128 * gg:128 * gg + 128], r_all[0:16, :],
                    start=True, stop=True), inc=ST)
                rotq[gg] = [Rps, tR, psl4, None, None]
            for gg in range(4):
                Rps, tR, psl4, _, _ = rotq[gg]
                for kse, kst in qrn_t[gg]:
                    W("vector", kse, kst)
                W("vector", SC, raw_t[gg])
                W("vector", ST, tR)
                tqrn = OP("vector", lambda e, gg=gg, Rps=Rps: e.tensor_tensor(
                    qrnb[gg][:], raw[gg][:], Rps, op=ALU.mult), inc=SV)
                raw_cons[gg] = [(SV, tqrn)]
                pool_t[psl4] = [(SV, tqrn)]
                rotq[gg][3] = tqrn
            for gg in range(4):
                tqrn = rotq[gg][3]
                psl4b = pool_next()
                rotps = pool[psl4b]
                for sem, t in pool_t[psl4b]:
                    W("tensor", sem, t)
                W("tensor", SV, tqrn)
                if tt == 0 and gg == 0:
                    W("tensor", "dPROT", b.cur("dPROT"))
                trot = OP("tensor", lambda e, gg=gg, rotps=rotps: e.matmul(
                    rotps, PROT_sb[:], qrnb[gg][:], start=True, stop=True), inc=ST)
                rotq[gg][4] = (rotps, trot, psl4b)
            for gg in range(4):
                tqrn = rotq[gg][3]
                rotps, trot, psl4b = rotq[gg][4]
                m1sl = m12i[0] % 2
                m12i[0] += 1
                mse, mst = m1_t[m1sl]
                if mse is not None:
                    W("vector", mse, mst)
                W("vector", SV, rot_t[0])
                tm1 = OP("vector", lambda e, gg=gg, m1sl=m1sl, tt=tt: e.tensor_tensor(
                    m1b[m1sl][:], qrnb[gg][:], rot[0][:, 512 * tt:512 * tt + 512], op=ALU.mult), inc=SV)
                mse, mst = m2_t[m1sl]
                if mse is not None:
                    W("vector", mse, mst)
                W("vector", ST, trot)
                W("vector", SV, rot_t[1])
                tm2 = OP("vector", lambda e, m1sl=m1sl, rotps=rotps, tt=tt: e.tensor_tensor(
                    m2b[m1sl][:], rotps, rot[1][:, 512 * tt:512 * tt + 512], op=ALU.mult), inc=SV)
                qrn_t[gg] = [(SV, tm1), (ST, trot)]
                fsl = fi[0] % 3
                fi[0] += 1
                fse, fst = fout_t[fsl]
                if fse is not None:
                    W("vector", fse, fst)
                W("vector", SV, tm2)
                tqf = OP("vector", lambda e, fsl=fsl, m1sl=m1sl: e.tensor_tensor(
                    fout[fsl][:], m1b[m1sl][:], m2b[m1sl][:], op=ALU.add), inc=SV)
                m1_t[m1sl] = (SV, tqf)
                m2_t[m1sl] = (SV, tqf)
                pool_t[psl4b] = [(SV, tm2)]
                W("gpsimd", SV, tqf)
                tst = DMA("gpsimd", sc_fq[128 * gg:128 * gg + 128, 512 * tt:512 * tt + 512],
                          fout[fsl][:], f"sfo{fsl}")
                fout_t[fsl] = (f"sfo{fsl}", tst)
            rotk = {}
            for gk in range(4):
                psl4 = pool_next()
                Rps = pool[psl4]
                for sem, t in pool_t[psl4]:
                    W("tensor", sem, t)
                W("tensor", SV, t4_)
                if tt == 0 and gk == 0:
                    W("tensor", "dSel", b.cur("dSel"))
                tR = OP("tensor", lambda e, gk=gk, Rps=Rps: e.matmul(
                    Rps, SelH_sb[0:16, 512 + 128 * gk:512 + 128 * gk + 128], r_all[0:16, :],
                    start=True, stop=True), inc=ST)
                rotk[gk] = [Rps, tR, psl4, None, None, None]
            for gk in range(4):
                gg = 4 + gk
                Rps, tR, psl4, _, _, _ = rotk[gk]
                krsl = gk
                for kse, kst in krn_t[krsl]:
                    W("vector", kse, kst)
                W("vector", SC, raw_t[gg])
                W("vector", ST, tR)
                tkrn = OP("vector", lambda e, gg=gg, krsl=krsl, Rps=Rps: e.tensor_tensor(
                    krnb[krsl][:], raw[gg][:], Rps, op=ALU.mult), inc=SV)
                raw_cons[gg] = [(SV, tkrn)]
                pool_t[psl4] = [(SV, tkrn)]
                rotk[gk][3] = tkrn
                rotk[gk][4] = krsl
            for gk in range(4):
                tkrn, krsl = rotk[gk][3], rotk[gk][4]
                psl4b = pool_next()
                rotps = pool[psl4b]
                for sem, t in pool_t[psl4b]:
                    W("tensor", sem, t)
                W("tensor", SV, tkrn)
                trot = OP("tensor", lambda e, krsl=krsl, rotps=rotps: e.matmul(
                    rotps, PROT_sb[:], krnb[krsl][:], start=True, stop=True), inc=ST)
                rotk[gk][5] = (rotps, trot, psl4b)
            for gk in range(4):
                tkrn, krsl = rotk[gk][3], rotk[gk][4]
                rotps, trot, psl4b = rotk[gk][5]
                m1sl = m12i[0] % 2
                m12i[0] += 1
                mse, mst = m1_t[m1sl]
                if mse is not None:
                    W("vector", mse, mst)
                W("vector", SV, tkrn)
                W("vector", SV, rot_t[2])
                tm1 = OP("vector", lambda e, krsl=krsl, m1sl=m1sl, tt=tt: e.tensor_tensor(
                    m1b[m1sl][:], krnb[krsl][:], rot[2][:, 512 * tt:512 * tt + 512], op=ALU.mult), inc=SV)
                mse, mst = m2_t[m1sl]
                if mse is not None:
                    W("vector", mse, mst)
                W("vector", ST, trot)
                W("vector", SV, rot_t[3])
                tm2 = OP("vector", lambda e, m1sl=m1sl, rotps=rotps, tt=tt: e.tensor_tensor(
                    m2b[m1sl][:], rotps, rot[3][:, 512 * tt:512 * tt + 512], op=ALU.mult), inc=SV)
                krn_t[krsl] = [(SV, tm1), (ST, trot)]
                fsl = fi[0] % 3
                fi[0] += 1
                fse, fst = fout_t[fsl]
                if fse is not None:
                    W("vector", fse, fst)
                W("vector", SV, tm2)
                tkf = OP("vector", lambda e, fsl=fsl, m1sl=m1sl: e.tensor_tensor(
                    fout[fsl][:], m1b[m1sl][:], m2b[m1sl][:], op=ALU.add), inc=SV)
                m1_t[m1sl] = (SV, tkf)
                m2_t[m1sl] = (SV, tkf)
                pool_t[psl4b] = [(SV, tm2)]
                W("gpsimd", SV, tkf)
                tst = DMA("gpsimd", sc_fk[128 * gk:128 * gk + 128, 512 * tt:512 * tt + 512],
                          fout[fsl][:], f"sfo{fsl}")
                fout_t[fsl] = (f"sfo{fsl}", tst)
            hT_cons[tsl] = b.cur(ST)
        arb.reset(m_b)
        arf.reset(m_f)

    def ph_b(l, yin_h, yout_h):
        m_b = arb.mark()
        m_f = arf.mark()
        wo2bf = [arb.alloc(1024) for _ in range(4)]
        wt = {}
        for ic in range(4):
            eng = "scalar" if ic % 2 == 0 else "vector"
            wt[ic] = ({"scalar": SC, "vector": SV}[eng],
                      load_conv(wo2_h[l, 128 * ic:128 * ic + 128, :], wo2bf[ic][:], 1024, eng))
        kwb = [arb.alloc(1024) for _ in range(4)]
        qwb = [arb.alloc(512) for _ in range(4)]
        vwb = [arb.alloc(512) for _ in range(8)]
        atb = [arb.alloc(640) for _ in range(4)]
        amb = [arb.alloc(640) for _ in range(4)]
        anb = [arb.alloc(640) for _ in range(4)]
        atT = [arb.alloc(640) for _ in range(4)]
        oT = [arb.alloc(512) for _ in range(4)]
        wcb = [arb.alloc(1024) for _ in range(2)]
        wcb_t = [(None, 0)] * 2
        wci = [0]

        def emit_wconv(src_ap, dst_ap, cols):
            sl_ = wst_i[0] % 2
            wst_i[0] += 1
            stg = wstage[sl_][:, 0:cols]
            ps, pt = wst_tick[sl_]
            if ps is not None:
                W("sync", ps, pt)
            td = DMA("sync", stg, src_ap, f"dWc{sl_}")
            eng = "scalar" if sl_ == 0 else "vector"
            sem = SC if sl_ == 0 else SV
            bsl = wci[0] % 2
            wci[0] += 1
            bse, bst = wcb_t[bsl]
            if bse is not None:
                W(eng, bse, bst)
            W(eng, f"dWc{sl_}", td)
            dstg = wcb[bsl][:, 0:cols]
            if eng == "scalar":
                tc_ = OP("scalar", lambda e, dstg=dstg, stg=stg: e.copy(dstg, stg), inc=sem)
            else:
                tc_ = OP("vector", lambda e, dstg=dstg, stg=stg: e.tensor_copy(dstg, stg), inc=sem)
            wst_tick[sl_] = (sem, tc_)
            W("gpsimd", sem, tc_)
            ts_ = DMA("gpsimd", dst_ap, dstg, f"swc{bsl}")
            wcb_t[bsl] = (f"swc{bsl}", ts_)

        wconv_jobs = []
        for d8 in range(8):
            for p6 in range(6):
                c0 = 1024 * p6
                cw = min(1024, W1C - c0)
                wconv_jobs.append((w1p_h[l, 128 * d8:128 * d8 + 128, c0:c0 + cw],
                                   sc_w1[l, 128 * d8:128 * d8 + 128, c0:c0 + cw], cw))
        for c23 in range(23):
            wconv_jobs.append((w2aug_h[l, 128 * c23:128 * c23 + 128, :],
                               sc_w2[l, 128 * c23:128 * c23 + 128, :], 1024))
        yf = [arf.alloc(1024) for _ in range(2)]
        y2f = [arf.alloc(1024) for _ in range(2)]
        kw_cons = [[] for _ in range(4)]
        qw_cons = [[] for _ in range(4)]
        vw_cons = [[] for _ in range(8)]
        kw_t = [0] * 4
        qw_t = [0] * 4
        vw_t = [0] * 8
        at_t = [(None, 0)] * 4
        am_t = [(None, 0)] * 4
        an_t = [(None, 0)] * 4
        atT_t = [[], [], [], []]
        oT_t = [(None, 0)] * 4
        yf_t = [0, 0]
        yf_cons = [[], []]
        y2_t = [(None, 0)] * 2
        sim_t = [[], []]
        vw_copy = [0, 0, 0, 0]
        pend = []
        pout_t = [[], []]
        wo_t = []
        pb_t = [[], []]
        ai = [0]
        pbi = [0]
        smbase = 96

        for w in range(8):
            for gg in range(4):
                if w > 0:
                    for sem, t in kw_cons[gg]:
                        W("vector", sem, t)
                    tcp = OP("vector", lambda e, gg=gg: e.tensor_copy(
                        kwb[gg][:, 0:512], kwb[gg][:, 512:1024]), inc=SV)
                    W("sync", SV, tcp)
                kw_cons[gg] = []
                kw_t[gg] = DMA("sync", kwb[gg][:, 512:1024],
                               sc_fk[128 * gg:128 * gg + 128, 512 * w:512 * w + 512], f"dkw{gg}")
                for sem, t in qw_cons[gg]:
                    W("sync", sem, t)
                qw_cons[gg] = []
                qw_t[gg] = DMA("sync", qwb[gg][:],
                               sc_fq[128 * gg:128 * gg + 128, 512 * w:512 * w + 512], f"dqw{gg}")
            for j4 in range(4):
                if w > 0:
                    for sem, t in vw_cons[j4]:
                        W("vector", sem, t)
                    for sem, t in vw_cons[4 + j4]:
                        W("vector", sem, t)
                    tcp = OP("vector", lambda e, j4=j4: e.tensor_copy(
                        vwb[j4][:], vwb[4 + j4][:]), inc=SV)
                    W("sync", SV, tcp)
                    vw_cons[j4] = []
                    vw_cons[4 + j4] = [(SV, tcp)]
                    vw_copy[j4] = tcp
                vw_t[4 + j4] = DMA("sync", vwb[4 + j4][:],
                                   sc_fv[512 * w + 128 * j4:512 * w + 128 * j4 + 128, :], f"dvw{j4}")
            def emit_tail(info):
                a2 = info["a2"]; nq = info["nq"]; njc = info["njc"]
                vbase = info["vbase"]; r0 = info["r0"]; hp = info["hp"]
                h2 = info["h2"]; tan = info["tan"]; pout = info["pout"]; gg = info["gg"]
                pbsl = pbi[0] % 2
                pbi[0] += 1
                PB = PB0 if pbsl == 0 else PB1
                for sem, t in pb_t[pbsl]:
                    W("tensor", sem, t)
                W("tensor", SV, tan)
                for j in range(njc):
                    wd = min(128, nq - 128 * j)
                    ttr = OP("tensor", lambda e, PB=PB, a2=a2, j=j, wd=wd: e.transpose(
                        PB[:, 128 * j:128 * j + wd], anb[a2][:, 128 * j:128 * j + wd], IDB[:]),
                        inc=ST)
                an_t[a2] = (ST, ttr)
                for sem, t in atT_t[a2]:
                    W("scalar", sem, t)
                W("scalar", ST, ttr)
                tate = OP("scalar", lambda e, PB=PB, a2=a2, nq=nq: e.copy(
                    atT[a2][:, 0:nq], PB[:, 0:nq]), inc=SC)
                pb_t[pbsl] = [(SC, tate)]
                W("tensor", SC, tate)
                for j in range(njc):
                    vs = vbase + j
                    if vs >= 4:
                        W("tensor", f"dvw{vs - 4}", vw_t[vs])
                    elif vw_copy[vs]:
                        W("tensor", SV, vw_copy[vs])
                    wd = min(128, nq - 128 * j)
                    tav = OP("tensor", lambda e, vs=vs, h2=h2, hp=hp, a2=a2, j=j, r0=r0,
                             wd=wd, pout=pout: e.matmul(
                        pout[64 * hp:64 * hp + 64, 128 * r0:128 * r0 + 128],
                        vwb[vs][:, 64 * h2:64 * h2 + 64],
                        atT[a2][:, 128 * j:128 * j + wd],
                        start=(j == 0), stop=(j == njc - 1),
                        skip_group_check=True), inc=ST)
                    vw_cons[vs].append((ST, tav))
                atT_t[a2] = [(ST, tav)]
                if info["last"]:
                    ose, ost = oT_t[gg]
                    if ose is not None:
                        W("scalar", ose, ost)
                    W("scalar", ST, b.cur(ST))
                    toev = OP("scalar", lambda e, gg=gg, pout=pout: e.copy(oT[gg][:], pout), inc=SC)
                    oT_t[gg] = (SC, toev)
                    pout_t[info["pslot"]] = [(SC, toev)]

            for gg in range(4):
                pslot = gg % 2
                pout = PK[4] if pslot == 0 else PK[5]
                first_inst_of_pair = True
                for r0 in range(4):
                    i = 4 * w + r0
                    if w == 0:
                        nq = 128 * (r0 + 1)
                        cst = 512
                        njc = r0 + 1
                        vbase = 4
                        mask = MW0[:, [0, 128, 384, 768][r0]:[0, 128, 384, 768][r0] + nq]
                    else:
                        nq = 640
                        cst = 128 * r0
                        njc = 5
                        vbase = r0
                        mask = MB[:, 0:640]
                    for hp in range(2):
                        h2 = 2 * gg + hp
                        ssl = hp
                        psim = (P0 if ssl == 0 else P1)[:, 0:nq]
                        for sem, t in sim_t[ssl]:
                            W("tensor", sem, t)
                        sim_t[ssl] = []
                        if first_inst_of_pair:
                            for sem, t in pout_t[pslot]:
                                W("tensor", sem, t)
                            first_inst_of_pair = False
                        W("tensor", f"dkw{gg}", kw_t[gg])
                        W("tensor", f"dqw{gg}", qw_t[gg])
                        nmm = (nq + 511) // 512
                        off = 0
                        for z in range(nmm):
                            wd = min(512, nq - off)
                            tsim = OP("tensor", lambda e, gg=gg, hp=hp, r0=r0, psim=psim,
                                      cst=cst, off=off, wd=wd: e.matmul(
                                psim[:, off:off + wd],
                                qwb[gg][64 * hp:64 * hp + 64, 128 * r0:128 * r0 + 128],
                                kwb[gg][64 * hp:64 * hp + 64, cst + off:cst + off + wd],
                                start=True, stop=True), inc=ST)
                            off += wd
                        qw_cons[gg].append((ST, tsim))
                        kw_cons[gg].append((ST, tsim))
                        cb = smbase + 4 * (ai[0] * 2 + hp) % 32
                        nm = smf[:, cb:cb + 1]
                        nb2 = smf[:, cb + 1:cb + 2]
                        rsum = smf[:, cb + 2:cb + 3]
                        rc = smf[:, cb + 3:cb + 4]
                        W("vector", ST, tsim)
                        tnm = OP("vector", lambda e, psim=psim, nm=nm: e.tensor_reduce(
                            nm, psim, AX.X, op=ALU.max, negate=True), inc=SV)
                        asl = ai[0] * 2 + hp
                        a2 = asl % 4
                        ase, ast = at_t[a2]
                        if ase is not None:
                            W("scalar", ase, ast)
                        W("scalar", SV, tnm)
                        texp = OP("scalar", lambda e, psim=psim, a2=a2, nq=nq, nm=nm: e.activation(
                            atb[a2][:, 0:nq], psim, AF.Exp, bias=nm), inc=SC)
                        sim_t[ssl] = [(SC, texp), (SV, tnm)]
                        mse, mst = am_t[a2]
                        if mse is not None:
                            W("vector", mse, mst)
                        W("vector", SC, texp)
                        tstt = OP("vector", lambda e, a2=a2, nq=nq, mask=mask, rsum=rsum: e.scalar_tensor_tensor(
                            amb[a2][:, 0:nq], atb[a2][:, 0:nq], 1.0, mask,
                            op0=ALU.mult, op1=ALU.mult, accum_out=rsum), inc=SV)
                        at_t[a2] = (SV, tstt)
                        W("vector", SV, tstt)
                        trc = OP("vector", lambda e, rc=rc, rsum=rsum: e.reciprocal(rc, rsum), inc=SV)
                        nse, nst = an_t[a2]
                        if nse is not None:
                            W("vector", nse, nst)
                        W("vector", SV, trc)
                        tan = OP("vector", lambda e, a2=a2, nq=nq, rc=rc: e.tensor_scalar(
                            anb[a2][:, 0:nq], amb[a2][:, 0:nq], rc, None, op0=ALU.mult), inc=SV)
                        am_t[a2] = (SV, tan)
                        info = dict(a2=a2, nq=nq, njc=njc, vbase=vbase, r0=r0, hp=hp,
                                    h2=h2, tan=tan, pout=pout, gg=gg, pslot=pslot,
                                    last=(r0 == 3 and hp == 1))
                        if len(pend) >= 1:
                            emit_tail(pend.pop(0))
                        pend.append(info)
                    ai[0] += 1
            while pend:
                emit_tail(pend.pop(0))
            oev = b.cur(SC)
            for r0 in range(4):
                i = 4 * w + r0
                sl = i % 2
                for sem, t in yf_cons[sl]:
                    W("sync", sem, t)
                yf_cons[sl] = []
                tyf = DMA("sync", yf[sl][:], yin_h[128 * i:128 * i + 128, :], f"dyb{sl}")
                for half in range(2):
                    for sem, t in wo_t:
                        W("tensor", sem, t)
                    wo_t = []
                    pwo = PK[5][:, :]
                    if r0 == 0 and half == 0:
                        for sem, t in pout_t[1]:
                            W("tensor", sem, t)
                    for ic in range(4):
                        W("tensor", SC, oev)
                        W("tensor", wt[ic][0], wt[ic][1])
                        tmm = OP("tensor", lambda e, ic=ic, r0=r0, half=half, pwo=pwo: e.matmul(
                            pwo, oT[ic][:, 128 * r0:128 * r0 + 128],
                            wo2bf[ic][:, 512 * half:512 * half + 512],
                            start=(ic == 0), stop=(ic == 3)), inc=ST)
                    y2e, y2tt = y2_t[sl]
                    if half == 0 and y2e is not None:
                        W("vector", y2e, y2tt)
                    W("vector", f"dyb{sl}", tyf)
                    W("vector", ST, tmm)
                    ty2 = OP("vector", lambda e, sl=sl, half=half, pwo=pwo: e.tensor_tensor(
                        y2f[sl][:, 512 * half:512 * half + 512], pwo,
                        yf[sl][:, 512 * half:512 * half + 512], op=ALU.add), inc=SV)
                    wo_t = [(SV, ty2)]
                yf_cons[sl].append((SV, ty2))
                W("gpsimd", SV, ty2)
                tsy = DMA("gpsimd", yout_h[128 * i:128 * i + 128, :], y2f[sl][:], f"sy2{sl}")
                y2_t[sl] = (f"sy2{sl}", tsy)
            nj = (len(wconv_jobs) + 7 - w) // (8 - w) if w < 8 else 0
            for _ in range(min(nj, len(wconv_jobs))):
                j_ = wconv_jobs.pop(0)
                emit_wconv(*j_)
            for gg in range(4):
                oT_t[gg] = (ST, b.cur(ST))
            pout_t[1] = [(SV, b.cur(SV))]
        arb.reset(m_b)
        arf.reset(m_f)

    def ph_ab(l, yin_h, yout_h):
        m_b = arb.mark()
        m_f = arf.mark()
        bank_t = {k: [] for k in ("0a", "0b", "1a", "1b", "2a", "2b", "pb0", "pb1")}

        def bwait(eng, keys):
            for k in keys:
                for sem, t in bank_t[k]:
                    W(eng, sem, t)

        def bset(keys, consumers):
            for k in keys:
                bank_t[k] = list(consumers)

        rot = [arb.alloc(4096) for _ in range(4)]
        t_qs = DMA("sync", qsks_sb[:], qsks_h[l], "dqsks")
        rot_t = []
        for r in range(4):
            tl_ = DMA("sync", rot[r], rotc_h[r], f"drot{r}")
            W("vector", f"drot{r}", tl_)
            W("vector", "dqsks", t_qs)
            rot_t.append(OP("vector", lambda e, r=r, rr=rot[r]: e.tensor_scalar(
                rr, rr, qsks_sb[:, r:r + 1], None, op0=ALU.mult), inc=SV))
        wbf = [arb.alloc(1536) for _ in range(8)]
        wt = {}
        for d in range(8):
            eng = "scalar" if d % 2 == 0 else "vector"
            t2 = load_conv(wqkv_h[l, 128 * d:128 * d + 128, :], wbf[d][:], 1536, eng)
            wt[d] = ({"scalar": SC, "vector": SV}[eng], t2)
        yf = [arf.alloc(512) for _ in range(2)]
        ysh = [arf.alloc(512) for _ in range(1)]
        hbf = [arb.alloc(1024) for _ in range(2)]
        hT = [arb.alloc(4096) for _ in range(2)]
        sqb = [arb.alloc(512) for _ in range(3)]
        raw = [arb.alloc(512) for _ in range(8)]
        krnb = [arb.alloc(512) for _ in range(4)]
        qrnb = [arb.alloc(512) for _ in range(4)]
        m1b = [arb.alloc(512) for _ in range(2)]
        m2b = [arb.alloc(512) for _ in range(2)]
        fout = [arb.alloc(512) for _ in range(3)]
        vout = [arb.alloc(512) for _ in range(2)]
        ssf = arf.alloc(512)
        rln = arf.alloc(512)
        r_all = rln
        yf_t = [0, 0]
        yf_cons = [[], []]
        ysh_cons = [[]]
        hbf_cons = [0, 0]
        hT_cons = [0, 0]
        hT_t = [0, 0]
        sq_t = [(None, 0)] * 3
        raw_cons = [[] for _ in range(8)]
        krn_t = [[], [], [], []]
        qrn_t = [[], [], [], []]
        m1_t = [(None, 0)] * 2
        m2_t = [(None, 0)] * 2
        fout_t = [(None, 0)] * 3
        vout_t = [(None, 0)] * 2
        pool = [("0a", P0[:, 0:512]), ("0b", P0[:, 512:1024]),
                ("1a", P1[:, 0:512]), ("2b", P2[:, 512:1024])]
        ssps = P2[0:16, 0:512]
        PBSL = [("pb0", PB0), ("pb1", PB1)]
        pli = [0]
        sqi = [0]
        fi = [0]
        m12i = [0]
        voi = [0]
        pbi = [0]
        fq_st = {}
        fk_st = {}
        fv_st = {}

        def pool_next():
            s = pli[0] % 4
            pli[0] += 1
            return pool[s]

        wo2bf = [arb.alloc(1024) for _ in range(4)]
        wtB = {}
        for ic in range(4):
            eng = "scalar" if ic % 2 == 0 else "vector"
            wtB[ic] = ({"scalar": SC, "vector": SV}[eng],
                       load_conv(wo2_h[l, 128 * ic:128 * ic + 128, :], wo2bf[ic][:], 1024, eng))
        kwb = [arb.alloc(1024) for _ in range(4)]
        qwb = [arb.alloc(512) for _ in range(4)]
        vwb = [arb.alloc(512) for _ in range(8)]
        atb = [arb.alloc(640) for _ in range(3)]
        amb = [arb.alloc(640) for _ in range(3)]
        anb = [arb.alloc(640) for _ in range(3)]
        atT = [arb.alloc(640) for _ in range(3)]
        oT = [arb.alloc(512) for _ in range(4)]
        yfB = [arf.alloc(1024) for _ in range(2)]
        y2f = [arf.alloc(1024) for _ in range(2)]
        kw_cons = [[] for _ in range(4)]
        qw_cons = [[] for _ in range(4)]
        vw_cons = [[] for _ in range(8)]
        kw_t = [0] * 4
        qw_t = [0] * 4
        vw_t = [0] * 8
        at_t = [(None, 0)] * 4
        am_t = [(None, 0)] * 4
        an_t = [(None, 0)] * 4
        atT_t = [[], [], [], []]
        oT_t = [(None, 0)] * 4
        yfB_cons = [[], []]
        y2_t = [(None, 0)] * 2
        vw_copy = [0, 0, 0, 0]
        pend = [None]
        ai = [0]
        smbase = 96

        def emit_tail(info):
            a2 = info["a2"]; nq = info["nq"]; njc = info["njc"]
            vbase = info["vbase"]; r0 = info["r0"]; hp = info["hp"]
            h2 = info["h2"]; tan = info["tan"]; pout = info["pout"]; gg = info["gg"]
            pbsl = pbi[0] % 2
            pbi[0] += 1
            pbk, PB = PBSL[pbsl]
            bwait("tensor", [pbk])
            W("tensor", SV, tan)
            for j in range(njc):
                wd = min(128, nq - 128 * j)
                ttr = OP("tensor", lambda e, PB=PB, a2=a2, j=j, wd=wd: e.transpose(
                    PB[:, 128 * j:128 * j + wd], anb[a2][:, 128 * j:128 * j + wd], IDB[:]),
                    inc=ST)
            an_t[a2] = (ST, ttr)
            for sem, t in atT_t[a2]:
                W("scalar", sem, t)
            W("scalar", ST, ttr)
            tate = OP("scalar", lambda e, PB=PB, a2=a2, nq=nq: e.copy(
                atT[a2][:, 0:nq], PB[:, 0:nq]), inc=SC)
            bset([pbk], [(SC, tate)])
            W("tensor", SC, tate)
            for j in range(njc):
                vs = vbase + j
                if vs >= 4:
                    W("tensor", f"dvw{vs - 4}", vw_t[vs])
                elif vw_copy[vs]:
                    W("tensor", SV, vw_copy[vs])
                wd = min(128, nq - 128 * j)
                tav = OP("tensor", lambda e, vs=vs, h2=h2, hp=hp, a2=a2, j=j, r0=r0,
                         wd=wd, pout=pout: e.matmul(
                    pout[64 * hp:64 * hp + 64, 128 * r0:128 * r0 + 128],
                    vwb[vs][:, 64 * h2:64 * h2 + 64],
                    atT[a2][:, 128 * j:128 * j + wd],
                    start=(j == 0), stop=(j == njc - 1),
                    skip_group_check=True), inc=ST)
                vw_cons[vs].append((ST, tav))
            atT_t[a2] = [(ST, tav)]
            if info["last"]:
                ose, ost = oT_t[gg]
                if ose is not None:
                    W("scalar", ose, ost)
                W("scalar", ST, b.cur(ST))
                toev = OP("scalar", lambda e, gg=gg, pout=pout: e.copy(oT[gg][:], pout), inc=SC)
                oT_t[gg] = (SC, toev)
                bset([info["poutk"]], [(SC, toev)])

        def emit_A(tt):
            tsl = tt % 2
            for s in range(4):
                i = 4 * tt + s
                sl = i % 2
                for sem, t in yf_cons[sl]:
                    W("sync", sem, t)
                yf_cons[sl] = []
                yf_t[sl] = DMA("sync", yf[sl][:], yin_h[128 * i:128 * i + 128, 0:512], f"dyf{sl}")
                for sem, t in ysh_cons[0]:
                    W("sync", sem, t)
                ysh_cons[0] = []
                if i == 0:
                    DMA("sync", ysh[0][1:128, :], yin_h[0:127, 512:1024], "dys0")
                    tsh = DMA("sync", ysh[0][0:1, :], zrow_h[0:1, 0:512], "dys0")
                else:
                    tsh = DMA("sync", ysh[0][:], yin_h[128 * i - 1:128 * i + 127, 512:1024], "dys0")
                if hbf_cons[sl]:
                    W("vector", ST, hbf_cons[sl])
                W("vector", f"dyf{sl}", yf_t[sl])
                tc1 = OP("vector", lambda e, sl=sl: e.tensor_copy(
                    hbf[sl][:, 0:512], yf[sl][:]), inc=SV)
                W("vector", "dys0", tsh)
                tc3 = OP("vector", lambda e, sl=sl: e.tensor_copy(
                    hbf[sl][:, 512:1024], ysh[0][:]), inc=SV)
                yf_cons[sl].append((SV, tc1))
                ysh_cons[0].append((SV, tc3))
                pbsl = pbi[0] % 2
                pbi[0] += 1
                pbk, PB = PBSL[pbsl]
                bwait("tensor", [pbk])
                W("tensor", SV, tc3)
                for k in range(8):
                    tmm = OP("tensor", lambda e, k=k, PB=PB, sl=sl: e.transpose(
                        PB[:, 128 * k:128 * k + 128], hbf[sl][:, 128 * k:128 * k + 128], IDB[:]),
                        inc=ST)
                hbf_cons[sl] = tmm
                dst = hT[tsl].rearrange("p (k c) -> p k c", k=8)
                W("scalar", ST, tmm)
                if s == 0 and hT_cons[tsl]:
                    W("scalar", ST, hT_cons[tsl])
                te1 = OP("scalar", lambda e, PB=PB, dst=dst, s=s: e.copy(
                    dst[:, 0:4, 128 * s:128 * s + 128], PB[:, 0:512].rearrange("p (k c) -> p k c", k=4)), inc=SC)
                te2 = OP("scalar", lambda e, PB=PB, dst=dst, s=s: e.copy(
                    dst[:, 4:8, 128 * s:128 * s + 128], PB[:, 512:1024].rearrange("p (k c) -> p k c", k=4)), inc=SC)
                bset([pbk], [(SC, te2)])
                hT_t[tsl] = te2
            raw_t = {}
            for gg in range(8):
                pk_, pp = pool_next()
                bwait("tensor", [pk_])
                W("tensor", SC, hT_t[tsl])
                for d in range(8):
                    W("tensor", wt[d][0], wt[d][1])
                    tmm = OP("tensor", lambda e, d=d, gg=gg, pp=pp, tsl=tsl: e.matmul(
                        pp, wbf[d][:, 128 * gg:128 * gg + 128],
                        hT[tsl][:, 512 * d:512 * d + 512],
                        start=(d == 0), stop=(d == 7)), inc=ST)
                sqsl = sqi[0] % 3
                sqi[0] += 1
                sse, sst = sq_t[sqsl]
                W("scalar", ST, tmm)
                if sse is not None:
                    W("scalar", sse, sst)
                tsq = OP("scalar", lambda e, sqsl=sqsl, pp=pp: e.activation(
                    sqb[sqsl][:], pp, AF.Square), inc=SC)
                for sem, t in raw_cons[gg]:
                    W("scalar", sem, t)
                raw_cons[gg] = []
                trw = OP("scalar", lambda e, gg=gg, pp=pp: e.copy(raw[gg][:], pp), inc=SC)
                raw_t[gg] = trw
                W("tensor", SC, tsq)
                if gg == 0:
                    W("tensor", "dE8", b.cur("dE8"))
                    bwait("tensor", ["2a"])
                tss = OP("tensor", lambda e, gg=gg, sqsl=sqsl: e.matmul(
                    ssps, E8_sb[:, 16 * gg:16 * gg + 16], sqb[sqsl][:],
                    start=(gg == 0), stop=(gg == 7)), inc=ST)
                sq_t[sqsl] = (ST, tss)
                bset([pk_], [(SC, trw)])
            W("vector", ST, tss)
            t1 = OP("vector", lambda e: e.tensor_scalar(ssf[0:16, :], ssps, 6e-20, None,
                                                        op0=ALU.max), inc=SV)
            bset(["2a"], [(SV, t1)])
            W("scalar", SV, t1)
            t2_ = OP("scalar", lambda e: e.activation(rln[0:16, :], ssf[0:16, :], AF.Ln), inc=SC)
            W("scalar", SC, t2_)
            t3_ = OP("scalar", lambda e: e.activation(ssf[0:16, :], rln[0:16, :], AF.Exp, scale=-0.5), inc=SC)
            W("vector", SC, t3_)
            t4_ = OP("vector", lambda e: e.tensor_scalar(r_all[0:16, :], ssf[0:16, :], 1e12, None,
                                                         op0=ALU.min), inc=SV)
            rotq = {}
            for gg in range(4):
                pk_, Rps = pool_next()
                bwait("tensor", [pk_])
                W("tensor", SV, t4_)
                if tt == 0 and gg == 0:
                    W("tensor", "dSel", b.cur("dSel"))
                tR = OP("tensor", lambda e, gg=gg, Rps=Rps: e.matmul(
                    Rps, SelH_sb[0:16, 128 * gg:128 * gg + 128], r_all[0:16, :],
                    start=True, stop=True), inc=ST)
                rotq[gg] = [Rps, tR, pk_, None, None]
            for gg in range(4):
                Rps, tR, pk_, _, _ = rotq[gg]
                for kse, kst in qrn_t[gg]:
                    W("vector", kse, kst)
                W("vector", SC, raw_t[gg])
                W("vector", ST, tR)
                tqrn = OP("vector", lambda e, gg=gg, Rps=Rps: e.tensor_tensor(
                    qrnb[gg][:], raw[gg][:], Rps, op=ALU.mult), inc=SV)
                raw_cons[gg] = [(SV, tqrn)]
                bset([pk_], [(SV, tqrn)])
                rotq[gg][3] = tqrn
            for gg in range(4):
                tqrn = rotq[gg][3]
                pk_b, rotps = pool_next()
                bwait("tensor", [pk_b])
                W("tensor", SV, tqrn)
                if tt == 0 and gg == 0:
                    W("tensor", "dPROT", b.cur("dPROT"))
                trot = OP("tensor", lambda e, gg=gg, rotps=rotps: e.matmul(
                    rotps, PROT_sb[:], qrnb[gg][:], start=True, stop=True), inc=ST)
                rotq[gg][4] = (rotps, trot, pk_b)
            for gg in range(4):
                tqrn = rotq[gg][3]
                rotps, trot, pk_b = rotq[gg][4]
                m1sl = m12i[0] % 2
                m12i[0] += 1
                mse, mst = m1_t[m1sl]
                if mse is not None:
                    W("vector", mse, mst)
                W("vector", SV, rot_t[0])
                tm1 = OP("vector", lambda e, gg=gg, m1sl=m1sl, tt=tt: e.tensor_tensor(
                    m1b[m1sl][:], qrnb[gg][:], rot[0][:, 512 * tt:512 * tt + 512], op=ALU.mult), inc=SV)
                mse, mst = m2_t[m1sl]
                if mse is not None:
                    W("vector", mse, mst)
                W("vector", ST, trot)
                W("vector", SV, rot_t[1])
                tm2 = OP("vector", lambda e, m1sl=m1sl, rotps=rotps, tt=tt: e.tensor_tensor(
                    m2b[m1sl][:], rotps, rot[1][:, 512 * tt:512 * tt + 512], op=ALU.mult), inc=SV)
                qrn_t[gg] = [(SV, tm1), (ST, trot)]
                fsl = fi[0] % 3
                fi[0] += 1
                fse, fst = fout_t[fsl]
                if fse is not None:
                    W("vector", fse, fst)
                W("vector", SV, tm2)
                tqf = OP("vector", lambda e, fsl=fsl, m1sl=m1sl: e.tensor_tensor(
                    fout[fsl][:], m1b[m1sl][:], m2b[m1sl][:], op=ALU.add), inc=SV)
                m1_t[m1sl] = (SV, tqf)
                m2_t[m1sl] = (SV, tqf)
                bset([pk_b], [(SV, tm2)])
                W("gpsimd", SV, tqf)
                tst = DMA("gpsimd", sc_fq[128 * gg:128 * gg + 128, 512 * tt:512 * tt + 512],
                          fout[fsl][:], f"sfo{fsl}")
                fout_t[fsl] = (f"sfo{fsl}", tst)
                fq_st[(tt, gg)] = (f"sfo{fsl}", tst)
            rotk = {}
            for gk in range(4):
                pk_, Rps = pool_next()
                bwait("tensor", [pk_])
                W("tensor", SV, t4_)
                tR = OP("tensor", lambda e, gk=gk, Rps=Rps: e.matmul(
                    Rps, SelH_sb[0:16, 512 + 128 * gk:512 + 128 * gk + 128], r_all[0:16, :],
                    start=True, stop=True), inc=ST)
                rotk[gk] = [Rps, tR, pk_, None, None]
            for gk in range(4):
                gg = 4 + gk
                Rps, tR, pk_, _, _ = rotk[gk]
                for kse, kst in krn_t[gk]:
                    W("vector", kse, kst)
                W("vector", SC, raw_t[gg])
                W("vector", ST, tR)
                tkrn = OP("vector", lambda e, gg=gg, gk=gk, Rps=Rps: e.tensor_tensor(
                    krnb[gk][:], raw[gg][:], Rps, op=ALU.mult), inc=SV)
                raw_cons[gg] = [(SV, tkrn)]
                bset([pk_], [(SV, tkrn)])
                rotk[gk][3] = tkrn
            for gk in range(4):
                tkrn = rotk[gk][3]
                pk_b, rotps = pool_next()
                bwait("tensor", [pk_b])
                W("tensor", SV, tkrn)
                trot = OP("tensor", lambda e, gk=gk, rotps=rotps: e.matmul(
                    rotps, PROT_sb[:], krnb[gk][:], start=True, stop=True), inc=ST)
                rotk[gk][4] = (rotps, trot, pk_b)
            for gk in range(4):
                tkrn = rotk[gk][3]
                rotps, trot, pk_b = rotk[gk][4]
                m1sl = m12i[0] % 2
                m12i[0] += 1
                mse, mst = m1_t[m1sl]
                if mse is not None:
                    W("vector", mse, mst)
                W("vector", SV, rot_t[2])
                tm1 = OP("vector", lambda e, gk=gk, m1sl=m1sl, tt=tt: e.tensor_tensor(
                    m1b[m1sl][:], krnb[gk][:], rot[2][:, 512 * tt:512 * tt + 512], op=ALU.mult), inc=SV)
                mse, mst = m2_t[m1sl]
                if mse is not None:
                    W("vector", mse, mst)
                W("vector", ST, trot)
                W("vector", SV, rot_t[3])
                tm2 = OP("vector", lambda e, m1sl=m1sl, rotps=rotps, tt=tt: e.tensor_tensor(
                    m2b[m1sl][:], rotps, rot[3][:, 512 * tt:512 * tt + 512], op=ALU.mult), inc=SV)
                krn_t[gk] = [(SV, tm1), (ST, trot)]
                fsl = fi[0] % 3
                fi[0] += 1
                fse, fst = fout_t[fsl]
                if fse is not None:
                    W("vector", fse, fst)
                W("vector", SV, tm2)
                tkf = OP("vector", lambda e, fsl=fsl, m1sl=m1sl: e.tensor_tensor(
                    fout[fsl][:], m1b[m1sl][:], m2b[m1sl][:], op=ALU.add), inc=SV)
                m1_t[m1sl] = (SV, tkf)
                m2_t[m1sl] = (SV, tkf)
                bset([pk_b], [(SV, tm2)])
                W("gpsimd", SV, tkf)
                tst = DMA("gpsimd", sc_fk[128 * gk:128 * gk + 128, 512 * tt:512 * tt + 512],
                          fout[fsl][:], f"sfo{fsl}")
                fout_t[fsl] = (f"sfo{fsl}", tst)
                fk_st[(tt, gk)] = (f"sfo{fsl}", tst)
            for s in range(4):
                i = 4 * tt + s
                pk_, pp = pool_next()
                bwait("tensor", [pk_])
                for d in range(8):
                    tmm = OP("tensor", lambda e, d=d, s=s, pp=pp, tsl=tsl: e.matmul(
                        pp, hT[tsl][:, 512 * d + 128 * s:512 * d + 128 * s + 128],
                        wbf[d][:, 1024:1536], start=(d == 0), stop=(d == 7)), inc=ST)
                vsl = voi[0] % 2
                voi[0] += 1
                vse, vst = vout_t[vsl]
                if vse is not None:
                    W("scalar", vse, vst)
                W("scalar", ST, tmm)
                tve = OP("scalar", lambda e, vsl=vsl, pp=pp: e.copy(vout[vsl][:], pp), inc=SC)
                bset([pk_], [(SC, tve)])
                W("gpsimd", SC, tve)
                tst = DMA("gpsimd", sc_fv[128 * i:128 * i + 128, :], vout[vsl][:], f"svo{vsl}")
                vout_t[vsl] = (f"svo{vsl}", tst)
                fv_st[(tt, s)] = (f"svo{vsl}", tst)
            hT_cons[tsl] = b.cur(ST)

        def emit_B(w):
            wo_t = []
            for gg in range(4):
                if w > 0:
                    for sem, t in kw_cons[gg]:
                        W("vector", sem, t)
                    tcp = OP("vector", lambda e, gg=gg: e.tensor_copy(
                        kwb[gg][:, 0:512], kwb[gg][:, 512:1024]), inc=SV)
                    W("sync", SV, tcp)
                kw_cons[gg] = []
                W("sync", fk_st[(w, gg)][0], fk_st[(w, gg)][1])
                kw_t[gg] = DMA("sync", kwb[gg][:, 512:1024],
                               sc_fk[128 * gg:128 * gg + 128, 512 * w:512 * w + 512], f"dkw{gg}")
                for sem, t in qw_cons[gg]:
                    W("sync", sem, t)
                qw_cons[gg] = []
                W("sync", fq_st[(w, gg)][0], fq_st[(w, gg)][1])
                qw_t[gg] = DMA("sync", qwb[gg][:],
                               sc_fq[128 * gg:128 * gg + 128, 512 * w:512 * w + 512], f"dqw{gg}")
            for j4 in range(4):
                if w > 0:
                    for sem, t in vw_cons[j4]:
                        W("vector", sem, t)
                    for sem, t in vw_cons[4 + j4]:
                        W("vector", sem, t)
                    tcp = OP("vector", lambda e, j4=j4: e.tensor_copy(
                        vwb[j4][:], vwb[4 + j4][:]), inc=SV)
                    W("sync", SV, tcp)
                    vw_cons[j4] = []
                    vw_cons[4 + j4] = [(SV, tcp)]
                    vw_copy[j4] = tcp
                W("sync", fv_st[(w, j4)][0], fv_st[(w, j4)][1])
                vw_t[4 + j4] = DMA("sync", vwb[4 + j4][:],
                                   sc_fv[512 * w + 128 * j4:512 * w + 128 * j4 + 128, :], f"dvw{j4}")
            for gg in range(4):
                poutk = "2a" if gg % 2 == 0 else "2b"
                pout = P2[:, 0:512] if gg % 2 == 0 else P2[:, 512:1024]
                first_inst_of_pair = True
                for r0 in range(4):
                    i = 4 * w + r0
                    if w == 0:
                        nq = 128 * (r0 + 1)
                        cst = 512
                        njc = r0 + 1
                        vbase = 4
                        mask = MW0[:, [0, 128, 384, 768][r0]:[0, 128, 384, 768][r0] + nq]
                    else:
                        nq = 640
                        cst = 128 * r0
                        njc = 5
                        vbase = r0
                        mask = MB[:, 0:640]
                    for hp in range(2):
                        h2 = 2 * gg + hp
                        simk = ["0a", "0b"] if hp == 0 else ["1a", "1b"]
                        if nq <= 512:
                            simk = simk[:1]
                        psim = (P0 if hp == 0 else P1)[:, 0:nq]
                        bwait("tensor", simk)
                        if first_inst_of_pair:
                            bwait("tensor", [poutk])
                            first_inst_of_pair = False
                        W("tensor", f"dkw{gg}", kw_t[gg])
                        W("tensor", f"dqw{gg}", qw_t[gg])
                        nmm = (nq + 511) // 512
                        off = 0
                        for z in range(nmm):
                            wd = min(512, nq - off)
                            tsim = OP("tensor", lambda e, gg=gg, hp=hp, r0=r0, psim=psim,
                                      cst=cst, off=off, wd=wd: e.matmul(
                                psim[:, off:off + wd],
                                qwb[gg][64 * hp:64 * hp + 64, 128 * r0:128 * r0 + 128],
                                kwb[gg][64 * hp:64 * hp + 64, cst + off:cst + off + wd],
                                start=True, stop=True), inc=ST)
                            off += wd
                        qw_cons[gg].append((ST, tsim))
                        kw_cons[gg].append((ST, tsim))
                        cb = smbase + 4 * (ai[0] * 2 + hp) % 32
                        nm = smf[:, cb:cb + 1]
                        W("vector", ST, tsim)
                        tnm = OP("vector", lambda e, psim=psim, nm=nm: e.tensor_reduce(
                            nm, psim, AX.X, op=ALU.max, negate=True), inc=SV)
                        asl = ai[0] * 2 + hp
                        a2 = asl % 3
                        ase, ast = at_t[a2]
                        if ase is not None:
                            W("scalar", ase, ast)
                        W("scalar", SV, tnm)
                        texp = OP("scalar", lambda e, psim=psim, a2=a2, nq=nq, nm=nm: e.activation(
                            atb[a2][:, 0:nq], psim, AF.Exp, bias=nm), inc=SC)
                        bset(simk, [(SC, texp), (SV, tnm)])
                        rsum = smf[:, cb + 2:cb + 3]
                        rc = smf[:, cb + 3:cb + 4]
                        mse, mst = am_t[a2]
                        if mse is not None:
                            W("vector", mse, mst)
                        W("vector", SC, texp)
                        tstt = OP("vector", lambda e, a2=a2, nq=nq, mask=mask, rsum=rsum: e.scalar_tensor_tensor(
                            amb[a2][:, 0:nq], atb[a2][:, 0:nq], 1.0, mask,
                            op0=ALU.mult, op1=ALU.mult, accum_out=rsum), inc=SV)
                        at_t[a2] = (SV, tstt)
                        W("vector", SV, tstt)
                        trc = OP("vector", lambda e, rc=rc, rsum=rsum: e.reciprocal(rc, rsum), inc=SV)
                        nse, nst = an_t[a2]
                        if nse is not None:
                            W("vector", nse, nst)
                        W("vector", SV, trc)
                        tan = OP("vector", lambda e, a2=a2, nq=nq, rc=rc: e.tensor_scalar(
                            anb[a2][:, 0:nq], amb[a2][:, 0:nq], rc, None, op0=ALU.mult), inc=SV)
                        am_t[a2] = (SV, tan)
                        info = dict(a2=a2, nq=nq, njc=njc, vbase=vbase, r0=r0, hp=hp,
                                    h2=h2, tan=tan, pout=pout, gg=gg, poutk=poutk,
                                    last=(r0 == 3 and hp == 1))
                        if pend[0] is not None:
                            emit_tail(pend[0])
                        pend[0] = info
                    ai[0] += 1
            if pend[0] is not None:
                emit_tail(pend[0])
                pend[0] = None
            oev = b.cur(SC)
            for r0 in range(4):
                i = 4 * w + r0
                sl = i % 2
                for sem, t in yfB_cons[sl]:
                    W("sync", sem, t)
                yfB_cons[sl] = []
                tyf = DMA("sync", yfB[sl][:], yin_h[128 * i:128 * i + 128, :], f"dyb{sl}")
                for half in range(2):
                    for sem, t in wo_t:
                        W("tensor", sem, t)
                    wo_t = []
                    pwo = P2[:, 512:1024]
                    if r0 == 0 and half == 0:
                        bwait("tensor", ["2b"])
                    for ic in range(4):
                        W("tensor", SC, oev)
                        W("tensor", wtB[ic][0], wtB[ic][1])
                        tmm = OP("tensor", lambda e, ic=ic, r0=r0, half=half, pwo=pwo: e.matmul(
                            pwo, oT[ic][:, 128 * r0:128 * r0 + 128],
                            wo2bf[ic][:, 512 * half:512 * half + 512],
                            start=(ic == 0), stop=(ic == 3)), inc=ST)
                    y2e, y2tt = y2_t[sl]
                    if half == 0 and y2e is not None:
                        W("vector", y2e, y2tt)
                    W("vector", f"dyb{sl}", tyf)
                    W("vector", ST, tmm)
                    ty2 = OP("vector", lambda e, sl=sl, half=half, pwo=pwo: e.tensor_tensor(
                        y2f[sl][:, 512 * half:512 * half + 512], pwo,
                        yfB[sl][:, 512 * half:512 * half + 512], op=ALU.add), inc=SV)
                    wo_t = [(SV, ty2)]
                yfB_cons[sl].append((SV, ty2))
                W("gpsimd", SV, ty2)
                tsy = DMA("gpsimd", yout_h[128 * i:128 * i + 128, :], y2f[sl][:], f"sy2{sl}")
                y2_t[sl] = (f"sy2{sl}", tsy)
            for gg in range(4):
                oT_t[gg] = (ST, b.cur(ST))
            bset(["2b"], [(SV, b.cur(SV))])

        for tt in range(8):
            emit_A(tt)
            if tt >= 1:
                emit_B(tt - 1)
        emit_B(7)
        arb.reset(m_b)
        arf.reset(m_f)

    def ph_c(l, yin_h, yout_h):
        m_b = arb.mark()
        m_f = arf.mark()
        w1bf = [arb.alloc(W1C) for _ in range(8)]
        w2bf = [arb.alloc(1024) for _ in range(22)]
        b2row = arb.alloc(1024)
        tlnT = arb.alloc(2048)
        actT = [arb.alloc(256) for _ in range(22)]
        tlnb = arb.alloc(1024)
        gb = [arb.alloc(256) for _ in range(2)]
        ab2 = [arb.alloc(256) for _ in range(2)]
        y2f = [arf.alloc(1024) for _ in range(3)]
        tf = [arf.alloc(1024) for _ in range(2)]
        rowb = arf.alloc(512)
        w1t = {}
        for part in (0, 2, 3, 1, 4, 5):
            tlast = None
            for d in range(8):
                c0 = 1024 * part
                cw = min(1024, W1C - c0)
                tlast = DMA("sync", w1bf[d][:, c0:c0 + cw],
                            sc_w1[l, 128 * d:128 * d + 128, c0:c0 + cw], f"dw1p{part}")
            for d in range(8):
                w1t[(d, part)] = (f"dw1p{part}", tlast)
        w2t = {}
        tw2last = None
        for c in range(22):
            tw2last = DMA("sync", w2bf[c][:], sc_w2[l, 128 * c:128 * c + 128, :], "dw2g")
        for c in range(22):
            w2t[c] = ("dw2g", tw2last)
        tb2d = DMA("sync", b2row[0:1, :], sc_w2[l, 2816:2817, :], "db2d")
        tb2 = None
        t_b1 = DMA("sync", b1c_sb[:], b1c_h[l], "db1c")
        t_lb = DMA("sync", lbc_sb[:], lbc_h[l], "dlbc")
        t_lg = DMA("sync", lgc_sb[:], lgc_h[l], "dlgc")
        b1e_t = {}
        prev_cg = [None]
        b1e_done = set()

        def emit_b1e(cg):
            if cg in b1e_done:
                return
            b1e_done.add(cg)
            prow = PK[4][0:1, :]
            for d in range(8):
                wp = w1t[(d, cg // 2)]
                W("tensor", wp[0], wp[1])
                if d == 0:
                    W("tensor", "dlbc", t_lb)
                    if prev_cg[0] is not None:
                        W("tensor", SC, trow[0])
                tmm = OP("tensor", lambda e, d=d, cg=cg, prow=prow: e.matmul(
                    prow, lbc_sb[:, d:d + 1], w1bf[d][:, 512 * cg:512 * cg + 512],
                    start=(d == 0), stop=(d == 7)), inc=ST)
            W("scalar", ST, tmm)
            trow[0] = OP("scalar", lambda e, prow=prow: e.copy(rowb[0:1, :], prow), inc=SC)
            for m4 in range(4):
                pcol = PK[5][:, 4 * m4:4 * m4 + 1]
                W("tensor", SC, trow[0])
                if prev_cg[0] is not None:
                    W("tensor", SV, b1e_t[4 * prev_cg[0] + m4])
                ttr = OP("tensor", lambda e, m4=m4, pcol=pcol: e.transpose(
                    pcol, rowb[0:1, 128 * m4:128 * m4 + 128], IDF[0:1, 0:1]), inc=ST)
                W("vector", ST, ttr)
                if cg == 0 and m4 == 0:
                    W("vector", "db1c", t_b1)
                cidx = 4 * cg + m4
                tbe = OP("vector", lambda e, cidx=cidx, pcol=pcol: e.tensor_tensor(
                    b1e[:, cidx:cidx + 1], pcol, b1c_sb[:, cidx:cidx + 1], op=ALU.add), inc=SV)
                b1e_t[cidx] = tbe
            prev_cg[0] = cg
        trow = [None]
        act_cons = [(None, 0)] * 22
        w2ps_t = [[], []]
        yf_t = [0] * 3
        yf_cons = [[] for _ in range(3)]
        tf_cons = [[], []]
        tlnb_cons = [0]
        tlnT_cons = [0]
        tlnT_t = [0]
        pb_t = [(None, 0)] * 2
        ag_t = [[], [], [], []]
        gb_t = [(None, 0)] * 2
        ab_t = [(None, 0)] * 2
        act_t = [(None, 0)] * 22
        y3_t = [(None, 0)] * 2
        pbi = [0]
        def ln_stage(tc):
            for s2 in range(2):
                i2 = 2 * tc + s2
                tsl2 = s2
                for sem, t in tf_cons[tsl2]:
                    W("sync", sem, t)
                tf_cons[tsl2] = []
                DMA("sync", tf[tsl2][:, 0:512], yin_h[128 * i2:128 * i2 + 128, 0:512], f"dtf{tsl2}")
                if i2 == 0:
                    DMA("sync", tf[tsl2][1:128, 512:1024], yin_h[0:127, 512:1024], f"dtf{tsl2}")
                    ttf = DMA("sync", tf[tsl2][0:1, 512:1024], zrow_h[0:1, 0:512], f"dtf{tsl2}")
                else:
                    ttf = DMA("sync", tf[tsl2][:, 512:1024],
                              yin_h[128 * i2 - 1:128 * i2 + 127, 512:1024], f"dtf{tsl2}")
                if tlnb_cons[0]:
                    W("vector", ST, tlnb_cons[0])
                tap = emit_ln(tf[tsl2][:], tlnb[:], ttf, f"dtf{tsl2}", c0=64 + 16 * s2)
                tf_cons[tsl2] = [(SV, tap)]
                pbsl = pbi[0] % 2
                pbi[0] += 1
                PB = PB0 if pbsl == 0 else PB1
                ps, pt = pb_t[pbsl]
                if ps is not None:
                    W("tensor", ps, pt)
                W("tensor", SV, tap)
                for k in range(8):
                    tmm = OP("tensor", lambda e, k=k, PB=PB: e.transpose(
                        PB[:, 128 * k:128 * k + 128], tlnb[:, 128 * k:128 * k + 128], IDB[:]),
                        inc=ST)
                tlnb_cons[0] = tmm
                W("scalar", ST, tmm)
                if s2 == 0 and tlnT_cons[0]:
                    W("scalar", ST, tlnT_cons[0])
                if i2 == 0:
                    W("scalar", "dlgc", t_lg)
                for k in range(8):
                    te = OP("scalar", lambda e, k=k, PB=PB, s2=s2: e.activation(
                        tlnT[:, 256 * k + 128 * s2:256 * k + 128 * s2 + 128],
                        PB[:, 128 * k:128 * k + 128], AF.Copy,
                        scale=lgc_sb[:, k:k + 1]), inc=SC)
                pb_t[pbsl] = (SC, te)
                tlnT_t[0] = te

        ln_stage(0)
        for tc in range(16):
            tev = tlnT_t[0]
            for c in range(22):
                pa = 2 * (c % 2)
                pg = 2 * (c % 2) + 1
                ppa = PK[pa][:, 0:256]
                ppg = PK[pg][:, 0:256]
                for sem, t in ag_t[pa]:
                    W("tensor", sem, t)
                for sem, t in ag_t[pg]:
                    W("tensor", sem, t)
                if tc == 0:
                    emit_b1e(c // 4)
                    emit_b1e((22 + c) // 4)
                W("tensor", SC, tev)
                for d in range(8):
                    if tc == 0:
                        wp = w1t[(d, c // 8)]
                        W("tensor", wp[0], wp[1])
                        wp = w1t[(d, (22 + c) // 8)]
                        W("tensor", wp[0], wp[1])
                    tma = OP("tensor", lambda e, d=d, c=c, ppa=ppa: e.matmul(
                        ppa, w1bf[d][:, 128 * c:128 * c + 128], tlnT[:, 256 * d:256 * d + 256],
                        start=(d == 0), stop=(d == 7)), inc=ST)
                for d in range(8):
                    tmg = OP("tensor", lambda e, d=d, c=c, ppg=ppg: e.matmul(
                        ppg, w1bf[d][:, 128 * (22 + c):128 * (22 + c) + 128],
                        tlnT[:, 256 * d:256 * d + 256],
                        start=(d == 0), stop=(d == 7)), inc=ST)
                g2 = c % 2
                gse, gst = gb_t[g2]
                if gse is not None:
                    W("scalar", gse, gst)
                W("scalar", ST, tmg)
                if tc == 0:
                    W("scalar", SV, b1e_t[22 + c])
                tgel = OP("scalar", lambda e, g2=g2, c=c, ppg=ppg: e.activation(
                    gb[g2][:], ppg, AF.Gelu, bias=b1e[:, 22 + c:22 + c + 1]), inc=SC)
                ase, ast = ab_t[g2]
                if ase is not None:
                    W("vector", ase, ast)
                W("vector", ST, tma)
                if tc == 0:
                    W("vector", SV, b1e_t[c])
                tab = OP("vector", lambda e, g2=g2, c=c, ppa=ppa: e.tensor_scalar(
                    ab2[g2][:], ppa, 1.0, b1e[:, c:c + 1], op0=ALU.mult, op1=ALU.add), inc=SV)
                ace, act_ = act_cons[c]
                if ace is not None:
                    W("vector", ace, act_)
                W("vector", SV, tab)
                W("vector", SC, tgel)
                tact = OP("vector", lambda e, c=c, g2=g2: e.tensor_tensor(
                    actT[c][:], ab2[g2][:], gb[g2][:], op=ALU.mult), inc=SV)
                gb_t[g2] = (SV, tact)
                ab_t[g2] = (SV, tact)
                ag_t[pa] = [(SV, tab)]
                ag_t[pg] = [(SC, tgel)]
                act_t[c] = (SV, tact)
            if tc < 15:
                ln_stage(tc + 1)
            if tc == 0:
                w2ps_t = [[(SV, b.cur(SV))], [(SV, b.cur(SV))]]
            for sub2 in range(2):
                i2 = 2 * tc + sub2
                sl4 = i2 % 3
                for sem, t in yf_cons[sl4]:
                    W("sync", sem, t)
                yf_cons[sl4] = []
                yf_t[sl4] = DMA("sync", y2f[sl4][:], yin_h[128 * i2:128 * i2 + 128, :], f"dyc{sl4}")
            for half in range(2):
                for sub in range(2):
                    pw = PK[4 + sub][:, :]
                    for sem, t in w2ps_t[sub]:
                        W("tensor", sem, t)
                    for c in range(22):
                        W("tensor", SV, act_t[c][1])
                        if tc == 0 and half == 0:
                            W("tensor", w2t[c][0], w2t[c][1])
                        tw2 = OP("tensor", lambda e, c=c, sub=sub, half=half, pw=pw: e.matmul(
                            pw, actT[c][:, 128 * sub:128 * sub + 128],
                            w2bf[c][:, 512 * half:512 * half + 512],
                            start=(c == 0), stop=False), inc=ST)
                    if tc == 0 and half == 0 and sub == 0:
                        W("tensor", "db2d", tb2d)
                        W("tensor", SG, b.init_sg)
                    i2 = 2 * tc + sub
                    sl4 = i2 % 3
                    tw2 = OP("tensor", lambda e, sub=sub, half=half, pw=pw: e.matmul(
                        pw, onesr[0:1, 0:128], b2row[0:1, 512 * half:512 * half + 512],
                        start=False, stop=True), inc=ST)
                    y3e, y3t = y3_t[sub]
                    if half == 0 and y3e is not None:
                        W("vector", y3e, y3t)
                    W("vector", f"dyc{sl4}", yf_t[sl4])
                    W("vector", ST, tw2)
                    ty3 = OP("vector", lambda e, sub=sub, half=half, sl4=sl4, pw=pw: e.tensor_tensor(
                        tf[sub][:, 512 * half:512 * half + 512], pw,
                        y2f[sl4][:, 512 * half:512 * half + 512], op=ALU.add), inc=SV)
                    w2ps_t[sub] = [(SV, ty3)]
                    if half == 1:
                        yf_cons[sl4].append((SV, ty3))
                        W("gpsimd", SV, ty3)
                        tsy = DMA("gpsimd", yout_h[128 * i2:128 * i2 + 128, :], tf[sub][:], f"sy3{sub}")
                        y3_t[sub] = (f"sy3{sub}", tsy)
                        tf_cons[sub].append((f"sy3{sub}", tsy))
            tlnT_cons[0] = b.cur(ST)
            for c in range(22):
                act_cons[c] = (ST, tlnT_cons[0])
        arb.reset(m_b)
        arf.reset(m_f)

    def ph_f():
        m_f = arf.mark()
        xf = [arf.alloc(1024) for _ in range(2)]
        yfv = [arf.alloc(1024) for _ in range(2)]
        ofv = [arf.alloc(1024) for _ in range(1)]
        add_t = [0, 0]
        o_cons = [(None, 0)] * 1
        out_tick = None
        for i in range(32):
            sl = i % 2
            if add_t[sl]:
                W("sync", SV, add_t[sl])
            tx = DMA("sync", xf[sl][:], sc_x1[128 * i:128 * i + 128, :], f"dfx{sl}")
            ty = DMA("sync", yfv[sl][:], sc_yA[128 * i:128 * i + 128, :], f"dfy{sl}")
            ose, ost = o_cons[0]
            if ose is not None:
                W("vector", ose, ost)
            W("vector", f"dfx{sl}", tx)
            W("vector", f"dfy{sl}", ty)
            to = OP("vector", lambda e, sl=sl: e.tensor_tensor(
                ofv[0][:], xf[sl][:], yfv[sl][:], op=ALU.add), inc=SV)
            add_t[sl] = to
            W("gpsimd", SV, to)
            out_tick = DMA("gpsimd", out_h[128 * i:128 * i + 128, :], ofv[0][:], "sout")
            o_cons[0] = ("sout", out_tick)
        W("gpsimd", "sout", out_tick)
        arf.reset(m_f)

    import os
    t_isc, t_isg = ph_init()
    b.init_sg = t_isg
    if "r0" in phases:
        ph_r0(t_isc, t_isg)
        phase_barrier()
    if "r1" in phases:
        ph_r1(int(os.environ.get("R1_CHUNKS", "8")))
        phase_barrier()
    if "r2" in phases:
        ph_r2()
        phase_barrier()
    arb.reset(0)
    FUSE_AB = os.environ.get("FUSE_AB") == "1"
    for l in range(2):
        if f"a{l}" in phases or f"b{l}" in phases:
            if FUSE_AB:
                ph_ab(l, sc_yA, sc_yB)
            else:
                ph_a(l, sc_yA)
                phase_barrier()
                ph_b(l, sc_yA, sc_yB)
            phase_barrier()
        if f"c{l}" in phases:
            ph_c(l, sc_yB, sc_yA)
            phase_barrier()
    if "f" in phases:
        phase_barrier()
        ph_f()

    b.emit()
    es.close()
    return nc


def kernel(**inputs):
    hp = _host_prep(inputs)
    x8 = np.ascontiguousarray(np.asarray(inputs["L0_attn_x"], np.float32))
    nc = build_program()
    shared = {k: v for k, v in hp.items() if v is not None}
    in_maps = [dict(shared, x=np.ascontiguousarray(x8[c])) for c in range(NCORE)]
    res = run_bass_kernel_spmd(nc, in_maps, core_ids=list(range(NCORE)))
    out = np.stack([np.asarray(r["out"]) for r in res.results]).astype(np.float32)
    return out


# version 19
# speedup vs baseline: 1.0233x; 1.0233x over previous
import math
import numpy as np
from contextlib import ExitStack

import concourse.bass as bass
import concourse.mybir as mybir
from concourse.bass_utils import run_bass_kernel_spmd

F32 = mybir.dt.float32
BF16 = mybir.dt.bfloat16
AF = mybir.ActivationFunctionType
ALU = mybir.AluOpType
AX = mybir.AxisListType

L = 4096
D = 1024
NCORE = 8
H1 = 16
HD = 64
CH = 512
H2 = 8
WIN = 512
HID = 2730
HIDP = 2816
NHC = HIDP // 128
W1C = 2 * HIDP
ENGS = ("sync", "tensor", "vector", "scalar", "gpsimd")


class Builder:

    def __init__(self, nc):
        self.nc = nc
        self.plan = {e: [] for e in ENGS}
        self.semh = {}
        self.semc = {}

    def sem(self, name):
        if name not in self.semh:
            self.semh[name] = self.es.enter_context(self.nc.semaphore(name))
            self.semc[name] = 0
        return name

    def wait(self, eng, sem, val):
        if val <= 0:
            return
        self.plan[eng].append(("w", sem, int(val)))

    def op(self, eng, fn, inc=None, amt=1):
        t = None
        if inc is not None:
            self.sem(inc)
            self.semc[inc] += amt
            t = self.semc[inc]
        self.plan[eng].append(("o", fn, inc, amt))
        return t

    def dma(self, eng, out, in_, sem):
        self.sem(sem)
        self.semc[sem] += 16
        t = self.semc[sem]
        self.plan[eng].append(("o", lambda e, o=out, i=in_: e.dma_start(out=o, in_=i), sem, 16))
        return t

    def cur(self, sem):
        self.sem(sem)
        return self.semc[sem]

    def check_deadlock(self):
        idx = {e: 0 for e in ENGS}
        vals = dict.fromkeys(self.semh, 0)
        total = sum(len(p) for p in self.plan.values())
        done = 0
        progress = True
        while progress:
            progress = False
            for e in ENGS:
                p = self.plan[e]
                while idx[e] < len(p):
                    it = p[idx[e]]
                    if it[0] == "w":
                        if vals[it[1]] >= it[2]:
                            idx[e] += 1
                            done += 1
                            progress = True
                        else:
                            break
                    else:
                        if it[2] is not None:
                            vals[it[2]] += it[3]
                        idx[e] += 1
                        done += 1
                        progress = True
        if done != total:
            heads = {e: (self.plan[e][idx[e]] if idx[e] < len(self.plan[e]) else None) for e in ENGS}
            raise RuntimeError(f"deadlock: {done}/{total}; heads={ {e: (h[1], h[2]) if h and h[0]=='w' else h and 'op' for e, h in heads.items()} }")

    def emit(self):
        self.check_deadlock()
        nc = self.nc
        block = self.es.enter_context(nc.Block())

        def runner(eng_name):
            def run(engobj):
                for it in self.plan[eng_name]:
                    if it[0] == "w":
                        engobj.wait_ge(self.semh[it[1]], it[2])
                    else:
                        ins = it[1](engobj)
                        if it[2] is not None:
                            ins.then_inc(self.semh[it[2]], it[3])
            return run

        block.sync(runner("sync"))
        block.tensor(runner("tensor"))
        block.vector(runner("vector"))
        block.scalar(runner("scalar"))
        block.gpsimd(runner("gpsimd"))


def _rotary_consts():
    t = np.arange(L, dtype=np.float64)
    half = np.arange(0, HD, 2, dtype=np.float64)
    inv_freq = 1.0 / (10000.0 ** (half / HD))
    freqs = t[:, None] * inv_freq[None, :]
    freqs = np.concatenate([freqs, freqs], axis=1)
    base = (half + 0.4 * HD) / (1.4 * HD)
    power = (t - L // 2) / (WIN // 2)
    scale = base[None, :] ** power[:, None]
    scale = np.concatenate([scale, scale], axis=1)
    cos = np.cos(freqs)
    sin = np.sin(freqs)
    sgn = np.where(np.arange(HD) < 32, -1.0, 1.0)[None, :]
    csq = (cos * scale * 8.0).T
    ssq = (sin * scale * 8.0 * sgn).T
    csk = (cos / scale).T
    ssk = (sin / scale * sgn).T

    def stack2(m):
        return np.concatenate([m, m], axis=0).astype(np.float32)
    return stack2(csq), stack2(ssq), stack2(csk), stack2(ssk)


def _host_prep(inp):
    import ml_dtypes
    d = {}
    d["x"] = None
    d["wqT"] = np.ascontiguousarray(inp["L0_attn_wq"].T, np.float32)
    d["wkT"] = np.ascontiguousarray(inp["L0_attn_wk"].T, np.float32)
    d["wvT"] = np.ascontiguousarray(inp["L0_attn_wv"].T, np.float32)
    d["woT"] = np.ascontiguousarray(inp["L0_attn_wo"].T, np.float32)
    d["gam128"] = np.tile(np.asarray(inp["L0_attn_gammas"], np.float32)[None, :], (128, 1))
    d["wqkv"] = np.ascontiguousarray(inp["L0_ffn_wqkv"], np.float32)
    d["wo2"] = np.ascontiguousarray(inp["L0_ffn_wo"], np.float32)
    w1 = np.asarray(inp["L0_ffn_w1"], np.float32)
    z = np.zeros((2, D, HIDP - HID), np.float32)
    d["w1p"] = np.ascontiguousarray(
        np.concatenate([w1[:, :, :HID], z, w1[:, :, HID:], z], axis=2))
    w2 = np.asarray(inp["L0_ffn_w2"], np.float32)
    w2p = np.concatenate([w2, np.zeros((2, HIDP - HID, 1024), np.float32)], axis=1)
    aug = np.zeros((2, 128, 1024), np.float32)
    aug[:, 0, :] = np.asarray(inp["L0_ffn_b2"], np.float32)
    d["w2aug"] = np.ascontiguousarray(np.concatenate([w2p, aug], axis=1))
    b1 = np.asarray(inp["L0_ffn_b1"], np.float32)
    zb = np.zeros((2, HIDP - HID), np.float32)
    b1p = np.concatenate([b1[:, :HID], zb, b1[:, HID:], zb], axis=1)
    d["b1c"] = np.ascontiguousarray(b1p.reshape(2, 2 * NHC, 128).transpose(0, 2, 1))
    d["lbc"] = np.ascontiguousarray(
        np.asarray(inp["L0_ffn_ln_b"], np.float32).reshape(2, 8, 128).transpose(0, 2, 1)
        .astype(ml_dtypes.bfloat16))
    d["lgc"] = np.ascontiguousarray(
        np.asarray(inp["L0_ffn_ln_g"], np.float32).reshape(2, 8, 128).transpose(0, 2, 1))
    qs = np.asarray(inp["L0_ffn_q_scale"], np.float32)
    ks = np.asarray(inp["L0_ffn_k_scale"], np.float32)
    def til(v):
        return np.tile(v, 2)[:, None]
    qsks = np.stack([
        np.concatenate([til(qs[l]), til(np.roll(qs[l], -32)),
                        til(ks[l]), til(np.roll(ks[l], -32))], axis=1)
        for l in range(2)])
    d["qsks"] = np.ascontiguousarray(qsks)
    import ml_dtypes
    csq, ssq, csk, ssk = _rotary_consts()
    d["rotc"] = np.ascontiguousarray(
        np.stack([csq, ssq, csk, ssk]).astype(ml_dtypes.bfloat16))
    E8 = np.zeros((8, 128, 16), np.float32)
    for g in range(4):
        E8[g, 0:64, 2 * g] = 1.0
        E8[g, 64:128, 2 * g + 1] = 1.0
        E8[4 + g, 0:64, 8 + 2 * g] = 1.0
        E8[4 + g, 64:128, 9 + 2 * g] = 1.0
    d["E8"] = np.ascontiguousarray(E8.transpose(1, 0, 2).reshape(128, 128).astype(ml_dtypes.bfloat16))
    SelP = np.zeros((16, 1024), np.float32)
    for g in range(4):
        SelP[2 * g, 128 * g:128 * g + 64] = 1.0
        SelP[2 * g + 1, 128 * g + 64:128 * g + 128] = 1.0
        SelP[8 + 2 * g, 512 + 128 * g:512 + 128 * g + 64] = 1.0
        SelP[9 + 2 * g, 512 + 128 * g + 64:512 + 128 * g + 128] = 1.0
    d["SelH"] = SelP
    P = np.zeros((128, 128), np.float32)
    for m in range(128):
        blk = (m // 64) * 64
        src = blk + ((m % 64) + 32) % 64
        P[src, m] = 1.0
    d["PROT"] = np.ascontiguousarray(P.astype(ml_dtypes.bfloat16))
    d["zrow"] = np.zeros((1, 512), np.float32)
    return d



def build_program(dbg=(), phases=("r0", "r1", "r2", "a0", "b0", "c0", "a1", "b1", "c1", "f")):
    nc = bass.Bass("TRN2", target_bir_lowering=False)
    es = ExitStack()
    b = Builder(nc)
    b.es = es

    def dram(name, shape, dtype, kind):
        if kind == "Internal" and name in dbg:
            kind = "ExternalOutput"
        return nc.dram_tensor(name, list(shape), dtype, kind=kind).ap()

    x_h = dram("x", (L, D), F32, "ExternalInput")
    out_h = dram("out", (L, D), F32, "ExternalOutput")
    wqT_h = dram("wqT", (D, D), F32, "ExternalInput")
    wkT_h = dram("wkT", (D, D), F32, "ExternalInput")
    wvT_h = dram("wvT", (D, D), F32, "ExternalInput")
    woT_h = dram("woT", (D, D), F32, "ExternalInput")
    gam_h = dram("gam128", (128, 16), F32, "ExternalInput")
    wqkv_h = dram("wqkv", (2, D, 1536), F32, "ExternalInput")
    wo2_h = dram("wo2", (2, 512, D), F32, "ExternalInput")
    w1p_h = dram("w1p", (2, D, W1C), F32, "ExternalInput")
    w2aug_h = dram("w2aug", (2, 2944, D), F32, "ExternalInput")
    b1c_h = dram("b1c", (2, 128, 2 * NHC), F32, "ExternalInput")
    lbc_h = dram("lbc", (2, 128, 8), BF16, "ExternalInput")
    lgc_h = dram("lgc", (2, 128, 8), F32, "ExternalInput")
    qsks_h = dram("qsks", (2, 128, 4), F32, "ExternalInput")
    rotc_h = dram("rotc", (4, 128, L), BF16, "ExternalInput")
    E8_h = dram("E8", (128, 128), BF16, "ExternalInput")
    SelH_h = dram("SelH", (16, 1024), F32, "ExternalInput")
    PROT_h = dram("PROT", (128, 128), BF16, "ExternalInput")
    zrow_h = dram("zrow", (1, 512), F32, "ExternalInput")

    sc_qT = dram("sc_qT", (D, L), BF16, "Internal")
    sc_kT = dram("sc_kT", (D, L), BF16, "Internal")
    sc_ktok = dram("sc_ktok", (L, D), BF16, "Internal")
    sc_v = dram("sc_v", (L, D), BF16, "Internal")
    sc_retT = dram("sc_retT", (D, L), BF16, "Internal")
    sc_x1 = dram("sc_x1", (L, D), F32, "Internal")
    sc_yA = dram("sc_yA", (L, D), F32, "Internal")
    sc_yB = dram("sc_yB", (L, D), F32, "Internal")
    sc_fq = dram("sc_fq", (512, L), BF16, "Internal")
    sc_fk = dram("sc_fk", (512, L), BF16, "Internal")
    sc_fv = dram("sc_fv", (L, 512), BF16, "Internal")

    def sb(name, shape, dtype):
        return es.enter_context(nc.sbuf_tensor(name, list(shape), dtype))

    import os as _os_env
    _fuse_ab = _os_env.environ.get("FUSE_AB") == "1"
    ABF_COLS = 78976 if _fuse_ab else 80000
    AF32_COLS = 8704 if _fuse_ab else 8192
    ABF = sb("abf", [128, ABF_COLS], BF16)
    AF32 = sb("af32", [128, AF32_COLS], F32)

    class Arena:
        def __init__(self, t, cols):
            self.t = t
            self.cols = cols
            self.pos = 0
        def alloc(self, n):
            assert self.pos + n <= self.cols, f"arena overflow {self.pos}+{n}>{self.cols}"
            a = self.t[:, self.pos:self.pos + n]
            self.pos += n
            return a
        def mark(self):
            return self.pos
        def reset(self, m):
            self.pos = m

    arb = Arena(ABF, ABF_COLS)
    arf = Arena(AF32, AF32_COLS)

    IOT = arf.alloc(512)
    Abf = arb.alloc(4096)
    KSbf = arb.alloc(4096)
    Bpbf = arb.alloc(4096)

    gam_sb = sb("gam_sb", [128, 16], F32)
    lngB = sb("lngB", [128, 16], F32)
    G512 = sb("G512", [128, 16], F32)
    E8_sb = sb("E8_sb", [128, 16 * 8], BF16)
    SelH_sb = sb("SelH_sb", [16, 1024], F32)
    PROT_sb = sb("PROT_sb", [128, 128], BF16)
    IDB = sb("IDB", [128, 128], BF16)
    IDF = sb("IDF", [128, 128], F32)
    TRI = sb("TRI", [128, 128], BF16)
    IOC = sb("IOC", [128, 4 * 64], F32)
    MB = sb("MB", [128, 640], BF16)
    MW0 = sb("MW0", [128, 1280], BF16)
    junkA = sb("junkA", [128, 1024], BF16)
    junkB = sb("junkB", [128, 1024], BF16)
    smf = sb("smf", [128, 160], F32)
    Ssb = sb("Ssb", [128, 512], F32)
    Sbf = sb("Sbf", [128, 512], BF16)
    b1e = sb("b1e", [128, 2 * NHC], F32)
    b1c_sb = sb("b1c_sb", [128, 2 * NHC], F32)
    lbc_sb = sb("lbc_sb", [128, 8], BF16)
    lgc_sb = sb("lgc_sb", [128, 8], F32)
    qsks_sb = sb("qsks_sb", [128, 4], F32)
    onesr = sb("onesr", [128, 512], BF16)
    epsc = sb("epsc", [128, 1], F32)

    P0 = es.enter_context(nc.psum_tensor("P0", [128, 1024], F32))
    P1 = es.enter_context(nc.psum_tensor("P1", [128, 1024], F32))
    P2 = es.enter_context(nc.psum_tensor("P2", [128, 1024], F32))
    PB0 = es.enter_context(nc.psum_tensor("PB0", [128, 1024], BF16))
    PB1 = es.enter_context(nc.psum_tensor("PB1", [128, 1024], BF16))

    PK = [P0[:, 0:512], P0[:, 512:1024], P1[:, 0:512], P1[:, 512:1024],
          P2[:, 0:512], P2[:, 512:1024]]
    PSLOT = list(PK)

    SL = "sL"
    SS = "sS"
    ST = "sT"
    SV = "sV"
    SC = "sC"
    SG = "sG"
    SO = "sO"
    for s in (SL, SS, ST, SV, SC, SG, SO):
        b.sem(s)

    W = b.wait
    OP = b.op
    DMA = b.dma

    def ph_init():
        t_g = DMA("sync", gam_sb[:], gam_h, "dgam")
        DMA("sync", E8_sb[:], E8_h, "dE8")
        t_sel = DMA("sync", SelH_sb[:], SelH_h, "dSel")
        t_prot = DMA("sync", PROT_sb[:], PROT_h, "dPROT")
        def gchain(fns):
            t = None
            for f in fns:
                if t is not None:
                    W("gpsimd", SG, t)
                t = OP("gpsimd", f, inc=SG)
            return t
        gchain([lambda e: e.memset(IDB[:], 0.0),
                lambda e: e.affine_select(out=IDB[:], in_=IDB[:], compare_op=ALU.not_equal,
                                          fill=1.0, base=0, pattern=[[-1, 128]], channel_multiplier=1)])
        gchain([lambda e: e.memset(IDF[:], 0.0),
                lambda e: e.affine_select(out=IDF[:], in_=IDF[:], compare_op=ALU.not_equal,
                                          fill=1.0, base=0, pattern=[[-1, 128]], channel_multiplier=1)])
        gchain([lambda e: e.memset(TRI[:], 1.0),
                lambda e: e.affine_select(out=TRI[:], in_=TRI[:], compare_op=ALU.is_ge,
                                          fill=0.0, base=0, pattern=[[1, 128]], channel_multiplier=-1)])
        gchain([lambda e: e.memset(MB[:], 1.0),
                lambda e: e.affine_select(out=MB[:], in_=MB[:], compare_op=ALU.is_ge,
                                          fill=0.0, base=0, pattern=[[1, 640]], channel_multiplier=-1),
                lambda e: e.affine_select(out=MB[:], in_=MB[:], compare_op=ALU.is_ge,
                                          fill=0.0, base=512, pattern=[[-1, 640]], channel_multiplier=1)])
        MW0OFF = [0, 128, 384, 768]
        mwfns = [lambda e: e.memset(MW0[:], 1.0)]
        tmw = gchain(mwfns)
        for r0 in range(4):
            w = 128 * r0 + 128
            o = MW0OFF[r0]
            W("gpsimd", SG, tmw)
            OP("gpsimd", lambda e, r0=r0, w=w, o=o: e.affine_select(
                out=MW0[:, o:o + w], in_=MW0[:, o:o + w],
                compare_op=ALU.is_ge, fill=0.0, base=128 * r0,
                pattern=[[-1, w]], channel_multiplier=1), inc=SG)
        OP("gpsimd", lambda e: e.iota(IOT[:], pattern=[[1, 512]], base=0, channel_multiplier=0,
                                      allow_small_or_imprecise_dtypes=True), inc=SG)
        for j in range(4):
            OP("gpsimd", lambda e, j=j: e.iota(IOC[:, 64 * j:64 * j + 64], pattern=[[0, 64]],
                                               base=512 - 128 * j, channel_multiplier=-1,
                                               allow_small_or_imprecise_dtypes=True), inc=SG)
        t_gps = OP("gpsimd", lambda e: e.memset(onesr[:], 0.0), inc=SG)
        W("gpsimd", SG, t_gps)
        OP("gpsimd", lambda e: e.memset(onesr[0:1, :], 1.0), inc=SG)
        OP("gpsimd", lambda e: e.memset(epsc[:], 1e-5), inc=SG)
        t_gps = OP("gpsimd", lambda e: e.memset(Ssb[:], 0.0), inc=SG)
        t_gps = OP("gpsimd", lambda e: e.memset(Sbf[:], 0.0), inc=SG)

        W("scalar", "dgam", t_g)
        t_ln = OP("scalar", lambda e: e.activation(lngB[:], gam_sb[:], AF.Ln), inc=SC)
        W("scalar", SC, t_ln)
        OP("scalar", lambda e: e.activation(G512[:], lngB[:], AF.Exp, scale=512.0), inc=SC)
        W("scalar", SG, b.cur(SG))
        for h in range(H1):
            hp, hh = h % 2, h // 2
            OP("scalar", lambda e, hp=hp, hh=hh, h=h: e.activation(
                Abf[64 * hp:64 * hp + 64, 512 * hh:512 * hh + 512],
                IOT[64 * hp:64 * hp + 64, :], AF.Exp,
                scale=lngB[64 * hp:64 * hp + 64, h:h + 1]), inc=SC)
        for j in range(4):
            for h in range(H1):
                OP("scalar", lambda e, j=j, h=h: e.activation(
                    KSbf[:, 1024 * j + 64 * h:1024 * j + 64 * h + 64],
                    IOC[:, 64 * j:64 * j + 64], AF.Exp, scale=lngB[:, h:h + 1]), inc=SC)
        for j in range(4):
            OP("gpsimd", lambda e, j=j: e.iota(smf[:, 0:64], pattern=[[0, 64]],
                                               base=-128 * j, channel_multiplier=-1,
                                               allow_small_or_imprecise_dtypes=True), inc=SG)
            tg = b.cur(SG)
            W("scalar", SG, tg)
            for h in range(H1):
                t_bp = OP("scalar", lambda e, j=j, h=h: e.activation(
                    Bpbf[:, 1024 * j + 64 * h:1024 * j + 64 * h + 64],
                    smf[:, 0:64], AF.Exp, scale=lngB[:, h:h + 1]), inc=SC)
            W("gpsimd", SC, t_bp)
        return b.cur(SC), b.cur(SG)

    def emit_ln(xf, ln_out, tx, sx, c0=0, extra_wait=None):
        s1 = smf[:, c0:c0 + 1]
        s2 = smf[:, c0 + 1:c0 + 2]
        mean = smf[:, c0 + 2:c0 + 3]
        q2 = smf[:, c0 + 3:c0 + 4]
        msq = smf[:, c0 + 4:c0 + 5]
        varr = smf[:, c0 + 5:c0 + 6]
        lnv = smf[:, c0 + 6:c0 + 7]
        rstd = smf[:, c0 + 7:c0 + 8]
        negmr = smf[:, c0 + 8:c0 + 9]
        W("scalar", sx, tx)
        if extra_wait is not None:
            W("scalar", extra_wait[0], extra_wait[1])
        tsq = OP("scalar", lambda e: e.activation(junkA[:], xf, AF.Square, accum_out=s2), inc=SC)
        W("vector", sx, tx)
        tsum = OP("vector", lambda e: e.tensor_scalar(junkB[:], xf, 1.0, None, op0=ALU.mult,
                                                      op1=ALU.add, accum_out=s1), inc=SV)
        W("vector", SC, tsq)
        tq2 = OP("vector", lambda e: e.tensor_scalar(q2, s2, 1.0 / D, None, op0=ALU.mult), inc=SV)
        W("vector", SV, tsum)
        tm = OP("vector", lambda e: e.tensor_scalar(mean, s1, 1.0 / D, None, op0=ALU.mult), inc=SV)
        W("vector", SV, tm)
        tms = OP("vector", lambda e: e.tensor_tensor(msq, mean, mean, op=ALU.mult), inc=SV)
        W("vector", SV, tms)
        tv = OP("vector", lambda e: e.scalar_tensor_tensor(varr, msq, -1.0, q2,
                                                           op0=ALU.mult, op1=ALU.add), inc=SV)
        W("scalar", SV, tv)
        tl = OP("scalar", lambda e: e.activation(lnv, varr, AF.Ln, bias=epsc[:, 0:1]), inc=SC)
        W("scalar", SC, tl)
        tr = OP("scalar", lambda e: e.activation(rstd, lnv, AF.Exp, scale=-0.5), inc=SC)
        W("vector", SC, tr)
        tn = OP("vector", lambda e: e.scalar_tensor_tensor(negmr, mean, -1.0, rstd,
                                                           op0=ALU.mult, op1=ALU.mult), inc=SV)
        W("vector", SV, tn)
        tap = OP("vector", lambda e: e.tensor_scalar(ln_out, xf, rstd, negmr,
                                                     op0=ALU.mult, op1=ALU.add), inc=SV)
        return tap

    wstage = [arf.alloc(1024), arf.alloc(1024)]
    wst_tick = [(None, 0), (None, 0)]
    wst_i = [0]

    def load_conv(src_ap, dst_bf, cols, conv_eng="scalar"):
        tc = None
        for c0 in range(0, cols, 1024):
            cw = min(1024, cols - c0)
            sl = wst_i[0] % 2
            wst_i[0] += 1
            q = "gpsimd"
            stg = wstage[sl][:, 0:cw]
            ps, pt = wst_tick[sl]
            if ps is not None:
                W(q, ps, pt)
            td = DMA(q, stg, src_ap[:, c0:c0 + cw], f"dW{sl}")
            W(conv_eng, f"dW{sl}", td)
            sem = {"scalar": SC, "vector": SV, "gpsimd": SG}[conv_eng]
            dpart = dst_bf[:, c0:c0 + cw]
            if conv_eng == "scalar":
                tc = OP("scalar", lambda e, dpart=dpart, stg=stg: e.copy(dpart, stg), inc=sem)
            else:
                tc = OP("vector", lambda e, dpart=dpart, stg=stg: e.tensor_copy(dpart, stg), inc=sem)
            wst_tick[sl] = (sem, tc)
        return tc

    def ph_r0(t_init_sc, t_init_sg):
        m_b = arb.mark()
        m_f = arf.mark()
        W("vector", SC, t_init_sc)
        W("tensor", SG, t_init_sg)
        W("tensor", SC, t_init_sc)
        wq_bf = [arb.alloc(1024) for _ in range(8)]
        wk_bf = [arb.alloc(1024) for _ in range(8)]
        wv_bf = [arb.alloc(1024) for _ in range(8)]
        wt = {}
        for d in range(8):
            wt[("q", d)] = load_conv(wqT_h[128 * d:128 * d + 128, :], wq_bf[d], 1024,
                                     "scalar" if d % 2 == 0 else "vector")
            wt[("k", d)] = load_conv(wkT_h[128 * d:128 * d + 128, :], wk_bf[d], 1024,
                                     "scalar" if d % 2 == 0 else "vector")
            wt[("v", d)] = load_conv(wvT_h[128 * d:128 * d + 128, :], wv_bf[d], 1024,
                                     "scalar" if d % 2 == 0 else "vector")
        xf = [arf.alloc(1024) for _ in range(2)]
        lnbf = [arb.alloc(1024) for _ in range(2)]
        lnT = [arb.alloc(4096) for _ in range(2)]
        ebf = [arb.alloc(512) for _ in range(6)]
        x_t = [0, 0]
        lnbf_t = [0, 0]
        lnbf_cons = [0, 0]
        lnT_t = [0, 0]
        lnT_cons = [0, 0]
        eb_t = [(None, 0)] * 6
        eb_i = [0]

        def store_evac(psrc, dst_hbm, scale=None, n=512, eng="scalar"):
            sl = eb_i[0] % 6
            eb_i[0] += 1
            buf = ebf[sl][:, 0:n]
            ps, pt = eb_t[sl]
            sem = SC if eng == "scalar" else SV
            if ps is not None:
                W(eng, ps, pt)
            if eng == "scalar":
                if scale is None:
                    tc = OP("scalar", lambda e: e.copy(buf, psrc), inc=SC)
                else:
                    tc = OP("scalar", lambda e: e.activation(buf, psrc, AF.Copy, scale=scale), inc=SC)
            else:
                tc = OP("vector", lambda e: e.tensor_copy(buf, psrc), inc=SV)
            W("gpsimd", sem, tc)
            ts = DMA("gpsimd", dst_hbm, buf, f"seb{sl}")
            eb_t[sl] = (f"seb{sl}", ts)
            return tc, ts

        ps_i = [0]
        ps_t = [(None, 0)] * 6

        def next_ps():
            sl = ps_i[0] % 6
            ps_i[0] += 1
            return sl

        pb_i = [0]
        pb_t = [(None, 0)] * 2

        for tt in range(8):
            for s in range(4):
                i = 4 * tt + s
                sl = i % 2
                if lnbf_t[sl]:
                    W("sync", SV, lnbf_t[sl])
                x_t[sl] = DMA("sync", xf[sl][:], x_h[128 * i:128 * i + 128, :], f"dx{sl}")
                if lnbf_cons[sl]:
                    W("vector", ST, lnbf_cons[sl])
                tap = emit_ln(xf[sl][:], lnbf[sl][:], x_t[sl], f"dx{sl}", c0=64 + 16 * sl)
                lnbf_t[sl] = tap
                pbsl = pb_i[0] % 2
                pb_i[0] += 1
                PB = PB0 if pbsl == 0 else PB1
                ps, pt = pb_t[pbsl]
                W("tensor", SV, tap)
                if ps is not None:
                    W("tensor", ps, pt)
                for k in range(8):
                    tmm = OP("tensor", lambda e, k=k, PB=PB, sl=sl: e.transpose(
                        PB[:, 128 * k:128 * k + 128], lnbf[sl][:, 128 * k:128 * k + 128], IDB[:]),
                        inc=ST)
                lnbf_cons[sl] = tmm
                tsl = tt % 2
                dst = lnT[tsl].rearrange("p (k c) -> p k c", k=8)
                W("scalar", ST, tmm)
                if s == 0 and lnT_cons[tsl]:
                    W("scalar", ST, lnT_cons[tsl])
                te1 = OP("scalar", lambda e, PB=PB, dst=dst, s=s: e.copy(
                    dst[:, 0:4, 128 * s:128 * s + 128], PB[:, 0:512].rearrange("p (k c) -> p k c", k=4)), inc=SC)
                te2 = OP("scalar", lambda e, PB=PB, dst=dst, s=s: e.copy(
                    dst[:, 4:8, 128 * s:128 * s + 128], PB[:, 512:1024].rearrange("p (k c) -> p k c", k=4)), inc=SC)
                pb_t[pbsl] = (SC, te2)
                lnT_t[tsl] = te2
            tsl = tt % 2
            tLN = lnT_t[tsl]
            for which, wbf, dsth, scale in (("q", wq_bf, sc_qT, 0.125), ("k", wk_bf, sc_kT, None)):
                for g in range(8):
                    sl6 = next_ps()
                    pp = PSLOT[sl6]
                    pse, pst = ps_t[sl6]
                    W("tensor", SC, tLN)
                    if pse is not None:
                        W("tensor", pse, pst)
                    for d in range(8):
                        W("tensor", SC if (d % 2 == 0) else SV, wt[(which, d)])
                        tmm = OP("tensor", lambda e, d=d, g=g, wbf=wbf, pp=pp, tsl=tsl: e.matmul(
                            pp, wbf[d][:, 128 * g:128 * g + 128],
                            lnT[tsl][:, 512 * d:512 * d + 512],
                            start=(d == 0), stop=(d == 7)), inc=ST)
                    W("scalar", ST, tmm)
                    tc, _ = store_evac(pp, dsth[128 * g:128 * g + 128, 512 * tt:512 * tt + 512],
                                       scale=scale)
                    ps_t[sl6] = (SC, tc)
            for which, wbf, dsth in (("k", wk_bf, sc_ktok), ("v", wv_bf, sc_v)):
                for s in range(4):
                    for half in range(2):
                        sl6 = next_ps()
                        pp = PSLOT[sl6]
                        pse, pst = ps_t[sl6]
                        W("tensor", SC, tLN)
                        if pse is not None:
                            W("tensor", pse, pst)
                        for d in range(8):
                            W("tensor", SC if (d % 2 == 0) else SV, wt[(which, d)])
                            tmm = OP("tensor", lambda e, d=d, s=s, half=half, wbf=wbf, pp=pp, tsl=tsl: e.matmul(
                                pp, lnT[tsl][:, 512 * d + 128 * s:512 * d + 128 * s + 128],
                                wbf[d][:, 512 * half:512 * half + 512],
                                start=(d == 0), stop=(d == 7)), inc=ST)
                        W("scalar", ST, tmm)
                        i = 4 * tt + s
                        tc, _ = store_evac(pp, dsth[128 * i:128 * i + 128, 512 * half:512 * half + 512])
                        ps_t[sl6] = (SC, tc)
            lnT_cons[tsl] = b.cur(ST)
        arb.reset(m_b)
        arf.reset(m_f)

    def phase_barrier():
        curs = [(s, b.semc[s]) for s in list(b.semh) if b.semc[s] > 0]
        for e in ENGS:
            for s, v in curs:
                W(e, s, v)

    def ph_r1(nchunks=8):
        import os
        NOSTATE = os.environ.get("R1_NOSTATE") == "1"
        NODIAG = os.environ.get("R1_NODIAG") == "1"
        NORET = os.environ.get("R1_NORET") == "1"
        NOTRI = os.environ.get("R1_NOTRI") == "1"
        m_b = arb.mark()
        m_f = arf.mark()
        qTb = [[arb.alloc(512) for _ in range(8)] for _ in range(2)]
        kTb = [[arb.alloc(512) for _ in range(8)] for _ in range(2)]
        ktb = [[arb.alloc(1024) for _ in range(4)] for _ in range(2)]
        vb = [[arb.alloc(1024) for _ in range(4)] for _ in range(2)]
        qA = [[arb.alloc(512) for _ in range(8)] for _ in range(2)]
        kpp = [[arb.alloc(1024) for _ in range(4)] for _ in range(2)]
        vpp = [[arb.alloc(1024) for _ in range(4)] for _ in range(2)]
        wT = [arb.alloc(512) for _ in range(3)]
        retb = [arb.alloc(512) for _ in range(3)]
        ld_t = [dict(), dict()]
        cons_t = [dict(), dict()]
        wT_t = [(None, 0)] * 3
        retb_t = [(None, 0)] * 3
        simslots = [PK[0][:, :], PK[1][:, :], PK[2][:, :]]
        sim_t = [[], [], []]
        pretslots = [PK[4][:, :], PK[5][:, :]]
        pret_t = [(None, 0)] * 2
        pP = PK[3][:, 0:64]
        pP_t = [None]
        si = [0]
        pi = [0]
        wti = [0]
        rbi = [0]
        s_up_t = {}
        chunk_st = {}

        for c in range(nchunks):
            sl = c % 2
            if c >= 2:
                W("vector", ST, chunk_st[c - 2])
            for g in range(8):
                if c >= 2:
                    for sem, t in cons_t[sl].get(("qT", g), []):
                        W("sync", sem, t)
                ld_t[sl][("qT", g)] = DMA("sync", qTb[sl][g][:],
                                          sc_qT[128 * g:128 * g + 128, 512 * c:512 * c + 512], f"dq{sl}")
                if c >= 2:
                    for sem, t in cons_t[sl].get(("kT", g), []):
                        W("sync", sem, t)
                ld_t[sl][("kT", g)] = DMA("sync", kTb[sl][g][:],
                                          sc_kT[128 * g:128 * g + 128, 512 * c:512 * c + 512], f"dk{sl}")
            for j in range(4):
                if c >= 2:
                    for sem, t in cons_t[sl].get(("kt", j), []):
                        W("sync", sem, t)
                ld_t[sl][("kt", j)] = DMA("sync", ktb[sl][j][:],
                                          sc_ktok[512 * c + 128 * j:512 * c + 128 * j + 128, :], f"dt{sl}")
                if c >= 2:
                    for sem, t in cons_t[sl].get(("v", j), []):
                        W("sync", sem, t)
                ld_t[sl][("v", j)] = DMA("sync", vb[sl][j][:],
                                         sc_v[512 * c + 128 * j:512 * c + 128 * j + 128, :], f"dv{sl}")
            grp = {"qT": ld_t[sl][("qT", 7)], "kT": ld_t[sl][("kT", 7)],
                   "kt": ld_t[sl][("kt", 3)], "v": ld_t[sl][("v", 3)]}
            cons_t[sl] = dict()
            prep_t = {}
            for g in range(8):
                W("vector", f"dq{sl}", grp["qT"])
                if c >= 2:
                    pass
                tq = OP("vector", lambda e, sl=sl, g=g: e.tensor_tensor(
                    qA[sl][g][:], qTb[sl][g][:], Abf[:, 512 * g:512 * g + 512], op=ALU.mult), inc=SV)
                prep_t[("qA", g)] = tq
                cons_t[sl].setdefault(("qT", g), []).append((SV, tq))
            for j in range(4):
                W("vector", f"dt{sl}", grp["kt"])
                tk = OP("vector", lambda e, sl=sl, j=j: e.tensor_tensor(
                    kpp[sl][j][:], ktb[sl][j][:], KSbf[:, 1024 * j:1024 * j + 1024], op=ALU.mult), inc=SV)
                prep_t[("kpp", j)] = tk
                cons_t[sl].setdefault(("kt", j), []).append((SV, tk))
                W("vector", f"dv{sl}", grp["v"])
                tv = OP("vector", lambda e, sl=sl, j=j: e.tensor_tensor(
                    vpp[sl][j][:], vb[sl][j][:], Bpbf[:, 1024 * j:1024 * j + 1024], op=ALU.mult), inc=SV)
                prep_t[("vpp", j)] = tv
                cons_t[sl].setdefault(("v", j), []).append((SV, tv))
            for g in range(8):
                prsl = pi[0] % 2
                pi[0] += 1
                pret = pretslots[prsl]
                pse, pst = pret_t[prsl]
                if pse is not None:
                    W("tensor", pse, pst)
                for hp in range(2):
                    h = 2 * g + hp
                    W("tensor", SV, prep_t[("qA", g)])
                    if c > 0 and not NOSTATE:
                        W("tensor", SV, s_up_t[h])
                        tsq_mm = OP("tensor", lambda e, hp=hp, g=g, sl=sl, pret=pret: e.matmul(
                            pret[64 * hp:64 * hp + 64, :],
                            Sbf[64 * hp:64 * hp + 64, 64 * g:64 * g + 64],
                            qA[sl][g][64 * hp:64 * hp + 64, :],
                            start=True, stop=False, skip_group_check=True), inc=ST)
                    rpend = None
                    for j in ([] if NODIAG else range(4)):
                        nq = 512 - 128 * j
                        ssl = si[0] % 3
                        si[0] += 1
                        psim = simslots[ssl][:, 0:nq]
                        for sse, sst in sim_t[ssl]:
                            W("tensor", sse, sst)
                        W("tensor", f"dk{sl}", grp["kT"])
                        tsim = OP("tensor", lambda e, sl=sl, g=g, hp=hp, j=j, nq=nq, psim=psim: e.matmul(
                            psim, kTb[sl][g][64 * hp:64 * hp + 64, 128 * j:128 * j + 128],
                            qA[sl][g][64 * hp:64 * hp + 64, 128 * j:512],
                            start=True, stop=True), inc=ST)
                        wsl = wti[0] % 3
                        wti[0] += 1
                        wse, wst = wT_t[wsl]
                        if wse is not None:
                            W("vector", wse, wst)
                        W("vector", ST, tsim)
                        t_tri = OP("vector", lambda e, wsl=wsl, psim=psim: e.tensor_tensor(
                            wT[wsl][:, 0:128], psim[:, 0:128], TRI[:], op=ALU.mult), inc=SV)
                        t_full = None
                        if nq > 128:
                            if wse is not None:
                                W("scalar", wse, wst)
                            W("scalar", ST, tsim)
                            W("scalar", SV, t_tri)
                            t_full = OP("scalar", lambda e, wsl=wsl, psim=psim, nq=nq: e.copy(
                                wT[wsl][:, 128:nq], psim[:, 128:nq]), inc=SC)
                        sim_t[ssl] = [(SV, t_tri)] + ([(SC, t_full)] if t_full is not None else [])

                        def emit_ret(rp):
                            W("tensor", SV, rp["t_tri"])
                            if rp["t_full"] is not None:
                                W("tensor", SC, rp["t_full"])
                            W("tensor", SV, prep_t[("vpp", rp["j"])])
                            tret = OP("tensor", lambda e, sl=sl, j=rp["j"], h=h, hp=hp,
                                      nq=rp["nq"], wsl=rp["wsl"], pret=pret, start=rp["start"]: e.matmul(
                                pret[64 * hp:64 * hp + 64, 128 * j:512],
                                vpp[sl][j][:, 64 * h:64 * h + 64], wT[wsl][:, 0:nq],
                                start=start, stop=(j == 3), skip_group_check=True), inc=ST)
                            wT_t[rp["wsl"]] = (ST, tret)
                            return tret
                        rp = dict(t_tri=t_tri, t_full=t_full, j=j, nq=nq, wsl=wsl,
                                  start=(c == 0 and j == 0))
                        if rpend is not None:
                            emit_ret(rpend)
                        rpend = rp
                    if rpend is not None:
                        tret = emit_ret(rpend)
                    if c < 7 and not NOSTATE:
                        if pP_t[0] is not None:
                            W("tensor", SV, pP_t[0])
                        for j in range(4):
                            W("tensor", SV, prep_t[("kpp", j)])
                            tp = OP("tensor", lambda e, sl=sl, j=j, h=h: e.matmul(
                                pP[0:64, :], kpp[sl][j][:, 64 * h:64 * h + 64],
                                vb[sl][j][:, 64 * h:64 * h + 64],
                                start=(j == 0), stop=(j == 3)), inc=ST)
                        W("vector", ST, tp)
                        if c > 0:
                            W("vector", ST, tsq_mm)
                        tsu = OP("vector", lambda e, hp=hp, g=g, h=h: e.scalar_tensor_tensor(
                            Ssb[64 * hp:64 * hp + 64, 64 * g:64 * g + 64],
                            Ssb[64 * hp:64 * hp + 64, 64 * g:64 * g + 64],
                            G512[64 * hp:64 * hp + 64, h:h + 1],
                            pP[0:64, :], op0=ALU.mult, op1=ALU.add), inc=SV)
                        W("vector", SV, tsu)
                        tsb = OP("vector", lambda e, hp=hp, g=g: e.tensor_copy(
                            Sbf[64 * hp:64 * hp + 64, 64 * g:64 * g + 64],
                            Ssb[64 * hp:64 * hp + 64, 64 * g:64 * g + 64]), inc=SV)
                        s_up_t[h] = tsb
                        pP_t[0] = tsu
                rsl = rbi[0] % 3
                rbi[0] += 1
                rse, rst = retb_t[rsl]
                if rse is not None:
                    W("scalar", rse, rst)
                W("scalar", ST, b.cur(ST))
                tev = OP("scalar", lambda e, rsl=rsl, pret=pret: e.copy(retb[rsl][:], pret[:]), inc=SC)
                pret_t[prsl] = (SC, tev)
                W("gpsimd", SC, tev)
                tst = DMA("gpsimd", sc_retT[128 * g:128 * g + 128, 512 * c:512 * c + 512],
                          retb[rsl][:], f"srb{rsl}")
                retb_t[rsl] = (f"srb{rsl}", tst)
            chunk_st[c] = b.cur(ST)
            for g in range(8):
                cons_t[sl][("kT", g)] = [(ST, chunk_st[c])]
            for j in range(4):
                cons_t[sl].setdefault(("v", j), []).append((ST, chunk_st[c]))
        arb.reset(m_b)
        arf.reset(m_f)

    def ph_r2():
        m_b = arb.mark()
        m_f = arf.mark()
        wo_bf = [arb.alloc(1024) for _ in range(8)]
        wt = {}
        for d in range(8):
            wt[d] = load_conv(woT_h[128 * d:128 * d + 128, :], wo_bf[d], 1024,
                              "scalar" if d % 2 == 0 else "vector")
        rT = [[arb.alloc(512) for _ in range(8)] for _ in range(2)]
        rT_t = [dict(), dict()]
        rT_cons = [0, 0]
        xf = [arf.alloc(1024) for _ in range(2)]
        x1f = [arf.alloc(1024) for _ in range(2)]
        y0f = [arf.alloc(1024) for _ in range(1)]
        x_cons = [0, 0]
        x1_cons = [(0, 0), (0, 0)]
        y0_cons = [0]
        psl_t = [(None, 0)] * 2
        for tt in range(8):
            sl = tt % 2
            if tt >= 2 and rT_cons[sl]:
                W("sync", ST, rT_cons[sl])
            for g in range(8):
                rT_t[sl][g] = DMA("sync", rT[sl][g][:],
                                  sc_retT[128 * g:128 * g + 128, 512 * tt:512 * tt + 512], f"drt{sl}")
            for s in range(4):
                i = 4 * tt + s
                isl = i % 2
                if x_cons[isl]:
                    W("sync", SV, x_cons[isl])
                tx = DMA("sync", xf[isl][:], x_h[128 * i:128 * i + 128, :], f"dx{isl}")
                t_half = []
                for half in range(2):
                    psl = (2 * i + half) % 2
                    pp = PK[psl][:, :]
                    pse, pst = psl_t[psl]
                    if pse is not None:
                        W("tensor", pse, pst)
                    for g in range(8):
                        W("tensor", f"drt{sl}", rT_t[sl][7])
                        W("tensor", SC if (g % 2 == 0) else SV, wt[g])
                        tmm = OP("tensor", lambda e, g=g, s=s, half=half, sl=sl, pp=pp: e.matmul(
                            pp, rT[sl][g][:, 128 * s:128 * s + 128],
                            wo_bf[g][:, 512 * half:512 * half + 512],
                            start=(g == 0), stop=(g == 7)), inc=ST)
                    W("vector", f"dx{isl}", tx)
                    W("vector", ST, tmm)
                    if half == 0 and x1_cons[isl][0]:
                        W("vector", f"sx1{isl}", x1_cons[isl][0])
                        W("vector", SV, x1_cons[isl][1])
                    th = OP("vector", lambda e, isl=isl, half=half, pp=pp: e.tensor_tensor(
                        x1f[isl][:, 512 * half:512 * half + 512], pp,
                        xf[isl][:, 512 * half:512 * half + 512], op=ALU.add), inc=SV)
                    t_half.append(th)
                    psl_t[psl] = (SV, th)
                rT_cons[sl] = b.cur(ST)
                x_cons[isl] = t_half[-1]
                W("gpsimd", SV, t_half[-1])
                tsx1 = DMA("gpsimd", sc_x1[128 * i:128 * i + 128, :], x1f[isl][:], f"sx1{isl}")
                if y0_cons[0]:
                    W("vector", "sy0", y0_cons[0])
                tap = emit_ln(x1f[isl][:], y0f[0][:], t_half[-1], SV, c0=64 + 16 * isl)
                x1_cons[isl] = (tsx1, tap)
                W("gpsimd", SV, tap)
                tsy = DMA("gpsimd", sc_yA[128 * i:128 * i + 128, :], y0f[0][:], f"sy0")
                y0_cons[0] = tsy
        arb.reset(m_b)
        arf.reset(m_f)

    def ph_a(l, yin_h):
        m_b = arb.mark()
        m_f = arf.mark()
        rot = [arb.alloc(4096) for _ in range(4)]
        t_qs = DMA("sync", qsks_sb[:], qsks_h[l], "dqsks")
        rot_t = []
        for r in range(4):
            tl_ = DMA("sync", rot[r], rotc_h[r], f"drot{r}")
            W("vector", f"drot{r}", tl_)
            W("vector", "dqsks", t_qs)
            rot_t.append(OP("vector", lambda e, r=r, rr=rot[r]: e.tensor_scalar(
                rr, rr, qsks_sb[:, r:r + 1], None, op0=ALU.mult), inc=SV))
        wbf = [arb.alloc(1536) for _ in range(8)]
        wt = {}
        for d in range(8):
            eng = "scalar" if d % 2 == 0 else "vector"
            load_conv(wqkv_h[l, 128 * d:128 * d + 128, 0:1024], wbf[d][:, 0:1024], 1024, eng)
            t2 = load_conv(wqkv_h[l, 128 * d:128 * d + 128, 1024:1536], wbf[d][:, 1024:1536], 512, eng)
            wt[d] = ({"scalar": SC, "vector": SV}[eng], t2)
        yf = [arf.alloc(512) for _ in range(2)]
        ysh = [arf.alloc(512) for _ in range(2)]
        hbf = [arb.alloc(1024) for _ in range(2)]
        hT = [arb.alloc(4096) for _ in range(2)]
        sqb = [arb.alloc(512) for _ in range(3)]
        raw = [arb.alloc(512) for _ in range(8)]
        krnb = [arb.alloc(512) for _ in range(4)]
        qrnb = [arb.alloc(512) for _ in range(4)]
        m1b = [arb.alloc(512) for _ in range(2)]
        m2b = [arb.alloc(512) for _ in range(2)]
        fout = [arb.alloc(512) for _ in range(3)]
        vout = [arb.alloc(512) for _ in range(2)]
        ssf = arf.alloc(512)
        rln = arf.alloc(512)
        r_all = arf.alloc(512)
        yf_t = [0, 0]
        yf_cons = [[], []]
        hbf_cons = [0, 0]
        hT_cons = [0, 0]
        hT_t = [0, 0]
        sq_t = [(None, 0)] * 3
        raw_cons = [[] for _ in range(8)]
        krn_t = [[], [], [], []]
        qrn_t = [[], [], [], []]
        m1_t = [(None, 0)] * 2
        m2_t = [(None, 0)] * 2
        fout_t = [(None, 0)] * 3
        vout_t = [(None, 0)] * 2
        pool = [PK[0][:, :], PK[1][:, :], PK[2][:, :], PK[5][:, :]]
        pool_t = [[] for _ in range(4)]
        ssps = PK[3][0:16, :]
        rtps = PK[4][:, :]
        pb_t = [(None, 0)] * 2
        ssps_cons = [0]
        pli = [0]
        sqi = [0]
        fi = [0]
        kri = [0]
        m12i = [0]
        voi = [0]
        pbi = [0]

        def pool_next():
            s = pli[0] % 4
            pli[0] += 1
            return s

        for tt in range(8):
            tsl = tt % 2
            for s in range(4):
                i = 4 * tt + s
                sl = i % 2
                psl = (i + 1) % 2
                for sem, t in yf_cons[sl]:
                    W("sync", sem, t)
                yf_cons[sl] = []
                yf_t[sl] = DMA("sync", yf[sl][:], yin_h[128 * i:128 * i + 128, 0:512], f"dyf{sl}")
                if i == 0:
                    DMA("sync", ysh[sl][1:128, :], yin_h[0:127, 512:1024], f"dys{sl}")
                    tsh = DMA("sync", ysh[sl][0:1, :], zrow_h[0:1, 0:512], f"dys{sl}")
                else:
                    tsh = DMA("sync", ysh[sl][:], yin_h[128 * i - 1:128 * i + 127, 512:1024], f"dys{sl}")
                if hbf_cons[sl]:
                    W("vector", ST, hbf_cons[sl])
                W("vector", f"dyf{sl}", yf_t[sl])
                tc1 = OP("vector", lambda e, sl=sl: e.tensor_copy(
                    hbf[sl][:, 0:512], yf[sl][:]), inc=SV)
                W("vector", f"dys{sl}", tsh)
                tc3 = OP("vector", lambda e, sl=sl: e.tensor_copy(
                    hbf[sl][:, 512:1024], ysh[sl][:]), inc=SV)
                tc2 = tc3
                yf_cons[sl].append((SV, tc3))
                pbsl = pbi[0] % 2
                pbi[0] += 1
                PB = PB0 if pbsl == 0 else PB1
                ps, pt = pb_t[pbsl]
                if ps is not None:
                    W("tensor", ps, pt)
                W("tensor", SV, tc3)
                W("tensor", SV, max(tc1, tc2))
                for k in range(8):
                    tmm = OP("tensor", lambda e, k=k, PB=PB, sl=sl: e.transpose(
                        PB[:, 128 * k:128 * k + 128], hbf[sl][:, 128 * k:128 * k + 128], IDB[:]),
                        inc=ST)
                hbf_cons[sl] = tmm
                dst = hT[tsl].rearrange("p (k c) -> p k c", k=8)
                W("scalar", ST, tmm)
                if s == 0 and hT_cons[tsl]:
                    W("scalar", ST, hT_cons[tsl])
                te1 = OP("scalar", lambda e, PB=PB, dst=dst, s=s: e.copy(
                    dst[:, 0:4, 128 * s:128 * s + 128], PB[:, 0:512].rearrange("p (k c) -> p k c", k=4)), inc=SC)
                te2 = OP("scalar", lambda e, PB=PB, dst=dst, s=s: e.copy(
                    dst[:, 4:8, 128 * s:128 * s + 128], PB[:, 512:1024].rearrange("p (k c) -> p k c", k=4)), inc=SC)
                pb_t[pbsl] = (SC, te2)
                hT_t[tsl] = te2
            raw_t = {}
            for gg in range(8):
                psl4 = pool_next()
                pp = pool[psl4]
                for sem, t in pool_t[psl4]:
                    W("tensor", sem, t)
                W("tensor", SC, hT_t[tsl])
                for d in range(8):
                    W("tensor", wt[d][0], wt[d][1])
                    tmm = OP("tensor", lambda e, d=d, gg=gg, pp=pp, tsl=tsl: e.matmul(
                        pp, wbf[d][:, 128 * gg:128 * gg + 128],
                        hT[tsl][:, 512 * d:512 * d + 512],
                        start=(d == 0), stop=(d == 7)), inc=ST)
                sqsl = sqi[0] % 3
                sqi[0] += 1
                sse, sst = sq_t[sqsl]
                W("scalar", ST, tmm)
                for sem, t in raw_cons[gg]:
                    W("scalar", sem, t)
                raw_cons[gg] = []
                trw = OP("scalar", lambda e, gg=gg, pp=pp: e.copy(raw[gg][:], pp), inc=SC)
                raw_t[gg] = trw
                if sse is not None:
                    W("vector", sse, sst)
                W("vector", SC, trw)
                tsq = OP("vector", lambda e, sqsl=sqsl, gg=gg: e.tensor_tensor(
                    sqb[sqsl][:], raw[gg][:], raw[gg][:], op=ALU.mult), inc=SV)
                raw_cons[gg].append((SV, tsq))
                W("tensor", SV, tsq)
                if gg == 0:
                    W("tensor", "dE8", b.cur("dE8"))
                    if ssps_cons[0]:
                        W("tensor", SV, ssps_cons[0])
                tss = OP("tensor", lambda e, gg=gg, sqsl=sqsl: e.matmul(
                    ssps, E8_sb[:, 16 * gg:16 * gg + 16], sqb[sqsl][:],
                    start=(gg == 0), stop=(gg == 7)), inc=ST)
                sq_t[sqsl] = (ST, tss)
                pool_t[psl4] = [(SC, trw)]
            W("vector", ST, tss)
            t1 = OP("vector", lambda e: e.tensor_scalar(ssf[0:16, :], ssps, 6e-20, None,
                                                        op0=ALU.max), inc=SV)
            ssps_cons[0] = t1
            W("scalar", SV, t1)
            t2_ = OP("scalar", lambda e: e.activation(rln[0:16, :], ssf[0:16, :], AF.Ln), inc=SC)
            W("scalar", SC, t2_)
            t3_ = OP("scalar", lambda e: e.activation(ssf[0:16, :], rln[0:16, :], AF.Exp, scale=-0.5), inc=SC)
            W("vector", SC, t3_)
            t4_ = OP("vector", lambda e: e.tensor_scalar(r_all[0:16, :], ssf[0:16, :], 1e12, None,
                                                         op0=ALU.min), inc=SV)
            rotq = {}
            for gg in range(4):
                psl4 = pool_next()
                Rps = pool[psl4]
                for sem, t in pool_t[psl4]:
                    W("tensor", sem, t)
                W("tensor", SV, t4_)
                if tt == 0 and gg == 0:
                    W("tensor", "dSel", b.cur("dSel"))
                tR = OP("tensor", lambda e, gg=gg, Rps=Rps: e.matmul(
                    Rps, SelH_sb[0:16, 128 * gg:128 * gg + 128], r_all[0:16, :],
                    start=True, stop=True), inc=ST)
                rotq[gg] = [Rps, tR, psl4, None, None]
            for gg in range(4):
                Rps, tR, psl4, _, _ = rotq[gg]
                for kse, kst in qrn_t[gg]:
                    W("vector", kse, kst)
                W("vector", SC, raw_t[gg])
                W("vector", ST, tR)
                tqrn = OP("vector", lambda e, gg=gg, Rps=Rps: e.tensor_tensor(
                    qrnb[gg][:], raw[gg][:], Rps, op=ALU.mult), inc=SV)
                raw_cons[gg] = [(SV, tqrn)]
                pool_t[psl4] = [(SV, tqrn)]
                rotq[gg][3] = tqrn
            for gg in range(4):
                tqrn = rotq[gg][3]
                psl4b = pool_next()
                rotps = pool[psl4b]
                for sem, t in pool_t[psl4b]:
                    W("tensor", sem, t)
                W("tensor", SV, tqrn)
                if tt == 0 and gg == 0:
                    W("tensor", "dPROT", b.cur("dPROT"))
                trot = OP("tensor", lambda e, gg=gg, rotps=rotps: e.matmul(
                    rotps, PROT_sb[:], qrnb[gg][:], start=True, stop=True), inc=ST)
                rotq[gg][4] = (rotps, trot, psl4b)
            for gg in range(4):
                tqrn = rotq[gg][3]
                rotps, trot, psl4b = rotq[gg][4]
                m1sl = m12i[0] % 2
                m12i[0] += 1
                mse, mst = m1_t[m1sl]
                if mse is not None:
                    W("vector", mse, mst)
                W("vector", SV, rot_t[0])
                tm1 = OP("vector", lambda e, gg=gg, m1sl=m1sl, tt=tt: e.tensor_tensor(
                    m1b[m1sl][:], qrnb[gg][:], rot[0][:, 512 * tt:512 * tt + 512], op=ALU.mult), inc=SV)
                mse, mst = m2_t[m1sl]
                if mse is not None:
                    W("vector", mse, mst)
                W("vector", ST, trot)
                W("vector", SV, rot_t[1])
                tm2 = OP("vector", lambda e, m1sl=m1sl, rotps=rotps, tt=tt: e.tensor_tensor(
                    m2b[m1sl][:], rotps, rot[1][:, 512 * tt:512 * tt + 512], op=ALU.mult), inc=SV)
                qrn_t[gg] = [(SV, tm1), (ST, trot)]
                fsl = fi[0] % 3
                fi[0] += 1
                fse, fst = fout_t[fsl]
                if fse is not None:
                    W("vector", fse, fst)
                W("vector", SV, tm2)
                tqf = OP("vector", lambda e, fsl=fsl, m1sl=m1sl: e.tensor_tensor(
                    fout[fsl][:], m1b[m1sl][:], m2b[m1sl][:], op=ALU.add), inc=SV)
                m1_t[m1sl] = (SV, tqf)
                m2_t[m1sl] = (SV, tqf)
                pool_t[psl4b] = [(SV, tm2)]
                W("gpsimd", SV, tqf)
                tst = DMA("gpsimd", sc_fq[128 * gg:128 * gg + 128, 512 * tt:512 * tt + 512],
                          fout[fsl][:], f"sfo{fsl}")
                fout_t[fsl] = (f"sfo{fsl}", tst)
            rotk = {}
            for gk in range(4):
                psl4 = pool_next()
                Rps = pool[psl4]
                for sem, t in pool_t[psl4]:
                    W("tensor", sem, t)
                W("tensor", SV, t4_)
                if tt == 0 and gk == 0:
                    W("tensor", "dSel", b.cur("dSel"))
                tR = OP("tensor", lambda e, gk=gk, Rps=Rps: e.matmul(
                    Rps, SelH_sb[0:16, 512 + 128 * gk:512 + 128 * gk + 128], r_all[0:16, :],
                    start=True, stop=True), inc=ST)
                rotk[gk] = [Rps, tR, psl4, None, None, None]
            for gk in range(4):
                gg = 4 + gk
                Rps, tR, psl4, _, _, _ = rotk[gk]
                krsl = gk
                for kse, kst in krn_t[krsl]:
                    W("vector", kse, kst)
                W("vector", SC, raw_t[gg])
                W("vector", ST, tR)
                tkrn = OP("vector", lambda e, gg=gg, krsl=krsl, Rps=Rps: e.tensor_tensor(
                    krnb[krsl][:], raw[gg][:], Rps, op=ALU.mult), inc=SV)
                raw_cons[gg] = [(SV, tkrn)]
                pool_t[psl4] = [(SV, tkrn)]
                rotk[gk][3] = tkrn
                rotk[gk][4] = krsl
            for gk in range(4):
                tkrn, krsl = rotk[gk][3], rotk[gk][4]
                psl4b = pool_next()
                rotps = pool[psl4b]
                for sem, t in pool_t[psl4b]:
                    W("tensor", sem, t)
                W("tensor", SV, tkrn)
                trot = OP("tensor", lambda e, krsl=krsl, rotps=rotps: e.matmul(
                    rotps, PROT_sb[:], krnb[krsl][:], start=True, stop=True), inc=ST)
                rotk[gk][5] = (rotps, trot, psl4b)
            for gk in range(4):
                tkrn, krsl = rotk[gk][3], rotk[gk][4]
                rotps, trot, psl4b = rotk[gk][5]
                m1sl = m12i[0] % 2
                m12i[0] += 1
                mse, mst = m1_t[m1sl]
                if mse is not None:
                    W("vector", mse, mst)
                W("vector", SV, tkrn)
                W("vector", SV, rot_t[2])
                tm1 = OP("vector", lambda e, krsl=krsl, m1sl=m1sl, tt=tt: e.tensor_tensor(
                    m1b[m1sl][:], krnb[krsl][:], rot[2][:, 512 * tt:512 * tt + 512], op=ALU.mult), inc=SV)
                mse, mst = m2_t[m1sl]
                if mse is not None:
                    W("vector", mse, mst)
                W("vector", ST, trot)
                W("vector", SV, rot_t[3])
                tm2 = OP("vector", lambda e, m1sl=m1sl, rotps=rotps, tt=tt: e.tensor_tensor(
                    m2b[m1sl][:], rotps, rot[3][:, 512 * tt:512 * tt + 512], op=ALU.mult), inc=SV)
                krn_t[krsl] = [(SV, tm1), (ST, trot)]
                fsl = fi[0] % 3
                fi[0] += 1
                fse, fst = fout_t[fsl]
                if fse is not None:
                    W("vector", fse, fst)
                W("vector", SV, tm2)
                tkf = OP("vector", lambda e, fsl=fsl, m1sl=m1sl: e.tensor_tensor(
                    fout[fsl][:], m1b[m1sl][:], m2b[m1sl][:], op=ALU.add), inc=SV)
                m1_t[m1sl] = (SV, tkf)
                m2_t[m1sl] = (SV, tkf)
                pool_t[psl4b] = [(SV, tm2)]
                W("gpsimd", SV, tkf)
                tst = DMA("gpsimd", sc_fk[128 * gk:128 * gk + 128, 512 * tt:512 * tt + 512],
                          fout[fsl][:], f"sfo{fsl}")
                fout_t[fsl] = (f"sfo{fsl}", tst)
            for s in range(4):
                i = 4 * tt + s
                psl4 = pool_next()
                pp = pool[psl4]
                for sem, t in pool_t[psl4]:
                    W("tensor", sem, t)
                for d in range(8):
                    tmm = OP("tensor", lambda e, d=d, s=s, pp=pp, tsl=tsl: e.matmul(
                        pp, hT[tsl][:, 512 * d + 128 * s:512 * d + 128 * s + 128],
                        wbf[d][:, 1024:1536], start=(d == 0), stop=(d == 7)), inc=ST)
                vsl = voi[0] % 2
                voi[0] += 1
                vse, vst = vout_t[vsl]
                if vse is not None:
                    W("scalar", vse, vst)
                W("scalar", ST, tmm)
                tve = OP("scalar", lambda e, vsl=vsl, pp=pp: e.copy(vout[vsl][:], pp), inc=SC)
                pool_t[psl4] = [(SC, tve)]
                W("gpsimd", SC, tve)
                tst = DMA("gpsimd", sc_fv[128 * i:128 * i + 128, :], vout[vsl][:], f"svo{vsl}")
                vout_t[vsl] = (f"svo{vsl}", tst)
            hT_cons[tsl] = b.cur(ST)
        arb.reset(m_b)
        arf.reset(m_f)

    def ph_b(l, yin_h, yout_h):
        m_b = arb.mark()
        m_f = arf.mark()
        wo2bf = [arb.alloc(1024) for _ in range(4)]
        wt = {}
        for ic in range(4):
            eng = "scalar" if ic % 2 == 0 else "vector"
            wt[ic] = ({"scalar": SC, "vector": SV}[eng],
                      load_conv(wo2_h[l, 128 * ic:128 * ic + 128, :], wo2bf[ic][:], 1024, eng))
        kwb = [arb.alloc(1024) for _ in range(4)]
        qwb = [arb.alloc(512) for _ in range(4)]
        vwb = [arb.alloc(512) for _ in range(8)]
        atb = [arb.alloc(640) for _ in range(4)]
        amb = [arb.alloc(640) for _ in range(4)]
        anb = [arb.alloc(640) for _ in range(4)]
        atT = [arb.alloc(640) for _ in range(4)]
        oT = [arb.alloc(512) for _ in range(4)]
        yf = [arf.alloc(1024) for _ in range(2)]
        y2f = [arf.alloc(1024) for _ in range(2)]
        kw_cons = [[] for _ in range(4)]
        qw_cons = [[] for _ in range(4)]
        vw_cons = [[] for _ in range(8)]
        kw_t = [0] * 4
        qw_t = [0] * 4
        vw_t = [0] * 8
        at_t = [(None, 0)] * 4
        am_t = [(None, 0)] * 4
        an_t = [(None, 0)] * 4
        atT_t = [[], [], [], []]
        oT_t = [(None, 0)] * 4
        yf_t = [0, 0]
        yf_cons = [[], []]
        y2_t = [(None, 0)] * 2
        sim_t = [[], []]
        vw_copy = [0, 0, 0, 0]
        pend = []
        pout_t = [[], []]
        wo_t = []
        pb_t = [[], []]
        ai = [0]
        pbi = [0]
        smbase = 96

        for w in range(8):
            for gg in range(4):
                if w > 0:
                    for sem, t in kw_cons[gg]:
                        W("vector", sem, t)
                    tcp = OP("vector", lambda e, gg=gg: e.tensor_copy(
                        kwb[gg][:, 0:512], kwb[gg][:, 512:1024]), inc=SV)
                    W("sync", SV, tcp)
                kw_cons[gg] = []
                kw_t[gg] = DMA("sync", kwb[gg][:, 512:1024],
                               sc_fk[128 * gg:128 * gg + 128, 512 * w:512 * w + 512], f"dkw{gg}")
                for sem, t in qw_cons[gg]:
                    W("sync", sem, t)
                qw_cons[gg] = []
                qw_t[gg] = DMA("sync", qwb[gg][:],
                               sc_fq[128 * gg:128 * gg + 128, 512 * w:512 * w + 512], f"dqw{gg}")
            for j4 in range(4):
                if w > 0:
                    for sem, t in vw_cons[j4]:
                        W("vector", sem, t)
                    for sem, t in vw_cons[4 + j4]:
                        W("vector", sem, t)
                    tcp = OP("vector", lambda e, j4=j4: e.tensor_copy(
                        vwb[j4][:], vwb[4 + j4][:]), inc=SV)
                    W("sync", SV, tcp)
                    vw_cons[j4] = []
                    vw_cons[4 + j4] = [(SV, tcp)]
                    vw_copy[j4] = tcp
                vw_t[4 + j4] = DMA("sync", vwb[4 + j4][:],
                                   sc_fv[512 * w + 128 * j4:512 * w + 128 * j4 + 128, :], f"dvw{j4}")
            def emit_tail(info):
                a2 = info["a2"]; nq = info["nq"]; njc = info["njc"]
                vbase = info["vbase"]; r0 = info["r0"]; hp = info["hp"]
                h2 = info["h2"]; tan = info["tan"]; pout = info["pout"]; gg = info["gg"]
                pbsl = pbi[0] % 2
                pbi[0] += 1
                PB = PB0 if pbsl == 0 else PB1
                for sem, t in pb_t[pbsl]:
                    W("tensor", sem, t)
                W("tensor", SV, tan)
                for j in range(njc):
                    wd = min(128, nq - 128 * j)
                    ttr = OP("tensor", lambda e, PB=PB, a2=a2, j=j, wd=wd: e.transpose(
                        PB[:, 128 * j:128 * j + wd], anb[a2][:, 128 * j:128 * j + wd], IDB[:]),
                        inc=ST)
                an_t[a2] = (ST, ttr)
                for sem, t in atT_t[a2]:
                    W("scalar", sem, t)
                W("scalar", ST, ttr)
                tate = OP("scalar", lambda e, PB=PB, a2=a2, nq=nq: e.copy(
                    atT[a2][:, 0:nq], PB[:, 0:nq]), inc=SC)
                pb_t[pbsl] = [(SC, tate)]
                W("tensor", SC, tate)
                for j in range(njc):
                    vs = vbase + j
                    if vs >= 4:
                        W("tensor", f"dvw{vs - 4}", vw_t[vs])
                    elif vw_copy[vs]:
                        W("tensor", SV, vw_copy[vs])
                    wd = min(128, nq - 128 * j)
                    tav = OP("tensor", lambda e, vs=vs, h2=h2, hp=hp, a2=a2, j=j, r0=r0,
                             wd=wd, pout=pout: e.matmul(
                        pout[64 * hp:64 * hp + 64, 128 * r0:128 * r0 + 128],
                        vwb[vs][:, 64 * h2:64 * h2 + 64],
                        atT[a2][:, 128 * j:128 * j + wd],
                        start=(j == 0), stop=(j == njc - 1),
                        skip_group_check=True), inc=ST)
                    vw_cons[vs].append((ST, tav))
                atT_t[a2] = [(ST, tav)]
                if info["last"]:
                    ose, ost = oT_t[gg]
                    if ose is not None:
                        W("scalar", ose, ost)
                    W("scalar", ST, b.cur(ST))
                    toev = OP("scalar", lambda e, gg=gg, pout=pout: e.copy(oT[gg][:], pout), inc=SC)
                    oT_t[gg] = (SC, toev)
                    pout_t[info["pslot"]] = [(SC, toev)]

            for gg in range(4):
                pslot = gg % 2
                pout = PK[4] if pslot == 0 else PK[5]
                first_inst_of_pair = True
                for r0 in range(4):
                    i = 4 * w + r0
                    if w == 0:
                        nq = 128 * (r0 + 1)
                        cst = 512
                        njc = r0 + 1
                        vbase = 4
                        mask = MW0[:, [0, 128, 384, 768][r0]:[0, 128, 384, 768][r0] + nq]
                    else:
                        nq = 640
                        cst = 128 * r0
                        njc = 5
                        vbase = r0
                        mask = MB[:, 0:640]
                    for hp in range(2):
                        h2 = 2 * gg + hp
                        ssl = hp
                        psim = (P0 if ssl == 0 else P1)[:, 0:nq]
                        for sem, t in sim_t[ssl]:
                            W("tensor", sem, t)
                        sim_t[ssl] = []
                        if first_inst_of_pair:
                            for sem, t in pout_t[pslot]:
                                W("tensor", sem, t)
                            first_inst_of_pair = False
                        W("tensor", f"dkw{gg}", kw_t[gg])
                        W("tensor", f"dqw{gg}", qw_t[gg])
                        nmm = (nq + 511) // 512
                        off = 0
                        for z in range(nmm):
                            wd = min(512, nq - off)
                            tsim = OP("tensor", lambda e, gg=gg, hp=hp, r0=r0, psim=psim,
                                      cst=cst, off=off, wd=wd: e.matmul(
                                psim[:, off:off + wd],
                                qwb[gg][64 * hp:64 * hp + 64, 128 * r0:128 * r0 + 128],
                                kwb[gg][64 * hp:64 * hp + 64, cst + off:cst + off + wd],
                                start=True, stop=True), inc=ST)
                            off += wd
                        qw_cons[gg].append((ST, tsim))
                        kw_cons[gg].append((ST, tsim))
                        cb = smbase + 4 * (ai[0] * 2 + hp) % 32
                        nm = smf[:, cb:cb + 1]
                        nb2 = smf[:, cb + 1:cb + 2]
                        rsum = smf[:, cb + 2:cb + 3]
                        rc = smf[:, cb + 3:cb + 4]
                        W("vector", ST, tsim)
                        tnm = OP("vector", lambda e, psim=psim, nm=nm: e.tensor_reduce(
                            nm, psim, AX.X, op=ALU.max, negate=True), inc=SV)
                        asl = ai[0] * 2 + hp
                        a2 = asl % 4
                        ase, ast = at_t[a2]
                        if ase is not None:
                            W("scalar", ase, ast)
                        W("scalar", SV, tnm)
                        texp = OP("scalar", lambda e, psim=psim, a2=a2, nq=nq, nm=nm: e.activation(
                            atb[a2][:, 0:nq], psim, AF.Exp, bias=nm), inc=SC)
                        sim_t[ssl] = [(SC, texp), (SV, tnm)]
                        mse, mst = am_t[a2]
                        if mse is not None:
                            W("vector", mse, mst)
                        W("vector", SC, texp)
                        tstt = OP("vector", lambda e, a2=a2, nq=nq, mask=mask, rsum=rsum: e.scalar_tensor_tensor(
                            amb[a2][:, 0:nq], atb[a2][:, 0:nq], 1.0, mask,
                            op0=ALU.mult, op1=ALU.mult, accum_out=rsum), inc=SV)
                        at_t[a2] = (SV, tstt)
                        W("vector", SV, tstt)
                        trc = OP("vector", lambda e, rc=rc, rsum=rsum: e.reciprocal(rc, rsum), inc=SV)
                        nse, nst = an_t[a2]
                        if nse is not None:
                            W("vector", nse, nst)
                        W("vector", SV, trc)
                        tan = OP("vector", lambda e, a2=a2, nq=nq, rc=rc: e.tensor_scalar(
                            anb[a2][:, 0:nq], amb[a2][:, 0:nq], rc, None, op0=ALU.mult), inc=SV)
                        am_t[a2] = (SV, tan)
                        info = dict(a2=a2, nq=nq, njc=njc, vbase=vbase, r0=r0, hp=hp,
                                    h2=h2, tan=tan, pout=pout, gg=gg, pslot=pslot,
                                    last=(r0 == 3 and hp == 1))
                        if len(pend) >= 1:
                            emit_tail(pend.pop(0))
                        pend.append(info)
                    ai[0] += 1
            while pend:
                emit_tail(pend.pop(0))
            oev = b.cur(SC)
            for r0 in range(4):
                i = 4 * w + r0
                sl = i % 2
                for sem, t in yf_cons[sl]:
                    W("sync", sem, t)
                yf_cons[sl] = []
                tyf = DMA("sync", yf[sl][:], yin_h[128 * i:128 * i + 128, :], f"dyb{sl}")
                for half in range(2):
                    for sem, t in wo_t:
                        W("tensor", sem, t)
                    wo_t = []
                    pwo = PK[5][:, :]
                    if r0 == 0 and half == 0:
                        for sem, t in pout_t[1]:
                            W("tensor", sem, t)
                    for ic in range(4):
                        W("tensor", SC, oev)
                        W("tensor", wt[ic][0], wt[ic][1])
                        tmm = OP("tensor", lambda e, ic=ic, r0=r0, half=half, pwo=pwo: e.matmul(
                            pwo, oT[ic][:, 128 * r0:128 * r0 + 128],
                            wo2bf[ic][:, 512 * half:512 * half + 512],
                            start=(ic == 0), stop=(ic == 3)), inc=ST)
                    y2e, y2tt = y2_t[sl]
                    if half == 0 and y2e is not None:
                        W("vector", y2e, y2tt)
                    W("vector", f"dyb{sl}", tyf)
                    W("vector", ST, tmm)
                    ty2 = OP("vector", lambda e, sl=sl, half=half, pwo=pwo: e.tensor_tensor(
                        y2f[sl][:, 512 * half:512 * half + 512], pwo,
                        yf[sl][:, 512 * half:512 * half + 512], op=ALU.add), inc=SV)
                    wo_t = [(SV, ty2)]
                yf_cons[sl].append((SV, ty2))
                W("gpsimd", SV, ty2)
                tsy = DMA("gpsimd", yout_h[128 * i:128 * i + 128, :], y2f[sl][:], f"sy2{sl}")
                y2_t[sl] = (f"sy2{sl}", tsy)
            for gg in range(4):
                oT_t[gg] = (ST, b.cur(ST))
            pout_t[1] = [(SV, b.cur(SV))]
        arb.reset(m_b)
        arf.reset(m_f)

    def ph_ab(l, yin_h, yout_h):
        m_b = arb.mark()
        m_f = arf.mark()
        bank_t = {k: [] for k in ("0a", "0b", "1a", "1b", "2a", "2b", "pb0", "pb1", "ssb")}

        def bwait(eng, keys):
            for k in keys:
                for sem, t in bank_t[k]:
                    W(eng, sem, t)

        def bset(keys, consumers):
            for k in keys:
                bank_t[k] = list(consumers)

        rot = [arb.alloc(4096) for _ in range(4)]
        t_qs = DMA("sync", qsks_sb[:], qsks_h[l], "dqsks")
        rot_t = []
        for r in range(4):
            tl_ = DMA("sync", rot[r], rotc_h[r], f"drot{r}")
            W("vector", f"drot{r}", tl_)
            W("vector", "dqsks", t_qs)
            rot_t.append(OP("vector", lambda e, r=r, rr=rot[r]: e.tensor_scalar(
                rr, rr, qsks_sb[:, r:r + 1], None, op0=ALU.mult), inc=SV))
        wbf = [arb.alloc(1536) for _ in range(8)]
        wt = {}
        for d in range(8):
            eng = "scalar" if d % 2 == 0 else "vector"
            t2 = load_conv(wqkv_h[l, 128 * d:128 * d + 128, :], wbf[d][:], 1536, eng)
            wt[d] = ({"scalar": SC, "vector": SV}[eng], t2)
        yf = [arf.alloc(512) for _ in range(2)]
        ysh = [arf.alloc(512) for _ in range(1)]
        hbf = [arb.alloc(1024) for _ in range(2)]
        hT = [arb.alloc(4096) for _ in range(2)]
        sqb = [arb.alloc(512) for _ in range(3)]
        raw = [arb.alloc(512) for _ in range(8)]
        krnb = [arb.alloc(512) for _ in range(4)]
        qrnb = [arb.alloc(512) for _ in range(4)]
        m1b = [arb.alloc(512) for _ in range(2)]
        m2b = [arb.alloc(512) for _ in range(2)]
        fout = [arb.alloc(512) for _ in range(3)]
        vout = [arb.alloc(512) for _ in range(2)]
        ssf = arf.alloc(512)
        rln = IOT
        r_all = rln
        yf_t = [0, 0]
        yf_cons = [[], []]
        ysh_cons = [[]]
        hbf_cons = [0, 0]
        hT_cons = [0, 0]
        hT_t = [0, 0]
        sq_t = [(None, 0)] * 3
        raw_cons = [[] for _ in range(8)]
        krn_t = [[], [], [], []]
        qrn_t = [[], [], [], []]
        m1_t = [(None, 0)] * 2
        m2_t = [(None, 0)] * 2
        fout_t = [(None, 0)] * 3
        vout_t = [(None, 0)] * 2
        pool = [("0a", P0[:, 0:512]), ("0b", P0[:, 512:1024]),
                ("1a", P1[:, 0:512]), ("2a", P2[:, 0:512])]
        ssps = P1[0:16, 512:1024]
        PBSL = [("pb0", PB0), ("pb1", PB1)]
        pli = [0]
        sqi = [0]
        fi = [0]
        m12i = [0]
        voi = [0]
        pbi = [0]
        fq_st = {}
        fk_st = {}
        fv_st = {}

        def pool_next():
            s = pli[0] % 4
            pli[0] += 1
            return pool[s]

        wo2bf = [arb.alloc(1024) for _ in range(4)]
        wtB = {}
        for ic in range(4):
            eng = "scalar" if ic % 2 == 0 else "vector"
            wtB[ic] = ({"scalar": SC, "vector": SV}[eng],
                       load_conv(wo2_h[l, 128 * ic:128 * ic + 128, :], wo2bf[ic][:], 1024, eng))
        kwb = [arb.alloc(1024) for _ in range(4)]
        qwb = [arb.alloc(512) for _ in range(4)]
        vwb = [arb.alloc(512) for _ in range(8)]
        atb = [arb.alloc(640) for _ in range(3)]
        amb = [arb.alloc(640) for _ in range(3)]
        anb = [arb.alloc(640) for _ in range(3)]
        atT = [arb.alloc(640) for _ in range(3)]
        oT = [arb.alloc(512) for _ in range(4)]
        yfB = [arf.alloc(1024) for _ in range(2)]
        y2f = [arf.alloc(1024) for _ in range(2)]
        kw_cons = [[] for _ in range(4)]
        qw_cons = [[] for _ in range(4)]
        vw_cons = [[] for _ in range(8)]
        kw_t = [0] * 4
        qw_t = [0] * 4
        vw_t = [0] * 8
        at_t = [(None, 0)] * 4
        am_t = [(None, 0)] * 4
        an_t = [(None, 0)] * 4
        atT_t = [[], [], [], []]
        oT_t = [(None, 0)] * 4
        yfB_cons = [[], []]
        y2_t = [(None, 0)] * 2
        vw_copy = [0, 0, 0, 0]
        pend = [None]
        ai = [0]
        smbase = 96

        def emit_tail(info):
            a2 = info["a2"]; nq = info["nq"]; njc = info["njc"]
            vbase = info["vbase"]; r0 = info["r0"]; hp = info["hp"]
            h2 = info["h2"]; tan = info["tan"]; pout = info["pout"]; gg = info["gg"]
            pbsl = pbi[0] % 2
            pbi[0] += 1
            pbk, PB = PBSL[pbsl]
            bwait("tensor", [pbk])
            W("tensor", SV, tan)
            for j in range(njc):
                wd = min(128, nq - 128 * j)
                ttr = OP("tensor", lambda e, PB=PB, a2=a2, j=j, wd=wd: e.transpose(
                    PB[:, 128 * j:128 * j + wd], anb[a2][:, 128 * j:128 * j + wd], IDB[:]),
                    inc=ST)
            an_t[a2] = (ST, ttr)
            for sem, t in atT_t[a2]:
                W("scalar", sem, t)
            W("scalar", ST, ttr)
            tate = OP("scalar", lambda e, PB=PB, a2=a2, nq=nq: e.copy(
                atT[a2][:, 0:nq], PB[:, 0:nq]), inc=SC)
            bset([pbk], [(SC, tate)])
            W("tensor", SC, tate)
            for j in range(njc):
                vs = vbase + j
                if vs >= 4:
                    W("tensor", f"dvw{vs - 4}", vw_t[vs])
                elif vw_copy[vs]:
                    W("tensor", SV, vw_copy[vs])
                wd = min(128, nq - 128 * j)
                tav = OP("tensor", lambda e, vs=vs, h2=h2, hp=hp, a2=a2, j=j, r0=r0,
                         wd=wd, pout=pout: e.matmul(
                    pout[64 * hp:64 * hp + 64, 128 * r0:128 * r0 + 128],
                    vwb[vs][:, 64 * h2:64 * h2 + 64],
                    atT[a2][:, 128 * j:128 * j + wd],
                    start=(j == 0), stop=(j == njc - 1),
                    skip_group_check=True), inc=ST)
                vw_cons[vs].append((ST, tav))
            atT_t[a2] = [(ST, tav)]
            if info["last"]:
                ose, ost = oT_t[gg]
                if ose is not None:
                    W("scalar", ose, ost)
                W("scalar", ST, b.cur(ST))
                toev = OP("scalar", lambda e, gg=gg, pout=pout: e.copy(oT[gg][:], pout), inc=SC)
                oT_t[gg] = (SC, toev)
                bset([info["poutk"]], [(SC, toev)])

        def emit_A(tt):
            tsl = tt % 2
            for s in range(4):
                i = 4 * tt + s
                sl = i % 2
                for sem, t in yf_cons[sl]:
                    W("sync", sem, t)
                yf_cons[sl] = []
                yf_t[sl] = DMA("sync", yf[sl][:], yin_h[128 * i:128 * i + 128, 0:512], f"dyf{sl}")
                for sem, t in ysh_cons[0]:
                    W("sync", sem, t)
                ysh_cons[0] = []
                if i == 0:
                    DMA("sync", ysh[0][1:128, :], yin_h[0:127, 512:1024], "dys0")
                    tsh = DMA("sync", ysh[0][0:1, :], zrow_h[0:1, 0:512], "dys0")
                else:
                    tsh = DMA("sync", ysh[0][:], yin_h[128 * i - 1:128 * i + 127, 512:1024], "dys0")
                if hbf_cons[sl]:
                    W("vector", ST, hbf_cons[sl])
                W("vector", f"dyf{sl}", yf_t[sl])
                tc1 = OP("vector", lambda e, sl=sl: e.tensor_copy(
                    hbf[sl][:, 0:512], yf[sl][:]), inc=SV)
                W("vector", "dys0", tsh)
                tc3 = OP("vector", lambda e, sl=sl: e.tensor_copy(
                    hbf[sl][:, 512:1024], ysh[0][:]), inc=SV)
                yf_cons[sl].append((SV, tc1))
                ysh_cons[0].append((SV, tc3))
                pbsl = pbi[0] % 2
                pbi[0] += 1
                pbk, PB = PBSL[pbsl]
                bwait("tensor", [pbk])
                W("tensor", SV, tc3)
                for k in range(8):
                    tmm = OP("tensor", lambda e, k=k, PB=PB, sl=sl: e.transpose(
                        PB[:, 128 * k:128 * k + 128], hbf[sl][:, 128 * k:128 * k + 128], IDB[:]),
                        inc=ST)
                hbf_cons[sl] = tmm
                dst = hT[tsl].rearrange("p (k c) -> p k c", k=8)
                W("scalar", ST, tmm)
                if s == 0 and hT_cons[tsl]:
                    W("scalar", ST, hT_cons[tsl])
                te1 = OP("scalar", lambda e, PB=PB, dst=dst, s=s: e.copy(
                    dst[:, 0:4, 128 * s:128 * s + 128], PB[:, 0:512].rearrange("p (k c) -> p k c", k=4)), inc=SC)
                te2 = OP("scalar", lambda e, PB=PB, dst=dst, s=s: e.copy(
                    dst[:, 4:8, 128 * s:128 * s + 128], PB[:, 512:1024].rearrange("p (k c) -> p k c", k=4)), inc=SC)
                bset([pbk], [(SC, te2)])
                hT_t[tsl] = te2
            raw_t = {}
            for gg in range(8):
                pk_, pp = pool_next()
                bwait("tensor", [pk_])
                W("tensor", SC, hT_t[tsl])
                for d in range(8):
                    W("tensor", wt[d][0], wt[d][1])
                    tmm = OP("tensor", lambda e, d=d, gg=gg, pp=pp, tsl=tsl: e.matmul(
                        pp, wbf[d][:, 128 * gg:128 * gg + 128],
                        hT[tsl][:, 512 * d:512 * d + 512],
                        start=(d == 0), stop=(d == 7)), inc=ST)
                sqsl = sqi[0] % 3
                sqi[0] += 1
                sse, sst = sq_t[sqsl]
                W("scalar", ST, tmm)
                if sse is not None:
                    W("scalar", sse, sst)
                tsq = OP("scalar", lambda e, sqsl=sqsl, pp=pp: e.activation(
                    sqb[sqsl][:], pp, AF.Square), inc=SC)
                for sem, t in raw_cons[gg]:
                    W("scalar", sem, t)
                raw_cons[gg] = []
                trw = OP("scalar", lambda e, gg=gg, pp=pp: e.copy(raw[gg][:], pp), inc=SC)
                raw_t[gg] = trw
                W("tensor", SC, tsq)
                if gg == 0:
                    W("tensor", "dE8", b.cur("dE8"))
                    bwait("tensor", ["1b"])
                tss = OP("tensor", lambda e, gg=gg, sqsl=sqsl: e.matmul(
                    ssps, E8_sb[:, 16 * gg:16 * gg + 16], sqb[sqsl][:],
                    start=(gg == 0), stop=(gg == 7)), inc=ST)
                sq_t[sqsl] = (ST, tss)
                bset([pk_], [(SC, trw)])
            W("vector", ST, tss)
            t1 = OP("vector", lambda e: e.tensor_scalar(ssf[0:16, :], ssps, 6e-20, None,
                                                        op0=ALU.max), inc=SV)
            bset(["1b"], [(SV, t1)])
            W("scalar", SV, t1)
            t2_ = OP("scalar", lambda e: e.activation(rln[0:16, :], ssf[0:16, :], AF.Ln), inc=SC)
            W("scalar", SC, t2_)
            t3_ = OP("scalar", lambda e: e.activation(ssf[0:16, :], rln[0:16, :], AF.Exp, scale=-0.5), inc=SC)
            W("vector", SC, t3_)
            t4_ = OP("vector", lambda e: e.tensor_scalar(r_all[0:16, :], ssf[0:16, :], 1e12, None,
                                                         op0=ALU.min), inc=SV)
            rotq = {}
            for gg in range(4):
                pk_, Rps = pool_next()
                bwait("tensor", [pk_])
                W("tensor", SV, t4_)
                if tt == 0 and gg == 0:
                    W("tensor", "dSel", b.cur("dSel"))
                tR = OP("tensor", lambda e, gg=gg, Rps=Rps: e.matmul(
                    Rps, SelH_sb[0:16, 128 * gg:128 * gg + 128], r_all[0:16, :],
                    start=True, stop=True), inc=ST)
                rotq[gg] = [Rps, tR, pk_, None, None]
            for gg in range(4):
                Rps, tR, pk_, _, _ = rotq[gg]
                for kse, kst in qrn_t[gg]:
                    W("vector", kse, kst)
                W("vector", SC, raw_t[gg])
                W("vector", ST, tR)
                tqrn = OP("vector", lambda e, gg=gg, Rps=Rps: e.tensor_tensor(
                    qrnb[gg][:], raw[gg][:], Rps, op=ALU.mult), inc=SV)
                raw_cons[gg] = [(SV, tqrn)]
                bset([pk_], [(SV, tqrn)])
                rotq[gg][3] = tqrn
            for gg in range(4):
                tqrn = rotq[gg][3]
                pk_b, rotps = pool_next()
                bwait("tensor", [pk_b])
                W("tensor", SV, tqrn)
                if tt == 0 and gg == 0:
                    W("tensor", "dPROT", b.cur("dPROT"))
                trot = OP("tensor", lambda e, gg=gg, rotps=rotps: e.matmul(
                    rotps, PROT_sb[:], qrnb[gg][:], start=True, stop=True), inc=ST)
                rotq[gg][4] = (rotps, trot, pk_b)
            for gg in range(4):
                tqrn = rotq[gg][3]
                rotps, trot, pk_b = rotq[gg][4]
                m1sl = m12i[0] % 2
                m12i[0] += 1
                mse, mst = m1_t[m1sl]
                if mse is not None:
                    W("vector", mse, mst)
                W("vector", SV, rot_t[0])
                tm1 = OP("vector", lambda e, gg=gg, m1sl=m1sl, tt=tt: e.tensor_tensor(
                    m1b[m1sl][:], qrnb[gg][:], rot[0][:, 512 * tt:512 * tt + 512], op=ALU.mult), inc=SV)
                mse, mst = m2_t[m1sl]
                if mse is not None:
                    W("vector", mse, mst)
                W("vector", ST, trot)
                W("vector", SV, rot_t[1])
                tm2 = OP("vector", lambda e, m1sl=m1sl, rotps=rotps, tt=tt: e.tensor_tensor(
                    m2b[m1sl][:], rotps, rot[1][:, 512 * tt:512 * tt + 512], op=ALU.mult), inc=SV)
                qrn_t[gg] = [(SV, tm1), (ST, trot)]
                fsl = fi[0] % 3
                fi[0] += 1
                fse, fst = fout_t[fsl]
                if fse is not None:
                    W("vector", fse, fst)
                W("vector", SV, tm2)
                tqf = OP("vector", lambda e, fsl=fsl, m1sl=m1sl: e.tensor_tensor(
                    fout[fsl][:], m1b[m1sl][:], m2b[m1sl][:], op=ALU.add), inc=SV)
                m1_t[m1sl] = (SV, tqf)
                m2_t[m1sl] = (SV, tqf)
                bset([pk_b], [(SV, tm2)])
                W("gpsimd", SV, tqf)
                tst = DMA("gpsimd", sc_fq[128 * gg:128 * gg + 128, 512 * tt:512 * tt + 512],
                          fout[fsl][:], f"sfo{fsl}")
                fout_t[fsl] = (f"sfo{fsl}", tst)
                fq_st[(tt, gg)] = (f"sfo{fsl}", tst)
            rotk = {}
            for gk in range(4):
                pk_, Rps = pool_next()
                bwait("tensor", [pk_])
                W("tensor", SV, t4_)
                tR = OP("tensor", lambda e, gk=gk, Rps=Rps: e.matmul(
                    Rps, SelH_sb[0:16, 512 + 128 * gk:512 + 128 * gk + 128], r_all[0:16, :],
                    start=True, stop=True), inc=ST)
                rotk[gk] = [Rps, tR, pk_, None, None]
            for gk in range(4):
                gg = 4 + gk
                Rps, tR, pk_, _, _ = rotk[gk]
                for kse, kst in krn_t[gk]:
                    W("vector", kse, kst)
                W("vector", SC, raw_t[gg])
                W("vector", ST, tR)
                tkrn = OP("vector", lambda e, gg=gg, gk=gk, Rps=Rps: e.tensor_tensor(
                    krnb[gk][:], raw[gg][:], Rps, op=ALU.mult), inc=SV)
                raw_cons[gg] = [(SV, tkrn)]
                bset([pk_], [(SV, tkrn)])
                rotk[gk][3] = tkrn
            for gk in range(4):
                tkrn = rotk[gk][3]
                pk_b, rotps = pool_next()
                bwait("tensor", [pk_b])
                W("tensor", SV, tkrn)
                trot = OP("tensor", lambda e, gk=gk, rotps=rotps: e.matmul(
                    rotps, PROT_sb[:], krnb[gk][:], start=True, stop=True), inc=ST)
                rotk[gk][4] = (rotps, trot, pk_b)
            for gk in range(4):
                tkrn = rotk[gk][3]
                rotps, trot, pk_b = rotk[gk][4]
                m1sl = m12i[0] % 2
                m12i[0] += 1
                mse, mst = m1_t[m1sl]
                if mse is not None:
                    W("vector", mse, mst)
                W("vector", SV, rot_t[2])
                tm1 = OP("vector", lambda e, gk=gk, m1sl=m1sl, tt=tt: e.tensor_tensor(
                    m1b[m1sl][:], krnb[gk][:], rot[2][:, 512 * tt:512 * tt + 512], op=ALU.mult), inc=SV)
                mse, mst = m2_t[m1sl]
                if mse is not None:
                    W("vector", mse, mst)
                W("vector", ST, trot)
                W("vector", SV, rot_t[3])
                tm2 = OP("vector", lambda e, m1sl=m1sl, rotps=rotps, tt=tt: e.tensor_tensor(
                    m2b[m1sl][:], rotps, rot[3][:, 512 * tt:512 * tt + 512], op=ALU.mult), inc=SV)
                krn_t[gk] = [(SV, tm1), (ST, trot)]
                fsl = fi[0] % 3
                fi[0] += 1
                fse, fst = fout_t[fsl]
                if fse is not None:
                    W("vector", fse, fst)
                W("vector", SV, tm2)
                tkf = OP("vector", lambda e, fsl=fsl, m1sl=m1sl: e.tensor_tensor(
                    fout[fsl][:], m1b[m1sl][:], m2b[m1sl][:], op=ALU.add), inc=SV)
                m1_t[m1sl] = (SV, tkf)
                m2_t[m1sl] = (SV, tkf)
                bset([pk_b], [(SV, tm2)])
                W("gpsimd", SV, tkf)
                tst = DMA("gpsimd", sc_fk[128 * gk:128 * gk + 128, 512 * tt:512 * tt + 512],
                          fout[fsl][:], f"sfo{fsl}")
                fout_t[fsl] = (f"sfo{fsl}", tst)
                fk_st[(tt, gk)] = (f"sfo{fsl}", tst)
            for s in range(4):
                i = 4 * tt + s
                pk_, pp = pool_next()
                bwait("tensor", [pk_])
                for d in range(8):
                    tmm = OP("tensor", lambda e, d=d, s=s, pp=pp, tsl=tsl: e.matmul(
                        pp, hT[tsl][:, 512 * d + 128 * s:512 * d + 128 * s + 128],
                        wbf[d][:, 1024:1536], start=(d == 0), stop=(d == 7)), inc=ST)
                vsl = voi[0] % 2
                voi[0] += 1
                vse, vst = vout_t[vsl]
                if vse is not None:
                    W("scalar", vse, vst)
                W("scalar", ST, tmm)
                tve = OP("scalar", lambda e, vsl=vsl, pp=pp: e.copy(vout[vsl][:], pp), inc=SC)
                bset([pk_], [(SC, tve)])
                W("gpsimd", SC, tve)
                tst = DMA("gpsimd", sc_fv[128 * i:128 * i + 128, :], vout[vsl][:], f"svo{vsl}")
                vout_t[vsl] = (f"svo{vsl}", tst)
                fv_st[(tt, s)] = (f"svo{vsl}", tst)
            hT_cons[tsl] = b.cur(ST)

        def emit_B(w):
            wo_t = []
            for gg in range(4):
                if w > 0:
                    for sem, t in kw_cons[gg]:
                        W("vector", sem, t)
                    tcp = OP("vector", lambda e, gg=gg: e.tensor_copy(
                        kwb[gg][:, 0:512], kwb[gg][:, 512:1024]), inc=SV)
                    W("sync", SV, tcp)
                kw_cons[gg] = []
                W("sync", fk_st[(w, gg)][0], fk_st[(w, gg)][1])
                kw_t[gg] = DMA("sync", kwb[gg][:, 512:1024],
                               sc_fk[128 * gg:128 * gg + 128, 512 * w:512 * w + 512], f"dkw{gg}")
                for sem, t in qw_cons[gg]:
                    W("sync", sem, t)
                qw_cons[gg] = []
                W("sync", fq_st[(w, gg)][0], fq_st[(w, gg)][1])
                qw_t[gg] = DMA("sync", qwb[gg][:],
                               sc_fq[128 * gg:128 * gg + 128, 512 * w:512 * w + 512], f"dqw{gg}")
            for j4 in range(4):
                if w > 0:
                    for sem, t in vw_cons[j4]:
                        W("vector", sem, t)
                    for sem, t in vw_cons[4 + j4]:
                        W("vector", sem, t)
                    tcp = OP("vector", lambda e, j4=j4: e.tensor_copy(
                        vwb[j4][:], vwb[4 + j4][:]), inc=SV)
                    W("sync", SV, tcp)
                    vw_cons[j4] = []
                    vw_cons[4 + j4] = [(SV, tcp)]
                    vw_copy[j4] = tcp
                W("sync", fv_st[(w, j4)][0], fv_st[(w, j4)][1])
                vw_t[4 + j4] = DMA("sync", vwb[4 + j4][:],
                                   sc_fv[512 * w + 128 * j4:512 * w + 128 * j4 + 128, :], f"dvw{j4}")
            for gg in range(4):
                poutk = "2a" if gg % 2 == 0 else "2b"
                pout = P2[:, 0:512] if gg % 2 == 0 else P2[:, 512:1024]
                first_inst_of_pair = True
                for r0 in range(4):
                    i = 4 * w + r0
                    if w == 0:
                        nq = 128 * (r0 + 1)
                        cst = 512
                        njc = r0 + 1
                        vbase = 4
                        mask = MW0[:, [0, 128, 384, 768][r0]:[0, 128, 384, 768][r0] + nq]
                    else:
                        nq = 640
                        cst = 128 * r0
                        njc = 5
                        vbase = r0
                        mask = MB[:, 0:640]
                    for hp in range(2):
                        h2 = 2 * gg + hp
                        simk = ["0a", "0b"] if hp == 0 else ["1a", "1b"]
                        if nq <= 512:
                            simk = simk[:1]
                        psim = (P0 if hp == 0 else P1)[:, 0:nq]
                        bwait("tensor", simk)
                        if first_inst_of_pair:
                            bwait("tensor", [poutk])
                            first_inst_of_pair = False
                        W("tensor", f"dkw{gg}", kw_t[gg])
                        W("tensor", f"dqw{gg}", qw_t[gg])
                        nmm = (nq + 511) // 512
                        off = 0
                        for z in range(nmm):
                            wd = min(512, nq - off)
                            tsim = OP("tensor", lambda e, gg=gg, hp=hp, r0=r0, psim=psim,
                                      cst=cst, off=off, wd=wd: e.matmul(
                                psim[:, off:off + wd],
                                qwb[gg][64 * hp:64 * hp + 64, 128 * r0:128 * r0 + 128],
                                kwb[gg][64 * hp:64 * hp + 64, cst + off:cst + off + wd],
                                start=True, stop=True), inc=ST)
                            off += wd
                        qw_cons[gg].append((ST, tsim))
                        kw_cons[gg].append((ST, tsim))
                        cb = smbase + 4 * (ai[0] * 2 + hp) % 32
                        nm = smf[:, cb:cb + 1]
                        W("vector", ST, tsim)
                        tnm = OP("vector", lambda e, psim=psim, nm=nm: e.tensor_reduce(
                            nm, psim, AX.X, op=ALU.max, negate=True), inc=SV)
                        asl = ai[0] * 2 + hp
                        a2 = asl % 3
                        ase, ast = at_t[a2]
                        if ase is not None:
                            W("scalar", ase, ast)
                        W("scalar", SV, tnm)
                        texp = OP("scalar", lambda e, psim=psim, a2=a2, nq=nq, nm=nm: e.activation(
                            atb[a2][:, 0:nq], psim, AF.Exp, bias=nm), inc=SC)
                        bset(simk, [(SC, texp), (SV, tnm)])
                        rsum = smf[:, cb + 2:cb + 3]
                        rc = smf[:, cb + 3:cb + 4]
                        mse, mst = am_t[a2]
                        if mse is not None:
                            W("vector", mse, mst)
                        W("vector", SC, texp)
                        tstt = OP("vector", lambda e, a2=a2, nq=nq, mask=mask, rsum=rsum: e.scalar_tensor_tensor(
                            amb[a2][:, 0:nq], atb[a2][:, 0:nq], 1.0, mask,
                            op0=ALU.mult, op1=ALU.mult, accum_out=rsum), inc=SV)
                        at_t[a2] = (SV, tstt)
                        W("vector", SV, tstt)
                        trc = OP("vector", lambda e, rc=rc, rsum=rsum: e.reciprocal(rc, rsum), inc=SV)
                        nse, nst = an_t[a2]
                        if nse is not None:
                            W("vector", nse, nst)
                        W("vector", SV, trc)
                        tan = OP("vector", lambda e, a2=a2, nq=nq, rc=rc: e.tensor_scalar(
                            anb[a2][:, 0:nq], amb[a2][:, 0:nq], rc, None, op0=ALU.mult), inc=SV)
                        am_t[a2] = (SV, tan)
                        info = dict(a2=a2, nq=nq, njc=njc, vbase=vbase, r0=r0, hp=hp,
                                    h2=h2, tan=tan, pout=pout, gg=gg, poutk=poutk,
                                    last=(r0 == 3 and hp == 1))
                        if pend[0] is not None:
                            emit_tail(pend[0])
                        pend[0] = info
                    ai[0] += 1
            if pend[0] is not None:
                emit_tail(pend[0])
                pend[0] = None
            oev = b.cur(SC)
            for r0 in range(4):
                i = 4 * w + r0
                sl = i % 2
                for sem, t in yfB_cons[sl]:
                    W("sync", sem, t)
                yfB_cons[sl] = []
                tyf = DMA("sync", yfB[sl][:], yin_h[128 * i:128 * i + 128, :], f"dyb{sl}")
                for half in range(2):
                    for sem, t in wo_t:
                        W("tensor", sem, t)
                    wo_t = []
                    pwo = P2[:, 512:1024]
                    if r0 == 0 and half == 0:
                        bwait("tensor", ["2b"])
                    for ic in range(4):
                        W("tensor", SC, oev)
                        W("tensor", wtB[ic][0], wtB[ic][1])
                        tmm = OP("tensor", lambda e, ic=ic, r0=r0, half=half, pwo=pwo: e.matmul(
                            pwo, oT[ic][:, 128 * r0:128 * r0 + 128],
                            wo2bf[ic][:, 512 * half:512 * half + 512],
                            start=(ic == 0), stop=(ic == 3)), inc=ST)
                    y2e, y2tt = y2_t[sl]
                    if half == 0 and y2e is not None:
                        W("vector", y2e, y2tt)
                    W("vector", f"dyb{sl}", tyf)
                    W("vector", ST, tmm)
                    ty2 = OP("vector", lambda e, sl=sl, half=half, pwo=pwo: e.tensor_tensor(
                        y2f[sl][:, 512 * half:512 * half + 512], pwo,
                        yfB[sl][:, 512 * half:512 * half + 512], op=ALU.add), inc=SV)
                    wo_t = [(SV, ty2)]
                yfB_cons[sl].append((SV, ty2))
                W("gpsimd", SV, ty2)
                tsy = DMA("gpsimd", yout_h[128 * i:128 * i + 128, :], y2f[sl][:], f"sy2{sl}")
                y2_t[sl] = (f"sy2{sl}", tsy)
            for gg in range(4):
                oT_t[gg] = (ST, b.cur(ST))
            bset(["2b"], [(SV, b.cur(SV))])

        for tt in range(8):
            emit_A(tt)
            if tt >= 1:
                emit_B(tt - 1)
        emit_B(7)
        arb.reset(m_b)
        arf.reset(m_f)

    def ph_c(l, yin_h, yout_h):
        m_b = arb.mark()
        m_f = arf.mark()
        w1bf = [arb.alloc(W1C) for _ in range(8)]
        w2bf = [arb.alloc(1024) for _ in range(22)]
        b2row = arb.alloc(1024)
        tlnT = arb.alloc(2048)
        actT = [arb.alloc(256) for _ in range(22)]
        tlnb = arb.alloc(1024)
        gb = [arb.alloc(256) for _ in range(2)]
        ab2 = [arb.alloc(256) for _ in range(2)]
        y2f = [arf.alloc(1024) for _ in range(3)]
        tf = [arf.alloc(1024) for _ in range(2)]
        rowb = arf.alloc(512)
        w1t = {}
        for part in (0, 2, 3, 1, 4, 5):
            for d in range(8):
                eng = "scalar" if d % 2 == 0 else "vector"
                c0 = 1024 * part
                cw = min(1024, W1C - c0)
                tp_ = load_conv(w1p_h[l, 128 * d:128 * d + 128, c0:c0 + cw],
                                w1bf[d][:, c0:c0 + cw], cw, eng)
                w1t[(d, part)] = ({"scalar": SC, "vector": SV}[eng], tp_)
        w2t = {}
        for c in range(22):
            eng = "scalar" if c % 2 == 0 else "vector"
            w2t[c] = ({"scalar": SC, "vector": SV}[eng],
                      load_conv(w2aug_h[l, 128 * c:128 * c + 128, :], w2bf[c][:], 1024, eng))
        tdb2 = DMA("sync", rowb[0:1, 0:512], w2aug_h[l, 2816:2817, 0:512], "db2a")
        W("scalar", "db2a", tdb2)
        tb2 = OP("scalar", lambda e: e.copy(b2row[0:1, 0:512], rowb[0:1, 0:512]), inc=SC)
        W("sync", SC, tb2)
        tdb2c = DMA("sync", rowb[0:1, 0:512], w2aug_h[l, 2816:2817, 512:1024], "db2a")
        W("scalar", "db2a", tdb2c)
        tb2 = OP("scalar", lambda e: e.copy(b2row[0:1, 512:1024], rowb[0:1, 0:512]), inc=SC)
        t_b1 = DMA("sync", b1c_sb[:], b1c_h[l], "db1c")
        t_lb = DMA("sync", lbc_sb[:], lbc_h[l], "dlbc")
        t_lg = DMA("sync", lgc_sb[:], lgc_h[l], "dlgc")
        b1e_t = {}
        prev_cg = [None]
        for cg in (0, 1, 4, 5, 6, 7, 2, 3, 8, 9, 10):
            prow = PK[4][0:1, :]
            for d in range(8):
                wp = w1t[(d, cg // 2)]
                W("tensor", wp[0], wp[1])
                if d == 0:
                    W("tensor", "dlbc", t_lb)
                    if prev_cg[0] is not None:
                        W("tensor", SC, trow)
                tmm = OP("tensor", lambda e, d=d, cg=cg, prow=prow: e.matmul(
                    prow, lbc_sb[:, d:d + 1], w1bf[d][:, 512 * cg:512 * cg + 512],
                    start=(d == 0), stop=(d == 7)), inc=ST)
            W("scalar", ST, tmm)
            if cg == 0:
                W("scalar", SC, tb2)
            trow = OP("scalar", lambda e, prow=prow: e.copy(rowb[0:1, :], prow), inc=SC)
            for m4 in range(4):
                pcol = PK[5][:, 4 * m4:4 * m4 + 1]
                W("tensor", SC, trow)
                if prev_cg[0] is not None:
                    W("tensor", SV, b1e_t[4 * prev_cg[0] + m4])
                ttr = OP("tensor", lambda e, m4=m4, pcol=pcol: e.transpose(
                    pcol, rowb[0:1, 128 * m4:128 * m4 + 128], IDF[0:1, 0:1]), inc=ST)
                W("vector", ST, ttr)
                if cg == 0 and m4 == 0:
                    W("vector", "db1c", t_b1)
                cidx = 4 * cg + m4
                tbe = OP("vector", lambda e, cidx=cidx, pcol=pcol: e.tensor_tensor(
                    b1e[:, cidx:cidx + 1], pcol, b1c_sb[:, cidx:cidx + 1], op=ALU.add), inc=SV)
                b1e_t[cidx] = tbe
            prev_cg[0] = cg
        tb1e_all = b.cur(SV)
        w2ps_t = [[(SV, tb1e_all)], [(SV, tb1e_all)]]
        act_cons = [(None, 0)] * 22
        yf_t = [0] * 3
        yf_cons = [[] for _ in range(3)]
        tf_cons = [[], []]
        tlnb_cons = [0]
        tlnT_cons = [0]
        tlnT_t = [0]
        pb_t = [(None, 0)] * 2
        ag_t = [[], [], [], []]
        gb_t = [(None, 0)] * 2
        ab_t = [(None, 0)] * 2
        act_t = [(None, 0)] * 22
        y3_t = [(None, 0)] * 2
        pbi = [0]
        def ln_stage(tc):
            for s2 in range(2):
                i2 = 2 * tc + s2
                tsl2 = s2
                for sem, t in tf_cons[tsl2]:
                    W("sync", sem, t)
                tf_cons[tsl2] = []
                DMA("sync", tf[tsl2][:, 0:512], yin_h[128 * i2:128 * i2 + 128, 0:512], f"dtf{tsl2}")
                if i2 == 0:
                    DMA("sync", tf[tsl2][1:128, 512:1024], yin_h[0:127, 512:1024], f"dtf{tsl2}")
                    ttf = DMA("sync", tf[tsl2][0:1, 512:1024], zrow_h[0:1, 0:512], f"dtf{tsl2}")
                else:
                    ttf = DMA("sync", tf[tsl2][:, 512:1024],
                              yin_h[128 * i2 - 1:128 * i2 + 127, 512:1024], f"dtf{tsl2}")
                if tlnb_cons[0]:
                    W("vector", ST, tlnb_cons[0])
                tap = emit_ln(tf[tsl2][:], tlnb[:], ttf, f"dtf{tsl2}", c0=64 + 16 * s2)
                tf_cons[tsl2] = [(SV, tap)]
                pbsl = pbi[0] % 2
                pbi[0] += 1
                PB = PB0 if pbsl == 0 else PB1
                ps, pt = pb_t[pbsl]
                if ps is not None:
                    W("tensor", ps, pt)
                W("tensor", SV, tap)
                for k in range(8):
                    tmm = OP("tensor", lambda e, k=k, PB=PB: e.transpose(
                        PB[:, 128 * k:128 * k + 128], tlnb[:, 128 * k:128 * k + 128], IDB[:]),
                        inc=ST)
                tlnb_cons[0] = tmm
                W("scalar", ST, tmm)
                if s2 == 0 and tlnT_cons[0]:
                    W("scalar", ST, tlnT_cons[0])
                if i2 == 0:
                    W("scalar", "dlgc", t_lg)
                for k in range(8):
                    te = OP("scalar", lambda e, k=k, PB=PB, s2=s2: e.activation(
                        tlnT[:, 256 * k + 128 * s2:256 * k + 128 * s2 + 128],
                        PB[:, 128 * k:128 * k + 128], AF.Copy,
                        scale=lgc_sb[:, k:k + 1]), inc=SC)
                pb_t[pbsl] = (SC, te)
                tlnT_t[0] = te

        ln_stage(0)
        for tc in range(16):
            tev = tlnT_t[0]
            for c in range(22):
                pa = 2 * (c % 2)
                pg = 2 * (c % 2) + 1
                ppa = PK[pa][:, 0:256]
                ppg = PK[pg][:, 0:256]
                for sem, t in ag_t[pa]:
                    W("tensor", sem, t)
                for sem, t in ag_t[pg]:
                    W("tensor", sem, t)
                W("tensor", SC, tev)
                for d in range(8):
                    if tc == 0:
                        wp = w1t[(d, c // 8)]
                        W("tensor", wp[0], wp[1])
                        wp = w1t[(d, (22 + c) // 8)]
                        W("tensor", wp[0], wp[1])
                    tma = OP("tensor", lambda e, d=d, c=c, ppa=ppa: e.matmul(
                        ppa, w1bf[d][:, 128 * c:128 * c + 128], tlnT[:, 256 * d:256 * d + 256],
                        start=(d == 0), stop=(d == 7)), inc=ST)
                for d in range(8):
                    tmg = OP("tensor", lambda e, d=d, c=c, ppg=ppg: e.matmul(
                        ppg, w1bf[d][:, 128 * (22 + c):128 * (22 + c) + 128],
                        tlnT[:, 256 * d:256 * d + 256],
                        start=(d == 0), stop=(d == 7)), inc=ST)
                g2 = c % 2
                gse, gst = gb_t[g2]
                if gse is not None:
                    W("scalar", gse, gst)
                W("scalar", ST, tmg)
                if tc == 0:
                    W("scalar", SV, b1e_t[22 + c])
                tgel = OP("scalar", lambda e, g2=g2, c=c, ppg=ppg: e.activation(
                    gb[g2][:], ppg, AF.Gelu, bias=b1e[:, 22 + c:22 + c + 1]), inc=SC)
                ase, ast = ab_t[g2]
                if ase is not None:
                    W("vector", ase, ast)
                W("vector", ST, tma)
                if tc == 0:
                    W("vector", SV, b1e_t[c])
                tab = OP("vector", lambda e, g2=g2, c=c, ppa=ppa: e.tensor_scalar(
                    ab2[g2][:], ppa, 1.0, b1e[:, c:c + 1], op0=ALU.mult, op1=ALU.add), inc=SV)
                ace, act_ = act_cons[c]
                if ace is not None:
                    W("vector", ace, act_)
                W("vector", SV, tab)
                W("vector", SC, tgel)
                tact = OP("vector", lambda e, c=c, g2=g2: e.tensor_tensor(
                    actT[c][:], ab2[g2][:], gb[g2][:], op=ALU.mult), inc=SV)
                gb_t[g2] = (SV, tact)
                ab_t[g2] = (SV, tact)
                ag_t[pa] = [(SV, tab)]
                ag_t[pg] = [(SC, tgel)]
                act_t[c] = (SV, tact)
            if tc < 15:
                ln_stage(tc + 1)
            for sub2 in range(2):
                i2 = 2 * tc + sub2
                sl4 = i2 % 3
                for sem, t in yf_cons[sl4]:
                    W("sync", sem, t)
                yf_cons[sl4] = []
                yf_t[sl4] = DMA("sync", y2f[sl4][:], yin_h[128 * i2:128 * i2 + 128, :], f"dyc{sl4}")
            for half in range(2):
                for sub in range(2):
                    pw = PK[4 + sub][:, :]
                    for sem, t in w2ps_t[sub]:
                        W("tensor", sem, t)
                    for c in range(22):
                        W("tensor", SV, act_t[c][1])
                        if tc == 0 and half == 0:
                            W("tensor", w2t[c][0], w2t[c][1])
                        tw2 = OP("tensor", lambda e, c=c, sub=sub, half=half, pw=pw: e.matmul(
                            pw, actT[c][:, 128 * sub:128 * sub + 128],
                            w2bf[c][:, 512 * half:512 * half + 512],
                            start=(c == 0), stop=False), inc=ST)
                    if tc == 0 and half == 0 and sub == 0:
                        W("tensor", SC, tb2)
                        W("tensor", SG, b.init_sg)
                    i2 = 2 * tc + sub
                    sl4 = i2 % 3
                    tw2 = OP("tensor", lambda e, sub=sub, half=half, pw=pw: e.matmul(
                        pw, onesr[0:1, 0:128], b2row[0:1, 512 * half:512 * half + 512],
                        start=False, stop=True), inc=ST)
                    y3e, y3t = y3_t[sub]
                    if half == 0 and y3e is not None:
                        W("vector", y3e, y3t)
                    W("vector", f"dyc{sl4}", yf_t[sl4])
                    W("vector", ST, tw2)
                    ty3 = OP("vector", lambda e, sub=sub, half=half, sl4=sl4, pw=pw: e.tensor_tensor(
                        tf[sub][:, 512 * half:512 * half + 512], pw,
                        y2f[sl4][:, 512 * half:512 * half + 512], op=ALU.add), inc=SV)
                    w2ps_t[sub] = [(SV, ty3)]
                    if half == 1:
                        yf_cons[sl4].append((SV, ty3))
                        W("gpsimd", SV, ty3)
                        tsy = DMA("gpsimd", yout_h[128 * i2:128 * i2 + 128, :], tf[sub][:], f"sy3{sub}")
                        y3_t[sub] = (f"sy3{sub}", tsy)
                        tf_cons[sub].append((f"sy3{sub}", tsy))
            tlnT_cons[0] = b.cur(ST)
            for c in range(22):
                act_cons[c] = (ST, tlnT_cons[0])
        arb.reset(m_b)
        arf.reset(m_f)

    def ph_f():
        m_f = arf.mark()
        xf = [arf.alloc(1024) for _ in range(2)]
        yfv = [arf.alloc(1024) for _ in range(2)]
        ofv = [arf.alloc(1024) for _ in range(1)]
        add_t = [0, 0]
        o_cons = [(None, 0)] * 1
        out_tick = None
        for i in range(32):
            sl = i % 2
            if add_t[sl]:
                W("sync", SV, add_t[sl])
            tx = DMA("sync", xf[sl][:], sc_x1[128 * i:128 * i + 128, :], f"dfx{sl}")
            ty = DMA("sync", yfv[sl][:], sc_yA[128 * i:128 * i + 128, :], f"dfy{sl}")
            ose, ost = o_cons[0]
            if ose is not None:
                W("vector", ose, ost)
            W("vector", f"dfx{sl}", tx)
            W("vector", f"dfy{sl}", ty)
            to = OP("vector", lambda e, sl=sl: e.tensor_tensor(
                ofv[0][:], xf[sl][:], yfv[sl][:], op=ALU.add), inc=SV)
            add_t[sl] = to
            W("gpsimd", SV, to)
            out_tick = DMA("gpsimd", out_h[128 * i:128 * i + 128, :], ofv[0][:], "sout")
            o_cons[0] = ("sout", out_tick)
        W("gpsimd", "sout", out_tick)
        arf.reset(m_f)

    import os
    t_isc, t_isg = ph_init()
    b.init_sg = t_isg
    if "r0" in phases:
        ph_r0(t_isc, t_isg)
        phase_barrier()
    if "r1" in phases:
        ph_r1(int(os.environ.get("R1_CHUNKS", "8")))
        phase_barrier()
    if "r2" in phases:
        ph_r2()
        phase_barrier()
    arb.reset(0)
    FUSE_AB = os.environ.get("FUSE_AB") == "1"
    for l in range(2):
        if f"a{l}" in phases or f"b{l}" in phases:
            if FUSE_AB:
                ph_ab(l, sc_yA, sc_yB)
            else:
                ph_a(l, sc_yA)
                phase_barrier()
                ph_b(l, sc_yA, sc_yB)
            phase_barrier()
        if f"c{l}" in phases:
            ph_c(l, sc_yB, sc_yA)
            phase_barrier()
    if "f" in phases:
        phase_barrier()
        ph_f()

    b.emit()
    es.close()
    return nc


def kernel(**inputs):
    hp = _host_prep(inputs)
    x8 = np.ascontiguousarray(np.asarray(inputs["L0_attn_x"], np.float32))
    nc = build_program()
    shared = {k: v for k, v in hp.items() if v is not None}
    in_maps = [dict(shared, x=np.ascontiguousarray(x8[c])) for c in range(NCORE)]
    res = run_bass_kernel_spmd(nc, in_maps, core_ids=list(range(NCORE)))
    out = np.stack([np.asarray(r["out"]) for r in res.results]).astype(np.float32)
    return out
